# version 27
# baseline (speedup 1.0000x reference)
import numpy as np
from contextlib import ExitStack

import concourse.bass as bass
import concourse.mybir as mybir
from concourse.bass_utils import run_bass_kernel_spmd

F32 = mybir.dt.float32
BF16 = mybir.dt.bfloat16
I32 = mybir.dt.int32
F8 = mybir.dt.float8e4
AF = mybir.ActivationFunctionType
ALU = mybir.AluOpType
AX = mybir.AxisListType
AP = bass.AP

NCORES = 8
NB = 2
T = 2048
C = 1024
H = 16
HD = 64
FF = 4096
NSUB = 16
NTC = 4
NTC3 = 8
EPS = 1e-5
WN = float(FF * C)


class Walk:

    def __init__(self, nc, semnames):
        self.nc = nc
        self.sems = {}
        self.cnt = {n: 0 for n in semnames}
        self.names = list(semnames)
        self.cur_name = None
        self.cur_eng = None

    def op(self, eng, fn, inc=None, k=1):
        if self.cur_name == eng:
            instr = fn(self.cur_eng)
            if inc is not None:
                instr.then_inc(self.sems[inc], k)
        if inc is not None:
            self.cnt[inc] += k

    def wait(self, eng, sem, val):
        if val <= 0:
            return
        if self.cur_name == eng:
            self.cur_eng.wait_ge(self.sems[sem], val)

    def v(self, sem):
        return self.cnt[sem]

    def vops(self, eng, fns, sem="sV"):
        for fn in fns:
            self.op(eng, fn, inc=sem)
            self.wait(eng, sem, self.v(sem))

    def block(self, fn, name=None):
        nc = self.nc
        base = dict(self.cnt)
        final = {}

        def mk(engname):
            def body(e):
                self.cnt = dict(base)
                self.cur_name = engname
                self.cur_eng = e
                fn(self)
                final.update(self.cnt)
            return body

        with nc.Block(name) as blk:
            blk.sync(mk("sync"))
            blk.scalar(mk("scalar"))
            blk.vector(mk("vector"))
            blk.tensor(mk("tensor"))
            blk.gpsimd(mk("gpsimd"))
        self.cnt = dict(final)


def build_program(debug=False, phases=('S','Q','P1','P2','P3'), p3sub='full'):
    nc = bass.Bass("TRN2", target_bir_lowering=False)

    x_d = nc.dram_tensor("x", [NB, T, C], F32, kind="ExternalInput").ap()
    qkvwT_d = nc.dram_tensor("qkvwT", [C, 3 * C], F32, kind="ExternalInput").ap()
    qbcols_d = nc.dram_tensor("qb_cols", [128, 8], F32, kind="ExternalInput").ap()
    kvbrow_d = nc.dram_tensor("kvb_row", [1, 2 * C], F32, kind="ExternalInput").ap()
    p1wT_d = nc.dram_tensor("p1wT", [C, C], F32, kind="ExternalInput").ap()
    p1bcols_d = nc.dram_tensor("p1b_cols", [128, 8], F32, kind="ExternalInput").ap()
    p2wT_d = nc.dram_tensor("p2wT", [C, C], F32, kind="ExternalInput").ap()
    p2brow_d = nc.dram_tensor("p2b_row", [1, C], F32, kind="ExternalInput").ap()
    w1T_d = nc.dram_tensor("w1T", [C, FF], F32, kind="ExternalInput").ap()
    b1cols_d = nc.dram_tensor("b1_cols", [128, 32], F32, kind="ExternalInput").ap()
    w2T_d = nc.dram_tensor("w2T", [FF, C], F32, kind="ExternalInput").ap()
    b2row_d = nc.dram_tensor("b2_row", [1, C], F32, kind="ExternalInput").ap()
    g64_d = nc.dram_tensor("g64", [1, 256], F32, kind="ExternalInput").ap()
    temp_d = nc.dram_tensor("tempr", [16, 1], F32, kind="ExternalInput").ap()
    abias_d = nc.dram_tensor("abias", [128, 512], F32, kind="ExternalInput").ap()
    out_d = nc.dram_tensor("out", [NB, T, C], F32, kind="ExternalOutput").ap()
    y1s_d = nc.dram_tensor("y1s", [NB, T, C], F32, kind="Internal").ap()
    wq1s_d = nc.dram_tensor("wq1s", [C, FF], F8, kind="Internal").ap()
    wq2s_d = nc.dram_tensor("wq2s", [FF, C], F8, kind="Internal").ap()
    gbscr_d = nc.dram_tensor("gbscr", [2, 2 * C], F32, kind="Internal").ap()
    smscr_d = nc.dram_tensor("smscr", [16, 1], F32, kind="Internal").ap()

    dbg = {}
    if debug:
        for nm, shp, dt in [("d_ln1fm", [128, 8 * T], F32),
                            ("d_q", [128, 8 * T], F32),
                            ("d_khat", [128, 3 * 2048], F32),
                            ("d_kvbd", [128, 1024], F32),
                            ("d_att2", [128, 8 * 512], F32),
                            ("d_gb", [128, 2 * 2048], F32),
                            ("d_invg", [128, 4], F32),
                            ("d_red16", [128, 16], F32),
                            ("d_g11", [1, 16], F32)]:
            dbg[nm] = nc.dram_tensor(nm, shp, dt, kind="ExternalOutput").ap()

    es = ExitStack()

    def sb(name, shape, dt):
        return es.enter_context(nc.sbuf_tensor(name, shape, dt))

    ident = sb("ident", [128, 128], BF16)
    colidx = sb("colidx", [128, 128], I32)
    rowidx = sb("rowidx", [128, 1], F32)
    ones_row = sb("ones_row", [1, 128], BF16)
    onescol_f = sb("onescol_f", [128, 1], F32)
    chalf = sb("chalf", [128, 1], F32)
    cnhalf = sb("cnhalf", [128, 1], F32)
    ceps = sb("ceps", [128, 1], F32)
    ceps2 = sb("ceps2", [128, 1], F32)
    qb_cols = sb("qb_cols_s", [128, 8], F32)
    p1b_cols = sb("p1b_cols_s", [128, 8], F32)
    b1_cols = sb("b1_cols_s", [128, 32], F32)
    kvb_row = sb("kvb_row_s", [1, 2 * C], BF16)
    p2b_row = sb("p2b_row_s", [1, C], BF16)
    B2tile = sb("B2tile", [128, C], F32)
    abias_sb = sb("abias_sb", [128, 512], F32)
    tempQ = sb("tempQ", [128, 8], F32)
    kvbd = sb("kvbd", [128, 8 * 128], BF16)
    invg_col = sb("invg_col", [128, 4], F32)
    r1_sb = sb("r1_sb", [128, NSUB], F32)
    nmr1_sb = sb("nmr1_sb", [128, NSUB], F32)
    sxc = sb("sxc", [128, 4], F32)
    ssqc = sb("ssqc", [128, 4], F32)
    negm = sb("negm", [128, 4], F32)
    msq = sb("msq", [128, 4], F32)
    vvar = sb("vvar", [128, 4], F32)
    srt = sb("srt", [128, 4], F32)
    rrc = sb("rrc", [128, 4], F32)
    nmrc = sb("nmrc", [128, 4], F32)
    khsum = sb("khsum", [128, 32], F32)
    khsq = sb("khsq", [128, 32], F32)
    khneg = sb("khneg", [128, 32], F32)
    khmsq = sb("khmsq", [128, 32], F32)
    khv = sb("khv", [128, 32], F32)
    khsrt = sb("khsrt", [128, 32], F32)
    khr = sb("khr", [128, 32], F32)
    junk2 = sb("junk2", [128, 1024], F32)

    qsc = ExitStack()
    g64 = qsc.enter_context(nc.sbuf_tensor("g64_s", [1, 256], F32))
    g_row = qsc.enter_context(nc.sbuf_tensor("g_row", [1, 2 * C], F32))
    b_row = qsc.enter_context(nc.sbuf_tensor("b_row", [1, 2 * C], F32))
    t16 = qsc.enter_context(nc.sbuf_tensor("t16", [16, 1], F32))
    red16 = qsc.enter_context(nc.sbuf_tensor("red16", [128, 16], F32))
    g11 = qsc.enter_context(nc.sbuf_tensor("g11", [1, 16], F32))

    semnames = ["sDX", "sDW", "sDO", "sT", "sV", "sS", "sP",
                "sXL", "sGB", "sTQ", "sWR0", "sWR1", "sIG", "sSO0", "sSO1",
                "sY0", "sY1", "sOS", "sWQ", "sGM", "sXL1", "sXL2", "sXL3",
                "sTR0", "sTR1"]
    w = Walk(nc, semnames)
    for n in semnames:
        w.sems[n] = es.enter_context(nc.semaphore(n))

    def startup(w):
        for dst, src in [(qb_cols[:], qbcols_d), (p1b_cols[:], p1bcols_d),
                         (b1_cols[:], b1cols_d), (abias_sb[:], abias_d),
                         (t16[:], temp_d), (g64[:], g64_d),
                         (B2tile[:], AP(b2row_d.tensor, 0, [[0, 128], [1, C]]))]:
            w.op("sync", lambda e, dst=dst, src=src: e.dma_start(out=dst, in_=src),
                 inc="sDX", k=16)
        w.op("gpsimd", lambda e: e.dma_start(out=kvb_row[:], in_=kvbrow_d),
             inc="sDW", k=16)
        w.op("gpsimd", lambda e: e.dma_start(out=p2b_row[:], in_=p2brow_d),
             inc="sDW", k=16)
        w.op("gpsimd", lambda e: e.iota(colidx[:], [[1, 128]], base=0,
                                        channel_multiplier=0), inc="sP")
        w.op("gpsimd", lambda e: e.iota(rowidx[:], [[1, 1]], base=0,
                                        channel_multiplier=1,
                                        allow_small_or_imprecise_dtypes=True),
             inc="sP")

        def msets(e):
            e.memset(ones_row[:], 1.0)
            e.memset(onescol_f[:], 1.0)
            e.memset(chalf[:], 0.5)
            e.memset(cnhalf[:], -0.5)
            e.memset(ceps[:], EPS)
            e.memset(ceps2[:], EPS * HD * HD)
            return e.memset(kvbd[:], 0.0)
        w.op("vector", msets, inc="sV")

        w.wait("vector", "sP", w.v("sP"))
        w.op("vector", lambda e: e.tensor_scalar(
            out=ident[:], in0=colidx[:], scalar1=rowidx[:], scalar2=None,
            op0=ALU.is_equal), inc="sV")

        w.wait("vector", "sDX", w.v("sDX"))

        def tile_gb(e):
            for half in range(2):
                e.tensor_copy(
                    out=g_row[0:1, half * C:(half + 1) * C].rearrange(
                        "p (a d) -> p a d", d=HD),
                    in_=g64[0:1, half * HD:(half + 1) * HD].unsqueeze(1)
                    .broadcast_to([1, H, HD]))
            for half in range(2):
                e.tensor_copy(
                    out=b_row[0:1, half * C:(half + 1) * C].rearrange(
                        "p (a d) -> p a d", d=HD),
                    in_=g64[0:1, 128 + half * HD: 128 + (half + 1) * HD]
                    .unsqueeze(1).broadcast_to([1, H, HD]))
            return e.tensor_scalar(out=t16[:], in0=t16[:], scalar1=1.0 / T,
                                   scalar2=None, op0=ALU.mult)
        w.op("vector", tile_gb, inc="sV")
        w.wait("gpsimd", "sV", w.v("sV"))
        w.op("gpsimd", lambda e: e.dma_start(out=gbscr_d[0:1, :], in_=g_row[:]),
             inc="sDO", k=16)
        w.op("gpsimd", lambda e: e.dma_start(out=gbscr_d[1:2, :], in_=b_row[:]),
             inc="sDO", k=16)
        w.op("gpsimd", lambda e: e.dma_start(out=smscr_d, in_=t16[:]),
             inc="sDO", k=16)
        w.wait("sync", "sDO", w.v("sDO"))
        with nc.allow_non_contiguous_dma(reason="tiny one-time tempQ broadcast"):
            w.op("sync", lambda e: e.dma_start(
                out=tempQ[0:64, :], in_=AP(smscr_d.tensor, 0, [[0, 64], [2, 8]])),
                inc="sTQ", k=16)
            w.op("sync", lambda e: e.dma_start(
                out=tempQ[64:128, :], in_=AP(smscr_d.tensor, 1, [[0, 64], [2, 8]])),
                inc="sTQ", k=16)
        w.wait("sync", "sTQ", w.v("sTQ"))
        w.wait("gpsimd", "sDW", w.v("sDW"))

    w.block(startup, "startup")
    run_p = phases

    qsc.close()

    def wchunk_src(gi):
        if gi < 32:
            return w1T_d[(gi // 4) * 128:(gi // 4 + 1) * 128,
                         (gi % 4) * 1024:(gi % 4 + 1) * 1024]
        r = gi - 32
        return w2T_d[r * 128:(r + 1) * 128, :]

    def wchunk_dst(gi):
        if gi < 32:
            return wq1s_d[(gi // 4) * 128:(gi // 4 + 1) * 128,
                          (gi % 4) * 1024:(gi % 4 + 1) * 1024]
        r = gi - 32
        return wq2s_d[r * 128:(r + 1) * 128, :]

    for b in (range(NB) if 'P1' in run_p else []):
        absc = ExitStack()
        ln1_fm = absc.enter_context(nc.sbuf_tensor(f"ln1fm{b}", [128, 8 * T], BF16))
        q_fm = absc.enter_context(nc.sbuf_tensor(f"qfm{b}", [128, 8 * T], BF16))
        p1sc = ExitStack()
        G64 = p1sc.enter_context(nc.sbuf_tensor(f"G64_{b}", [128, 2 * HD], F32))
        B64 = p1sc.enter_context(nc.sbuf_tensor(f"B64_{b}", [128, 2 * HD], F32))
        qkvwT_sb = p1sc.enter_context(
            nc.sbuf_tensor(f"qkvwT{b}", [128, 8 * 3072], BF16))
        xring = p1sc.enter_context(nc.sbuf_tensor(f"xring{b}", [128, 4096], F32))
        ln1tm = p1sc.enter_context(nc.sbuf_tensor(f"ln1tm{b}", [128, 2 * 1024], BF16))
        uu = p1sc.enter_context(nc.sbuf_tensor(f"uu{b}", [128, 2048], F32))
        uu2 = p1sc.enter_context(nc.sbuf_tensor(f"uu2{b}", [128, 2048], F32))
        khat = p1sc.enter_context(nc.sbuf_tensor(f"khat{b}", [128, 3 * 2048], BF16))
        psKV = p1sc.enter_context(nc.psum_tensor(f"psKV{b}", [128, 2048], F32))
        psQ = p1sc.enter_context(nc.psum_tensor(f"psQ{b}", [128, 1024], F32))
        psACC = p1sc.enter_context(nc.psum_tensor(f"psACC{b}", [128, 1024], F32))
        gw_ring = p1sc.enter_context(nc.sbuf_tensor(f"gwr{b}", [128, 2 * 1024], F32))
        gw_junk = p1sc.enter_context(nc.sbuf_tensor(f"gwj{b}", [128, 1024], F32))
        junk = p1sc.enter_context(nc.sbuf_tensor(f"junk{b}", [128, 2048], F32))
        red64 = p1sc.enter_context(nc.sbuf_tensor(f"red64{b}", [128, 64], F32))
        g11 = p1sc.enter_context(nc.sbuf_tensor(f"g11{b}", [1, 16], F32))

        def p1(w, b=b, ln1_fm=ln1_fm, q_fm=q_fm, G64=G64, B64=B64,
               qkvwT_sb=qkvwT_sb, xring=xring, ln1tm=ln1tm, uu=uu, uu2=uu2,
               khat=khat, psKV=psKV, psQ=psQ, psACC=psACC,
               gw_ring=gw_ring, gw_junk=gw_junk, red64=red64, g11=g11,
               junk=junk):
            gred_done = {}

            def gamma_chunk(gi):
                if b != 0:
                    return
                wr = "sWR0" if gi % 2 == 0 else "sWR1"
                if gi >= 2:
                    w.wait("sync", "sS", gred_done[gi - 2])
                w.op("sync", lambda e, gi=gi: e.dma_start(
                    out=gw_ring[:, (gi % 2) * 1024:(gi % 2 + 1) * 1024],
                    in_=wchunk_src(gi)), inc=wr, k=16)
                w.wait("scalar", wr, w.v(wr))
                w.wait("scalar", "sS", w.v("sS"))
                w.op("scalar", lambda e, gi=gi: e.activation(
                    out=gw_junk[:], in_=gw_ring[:, (gi % 2) * 1024:(gi % 2 + 1) * 1024],
                    func=AF.Abs, accum_out=red64[:, gi:gi + 1]), inc="sS")
                gred_done[gi] = w.v("sS")
            for a in range(8):
                w.op("gpsimd", lambda e, a=a: e.dma_start(
                    out=qkvwT_sb[:, a * 3072:(a + 1) * 3072],
                    in_=qkvwT_d[a * 128:(a + 1) * 128, :],
                    max_dma_last_dim=8192), inc="sDW", k=16)
            w.op("sync", lambda e: e.dma_start(
                out=G64[:, 0:HD], in_=AP(gbscr_d.tensor, 0, [[0, 128], [1, HD]])),
                inc="sGB", k=16)
            w.op("sync", lambda e: e.dma_start(
                out=G64[:, HD:2 * HD],
                in_=AP(gbscr_d.tensor, C, [[0, 128], [1, HD]])), inc="sGB", k=16)
            w.op("sync", lambda e: e.dma_start(
                out=B64[:, 0:HD],
                in_=AP(gbscr_d.tensor, 2 * C, [[0, 128], [1, HD]])), inc="sGB", k=16)
            w.op("sync", lambda e: e.dma_start(
                out=B64[:, HD:2 * HD],
                in_=AP(gbscr_d.tensor, 3 * C, [[0, 128], [1, HD]])), inc="sGB", k=16)
            wload = w.v("sDW")
            gbload = w.v("sGB")

            evac_done = {}
            pass1g = {}
            hp_done = {}
            sq_done_g = {}
            khat_done = {}
            kvacc_done = {}
            tr_done = {}
            qevac_done = {}
            applyln_done = {}
            xcons_done = {}
            chain1_done = {}
            ksq_red = {}
            stats_done_map = {}

            def p1ldstats(tc):
                if tc >= 1:
                    w.wait("sync", "sS", applyln_done[tc - 1])
                    w.wait("sync", "sV", xcons_done[tc - 1])
                w.op("sync", lambda e, tc=tc: e.dma_start(
                    out=xring[:].rearrange("p (s d) -> p s d", d=C),
                    in_=x_d[b, tc * 512:(tc + 1) * 512, :].rearrange(
                        "(s p) d -> p s d", p=128)), inc="sXL", k=16)
                xload = w.v("sXL")

                w.wait("scalar", "sXL", xload)
                if tc >= 1:
                    w.wait("scalar", "sV", chain1_done[tc - 1])
                w.wait("vector", "sXL", xload)
                for s4 in range(4):
                    xs = xring[:, s4 * 1024:(s4 + 1) * 1024]
                    w.wait("scalar", "sS", w.v("sS"))
                    w.op("scalar", lambda e, xs=xs, s4=s4: e.activation(
                        out=junk2[:], in_=xs, func=AF.Square,
                        accum_out=ssqc[:, s4:s4 + 1]), inc="sS")
                    w.op("vector", lambda e, xs=xs, s4=s4: e.tensor_reduce(
                        out=sxc[:, s4:s4 + 1], in_=xs, axis=AX.X, op=ALU.add),
                        inc="sV" if s4 == 3 else None)
                sq_done = w.v("sS")
                xred_done = w.v("sV")
                w.wait("vector", "sS", sq_done)
                w.wait("vector", "sV", xred_done)

                w.vops("vector", [
                    lambda e: e.tensor_scalar(out=negm[:], in0=sxc[:],
                                              scalar1=-1.0 / C, scalar2=None,
                                              op0=ALU.mult),
                    lambda e: e.tensor_tensor(out=msq[:], in0=negm[:],
                                              in1=negm[:], op=ALU.mult),
                    lambda e: e.scalar_tensor_tensor(
                        out=vvar[:], in0=ssqc[:], scalar=1.0 / C, in1=msq[:],
                        op0=ALU.mult, op1=ALU.subtract)])
                chain1_done[tc] = w.v("sV")
                w.wait("scalar", "sV", w.v("sV"))
                w.op("scalar", lambda e: e.activation(
                    out=srt[:], in_=vvar[:], func=AF.Sqrt, bias=ceps[:]), inc="sS")
                w.wait("vector", "sS", w.v("sS"))

                w.vops("vector", [
                    lambda e: e.reciprocal(rrc[:], srt[:]),
                    lambda e: e.tensor_tensor(out=nmrc[:], in0=negm[:],
                                              in1=rrc[:], op=ALU.mult),
                    lambda e, tc=tc: e.tensor_copy(
                        out=r1_sb[:, tc * 4:(tc + 1) * 4], in_=rrc[:]),
                    lambda e, tc=tc: e.tensor_copy(
                        out=nmr1_sb[:, tc * 4:(tc + 1) * 4], in_=nmrc[:])])
                stats_done_map[tc] = w.v("sV")
                xcons_done[tc] = max(stats_done_map[tc], xred_done)

            def p1fronts(tc):
                for s4 in range(4):
                    s = tc * 4 + s4
                    lslot = s % 2
                    xs = xring[:, s4 * 1024:(s4 + 1) * 1024]
                    w.wait("scalar", "sV", stats_done_map[tc])
                    if s >= 2:
                        w.wait("scalar", *tr_done[s - 2])
                    w.op("scalar", lambda e, xs=xs, s4=s4, lslot=lslot: e.activation(
                        out=ln1tm[:, lslot * 1024:(lslot + 1) * 1024], in_=xs,
                        func=AF.Identity, scale=rrc[:, s4:s4 + 1],
                        bias=nmrc[:, s4:s4 + 1]), inc="sS")
                    apl = w.v("sS")
                    if s4 == 3:
                        applyln_done[tc] = apl
                    trs = "sTR0" if lslot == 0 else "sTR1"
                    w.wait("scalar", "sS", apl)
                    w.op("scalar", lambda e, s=s, lslot=lslot: e.dma_start(
                        out=AP(ln1_fm, s * 128, [[8 * T, 128], [T, 8], [1, 128]]),
                        in_=ln1tm[:, lslot * 1024:(lslot + 1) * 1024],
                        transpose=True), inc=trs, k=16)
                    tr_done[s] = (trs, w.v(trs))

            def p1body(tc):
                for s4 in range(4):
                    s = tc * 4 + s4
                    lslot = s % 2
                    if s >= 3:
                        w.wait("tensor", "sP", khat_done[s - 3])

                        def kvacc(e, sp=s - 3):
                            ii = None
                            lsp = sp % 3
                            for p in range(8):
                                ii = e.matmul(
                                    psACC[:, p * 128:(p + 1) * 128],
                                    khat[:, lsp * 2048 + p * 128:
                                         lsp * 2048 + (p + 1) * 128],
                                    khat[:, lsp * 2048 + 1024 + p * 128:
                                         lsp * 2048 + 1024 + (p + 1) * 128],
                                    start=(sp == 0 and p in (0, 4)), stop=False,
                                    skip_group_check=True)
                            return ii
                        w.op("tensor", kvacc, inc="sT")
                        kvacc_done[s - 3] = w.v("sT")

                    for g in range(4):
                        bank = g
                        jcol = 1024 + g * 512
                        half = g // 2
                        if g == 0:
                            w.wait("tensor", *tr_done[s])
                            if s == 0:
                                w.wait("tensor", "sDW", wload)
                        if s >= 1:
                            w.wait("tensor", "sV", pass1g[(s - 1, g)])
                        if s >= 1:
                            w.wait("vector", "sP", hp_done.get((s - 1, g // 2), 0))

                        def kvproj(e, s=s, jcol=jcol, bank=bank):
                            reg = psKV[:, bank * 512:(bank + 1) * 512]
                            e.matmul(reg, ones_row[:],
                                     kvb_row[0:1, jcol - 1024:jcol - 512],
                                     start=True, stop=False)
                            ii = None
                            for ci in range(8):
                                ii = e.matmul(
                                    reg,
                                    ln1_fm[:, ci * T + s * 128:
                                           ci * T + (s + 1) * 128],
                                    qkvwT_sb[:, ci * 3072 + jcol:
                                             ci * 3072 + jcol + 512],
                                    start=False, stop=(ci == 7))
                            return ii
                        w.op("tensor", kvproj, inc="sT")
                        projg = w.v("sT")

                        w.wait("vector", "sT", projg)
                        w.op("vector", lambda e, bank=bank, g=g: e.tensor_reduce(
                            out=khneg[:, g * 8:(g + 1) * 8],
                            in_=psKV[:, bank * 512:(bank + 1) * 512].rearrange(
                                "p (h d) -> p h d", d=HD),
                            axis=AX.X, op=ALU.add, negate=True), inc="sV")
                        w.wait("vector", "sV", w.v("sV"))
                        if s == 0 and g == 0:
                            w.wait("vector", "sGB", gbload)
                        w.op("vector", lambda e, bank=bank, g=g:
                             e.scalar_tensor_tensor(
                                 out=uu[:, g * 512:(g + 1) * 512].rearrange(
                                     "p (h d) -> p h d", d=HD),
                                 in0=psKV[:, bank * 512:(bank + 1) * 512].rearrange(
                                     "p (h d) -> p h d", d=HD),
                                 scalar=float(HD),
                                 in1=khneg[:, g * 8:(g + 1) * 8].unsqueeze(2)
                                 .broadcast_to([128, 8, HD]),
                                 op0=ALU.mult, op1=ALU.add), inc="sV")
                        pass1g[(s, g)] = w.v("sV")
                        w.wait("scalar", "sV", pass1g[(s, g)])
                        w.wait("scalar", "sS", w.v("sS"))
                        if (s - 1, g) in ksq_red:
                            w.wait("scalar", "sV", ksq_red[(s - 1, g)])
                        w.op("scalar", lambda e, g=g: e.activation(
                            out=junk[:, g * 512:(g + 1) * 512],
                            in_=uu[:, g * 512:(g + 1) * 512],
                            func=AF.Square), inc="sS")
                        sq_done_g[(s, g)] = w.v("sS")

                    for g in range(4):
                        bank = g % 2
                        w.wait("vector", "sS", sq_done_g[(s, g)])
                        w.op("vector", lambda e, g=g: e.tensor_reduce(
                            out=khsq[:, g * 8:(g + 1) * 8],
                            in_=junk[:, g * 512:(g + 1) * 512].rearrange(
                                "p (h d) -> p h d", d=HD),
                            axis=AX.X, op=ALU.add), inc="sV")
                        ksq_red[(s, g)] = w.v("sV")

                    for half in range(2):
                        hc = slice(half * 16, (half + 1) * 16)
                        w.wait("vector", "sV", ksq_red[(s, half * 2 + 1)])
                        w.vops("vector", [
                            lambda e, hc=hc: e.tensor_scalar(
                                out=khv[:, hc], in0=khsq[:, hc],
                                scalar1=1.0 / HD, scalar2=None,
                                op0=ALU.mult)])
                        w.wait("scalar", "sV", w.v("sV"))
                        w.op("scalar", lambda e, hc=hc: e.activation(
                            out=khsrt[:, hc], in_=khv[:, hc], func=AF.Sqrt,
                            bias=ceps2[:]), inc="sS")
                        w.wait("vector", "sS", w.v("sS"))
                        w.vops("vector", [
                            lambda e, hc=hc: e.reciprocal(khr[:, hc],
                                                          khsrt[:, hc])])
                        hr = slice(half * 1024, (half + 1) * 1024)
                        w.op("vector", lambda e, hc=hc, hr=hr: e.tensor_tensor(
                            out=uu2[:, hr].rearrange("p (h d) -> p h d", d=HD),
                            in0=uu[:, hr].rearrange("p (h d) -> p h d", d=HD),
                            in1=khr[:, hc].unsqueeze(2)
                            .broadcast_to([128, 16, HD]),
                            op=ALU.mult), inc="sV")
                        w.wait("gpsimd", "sV", w.v("sV"))
                        w.op("gpsimd", lambda e, half=half, hr=hr: e.tensor_tensor(
                            out=uu[:, hr].rearrange("p (h d) -> p h d", d=HD),
                            in0=uu2[:, hr].rearrange("p (h d) -> p h d", d=HD),
                            in1=G64[:, half * HD:(half + 1) * HD].unsqueeze(1)
                            .broadcast_to([128, 16, HD]),
                            op=ALU.mult), inc="sP")
                        w.wait("gpsimd", "sP", w.v("sP"))
                        if s >= 3 and half == 0:
                            w.wait("gpsimd", "sT", kvacc_done[s - 3])
                        w.op("gpsimd", lambda e, half=half, lslot=s % 3, hr=hr:
                             e.tensor_tensor(
                                 out=khat[:, lslot * 2048 + half * 1024:
                                          lslot * 2048 + (half + 1) * 1024]
                                 .rearrange("p (h d) -> p h d", d=HD),
                                 in0=uu[:, hr].rearrange("p (h d) -> p h d", d=HD),
                                 in1=B64[:, half * HD:(half + 1) * HD].unsqueeze(1)
                                 .broadcast_to([128, 16, HD]),
                                 op=ALU.add), inc="sP")
                        w.wait("gpsimd", "sP", w.v("sP"))
                        khat_done[s] = w.v("sP")
                        hp_done[(s, half)] = w.v("sP")

                    for u in range(4):
                        gamma_chunk(s * 4 + u)

            def p1q(tc):
                for jc in range(8):
                    qslot = jc % 2
                    if jc == 0:
                        w.wait("tensor", *tr_done[tc * 4 + 2])
                        w.wait("tensor", *tr_done[tc * 4 + 3])
                    if jc >= 2:
                        w.wait("tensor", "sV", qevac_done[jc - 2])
                    elif tc >= 1:
                        w.wait("tensor", "sV", qevac_done[6 + qslot])

                    def qproj(e, jc=jc, qslot=qslot, tc=tc):
                        ii = None
                        for ci in range(8):
                            ii = e.matmul(
                                psQ[:, qslot * 512:(qslot + 1) * 512],
                                qkvwT_sb[:, ci * 3072 + jc * 128:
                                         ci * 3072 + (jc + 1) * 128],
                                ln1_fm[:, ci * T + tc * 512:ci * T + (tc + 1) * 512],
                                start=(ci == 0), stop=(ci == 7))
                        return ii
                    w.op("tensor", qproj, inc="sT")
                    w.wait("vector", "sT", w.v("sT"))
                    w.op("vector", lambda e, jc=jc, qslot=qslot, tc=tc: e.tensor_scalar(
                        out=q_fm[:, jc * T + tc * 512:jc * T + (tc + 1) * 512],
                        in0=psQ[:, qslot * 512:(qslot + 1) * 512],
                        scalar1=qb_cols[:, jc:jc + 1], scalar2=None,
                        op0=ALU.add), inc="sV")
                    qevac_done[jc] = w.v("sV")

            p1ldstats(0)
            for tc in range(NTC):
                p1fronts(tc)
                if tc + 1 < NTC:
                    p1ldstats(tc + 1)
                p1body(tc)
                p1q(tc)

            for sp in (NSUB - 3, NSUB - 2, NSUB - 1):
                w.wait("tensor", "sP", khat_done[sp])

                def kvacc_last(e, sp=sp):
                    ii = None
                    lsp = sp % 3
                    for p in range(8):
                        ii = e.matmul(
                            psACC[:, p * 128:(p + 1) * 128],
                            khat[:, lsp * 2048 + p * 128:
                                 lsp * 2048 + (p + 1) * 128],
                            khat[:, lsp * 2048 + 1024 + p * 128:
                                 lsp * 2048 + 1024 + (p + 1) * 128],
                            start=False, stop=(sp == NSUB - 1),
                            skip_group_check=True)
                    return ii
                w.op("tensor", kvacc_last, inc="sT")
                kvacc_done[sp] = w.v("sT")

            if b == 0:
                w.wait("tensor", "sS", gred_done[63])
                w.wait("tensor", "sV", qevac_done[7])

                def gtotmm(e):
                    e.matmul(psQ[0:1, 0:32], onescol_f[:],
                             red64[:, 0:32], start=True, stop=True)
                    return e.matmul(psQ[0:1, 32:64], onescol_f[:],
                                    red64[:, 32:64], start=True, stop=True,
                                    skip_group_check=True)
                w.op("tensor", gtotmm, inc="sT")
                w.wait("vector", "sT", w.v("sT"))
                w.vops("vector", [
                    lambda e: e.tensor_reduce(out=g11[0:1, 2:3],
                                              in_=psQ[0:1, 0:32], axis=AX.X,
                                              op=ALU.add),
                    lambda e: e.tensor_reduce(out=g11[0:1, 3:4],
                                              in_=psQ[0:1, 32:64], axis=AX.X,
                                              op=ALU.add),
                    lambda e: e.tensor_scalar(
                        out=g11[0:1, 4:6], in0=g11[0:1, 2:4], scalar1=1.0 / WN,
                        scalar2=EPS, op0=ALU.mult, op1=ALU.add),
                    lambda e: e.reciprocal(g11[0:1, 6:8], g11[0:1, 4:6]),
                    lambda e: e.tensor_scalar(
                        out=g11[0:1, 8:10], in0=g11[0:1, 6:8], scalar1=-1.0,
                        scalar2=None, op0=ALU.mult)])
                w.wait("gpsimd", "sV", w.v("sV"))
                w.op("gpsimd", lambda e: e.dma_start(out=smscr_d[0:4, 0:1],
                                                     in_=g11[0:1, 6:10]),
                     inc="sGM", k=16)
                w.wait("sync", "sGM", w.v("sGM"))
                w.op("sync", lambda e: e.dma_start(
                    out=invg_col[:, 0:4],
                    in_=AP(smscr_d.tensor, 0, [[0, 128], [1, 4]])), inc="sIG", k=16)
                w.wait("sync", "sIG", w.v("sIG"))

            w.wait("vector", "sT", kvacc_done[NSUB - 1])

            def kvfin(e):
                ii = None
                for h in range(H):
                    p, r = h // 2, h % 2
                    base = r * 64
                    ii = e.scalar_tensor_tensor(
                        out=kvbd[base:base + 64,
                                 p * 128 + r * 64: p * 128 + r * 64 + 64],
                        in0=psACC[base:base + 64,
                                  p * 128 + r * 64: p * 128 + r * 64 + 64],
                        scalar=tempQ[base:base + 64, p:p + 1],
                        in1=abias_sb[base:base + 64, p * 64:(p + 1) * 64],
                        op0=ALU.mult, op1=ALU.add)
                return ii
            w.op("vector", kvfin, inc="sV")

            if debug:
                w.wait("gpsimd", "sV", w.v("sV"))
                w.wait("gpsimd", "sS", w.v("sS"))
                w.wait("gpsimd", "sT", w.v("sT"))
                for nm, src in [("d_kvbd", kvbd[:]), ("d_ln1fm", ln1_fm[:]),
                                ("d_q", q_fm[:]), ("d_khat", khat[:])]:
                    w.op("gpsimd", lambda e, nm=nm, src=src: e.dma_start(
                        out=dbg[nm], in_=src), inc="sDO", k=16)
                w.wait("gpsimd", "sDO", w.v("sDO"))

        if 'P1' in run_p:
            w.block(p1, f"p1_{b}")
        p1sc.close()

        p2sc = ExitStack()
        p1T_sb = p2sc.enter_context(nc.sbuf_tensor(f"p1T{b}", [128, 8 * C], BF16))
        p2T_sb = p2sc.enter_context(nc.sbuf_tensor(f"p2T{b}", [128, 8 * C], BF16))
        xring2 = p2sc.enter_context(nc.sbuf_tensor(f"xr2{b}", [128, 2 * 4096], F32))
        att2 = p2sc.enter_context(nc.sbuf_tensor(f"att2{b}", [128, 8 * 512], BF16))
        hring = p2sc.enter_context(nc.sbuf_tensor(f"hring{b}", [128, 8 * 512], BF16))
        aoring = p2sc.enter_context(nc.sbuf_tensor(f"aor{b}", [128, 4 * 1024], BF16))
        zring = p2sc.enter_context(nc.sbuf_tensor(f"zr{b}", [128, 1024], F32))
        ln1tm2 = p2sc.enter_context(nc.sbuf_tensor(f"l1b{b}", [128, 1024], BF16))
        y1ring = p2sc.enter_context(nc.sbuf_tensor(f"y1r{b}", [128, 2 * 1024], F32))
        psATT = p2sc.enter_context(nc.psum_tensor(f"psATT{b}", [128, 1024], F32))
        psP1 = p2sc.enter_context(nc.psum_tensor(f"psP1{b}", [128, 1024], F32))
        psP2 = p2sc.enter_context(nc.psum_tensor(f"psP2{b}", [128, 2048], F32))
        qw_ring = p2sc.enter_context(nc.sbuf_tensor(f"qwr{b}", [128, 2 * 1024], F32))
        qbuf = p2sc.enter_context(nc.sbuf_tensor(f"qbuf{b}", [128, 4 * 1024], F8))
        qbufB = p2sc.enter_context(nc.sbuf_tensor(f"qbufB{b}", [128, 2 * 1024], F8))
        qsA = p2sc.enter_context(nc.sbuf_tensor(f"qsA{b}", [128, 1024], F32))

        def p2(w, b=b, ln1_fm=ln1_fm, q_fm=q_fm, p1T_sb=p1T_sb, p2T_sb=p2T_sb,
               xring2=xring2, att2=att2, hring=hring, aoring=aoring, zring=zring,
               ln1tm2=ln1tm2, y1ring=y1ring, psATT=psATT, psP1=psP1, psP2=psP2,
               qw_ring=qw_ring, qbuf=qbuf, qbufB=qbufB, qsA=qsA):
            qtt_done = {}
            qst_done = {}
            qm_done = {}
            qld = {}
            qctr = [0]

            def quant_load(gi):
                if b != 0 or gi >= 64:
                    return
                wr = "sWR0" if gi % 2 == 0 else "sWR1"
                slot = gi % 2
                if gi >= 2:
                    w.wait("sync", *qtt_done[gi - 2])
                w.op("sync", lambda e, gi=gi, slot=slot: e.dma_start(
                    out=qw_ring[:, slot * 1024:(slot + 1) * 1024],
                    in_=wchunk_src(gi)), inc=wr, k=16)
                qld[gi] = (wr, w.v(wr))

            def quant_chunk(_ignored=None):
                if b != 0:
                    return
                gi = qctr[0]
                if gi >= 64:
                    return
                qctr[0] += 1
                quant_comp(gi)
                quant_load(gi + 2)

            def quant_comp(gi):
                so = "sSO0" if gi % 2 == 0 else "sSO1"
                gc = 0 if gi < 32 else 1
                slot = gi % 2
                if gi % 2 == 0:
                    w.wait("vector", *qld[gi])
                    w.wait("vector", "sV", w.v("sV"))
                    if gi >= 2:
                        w.wait("vector", *qtt_done[gi - 2])
                    w.op("vector", lambda e, slot=slot, gc=gc: e.tensor_scalar(
                        out=qbuf[:, 0:1024],
                        in0=qw_ring[:, slot * 1024:(slot + 1) * 1024],
                        scalar1=invg_col[:, gc:gc + 1], scalar2=0.5,
                        op0=ALU.mult, op1=ALU.is_ge), inc="sV")
                    w.wait("vector", "sV", w.v("sV"))
                    w.op("vector", lambda e, slot=slot, gc=gc: e.tensor_scalar(
                        out=qbuf[:, 1024:2048],
                        in0=qw_ring[:, slot * 1024:(slot + 1) * 1024],
                        scalar1=invg_col[:, 2 + gc:3 + gc], scalar2=0.5,
                        op0=ALU.mult, op1=ALU.is_ge), inc="sV")
                    w.wait("vector", "sV", w.v("sV"))
                    qts_done = w.v("sV")
                    w.wait("gpsimd", "sV", qts_done)
                    op = ALU.subtract
                    src0, src1 = qbuf[:, 0:1024], qbuf[:, 1024:2048]
                else:
                    w.wait("scalar", *qld[gi])
                    w.wait("scalar", "sS", w.v("sS"))
                    if gi >= 2:
                        w.wait("scalar", "sV", qm_done.get(gi - 2, 0))
                    w.op("scalar", lambda e, slot=slot, gc=gc: e.activation(
                        out=qsA[:],
                        in_=qw_ring[:, slot * 1024:(slot + 1) * 1024],
                        func=AF.Abs, scale=invg_col[:, gc:gc + 1]), inc="sS")
                    qa_done = w.v("sS")
                    if gi >= 2:
                        w.wait("scalar", *qtt_done[gi - 2])
                    w.op("scalar", lambda e, slot=slot: e.activation(
                        out=qbufB[:, 1024:2048],
                        in_=qw_ring[:, slot * 1024:(slot + 1) * 1024],
                        func=AF.Sign), inc="sS")
                    qsg_done = w.v("sS")
                    w.wait("vector", "sS", qa_done)
                    if gi >= 2:
                        w.wait("vector", *qtt_done[gi - 2])
                    w.op("vector", lambda e: e.tensor_scalar(
                        out=qbufB[:, 0:1024], in0=qsA[:],
                        scalar1=0.5, scalar2=None, op0=ALU.is_ge), inc="sV")
                    qm_done[gi] = w.v("sV")
                    w.wait("gpsimd", "sV", qm_done[gi])
                    w.wait("gpsimd", "sS", qsg_done)
                    op = ALU.mult
                    src0, src1 = qbufB[:, 0:1024], qbufB[:, 1024:2048]
                if gi >= 2:
                    w.wait("gpsimd", so, qst_done[gi - 2])
                w.op("gpsimd", lambda e, slot=slot, op=op, src0=src0, src1=src1:
                     e.tensor_tensor(
                         out=qbuf[:, (2 + slot) * 1024:(3 + slot) * 1024],
                         in0=src0, in1=src1, op=op), inc="sP")
                qtt_done[gi] = ("sP", w.v("sP"))
                w.wait("gpsimd", "sP", w.v("sP"))
                w.op("gpsimd", lambda e, gi=gi, slot=slot: e.dma_start(
                    out=wchunk_dst(gi),
                    in_=qbuf[:, (2 + slot) * 1024:(3 + slot) * 1024]), inc=so, k=16)
                qst_done[gi] = w.v(so)
            w.op("gpsimd", lambda e: e.dma_start(
                out=p1T_sb[:].rearrange("p (a j) -> p a j", j=C),
                in_=p1wT_d.rearrange("(a p) j -> p a j", p=128)), inc="sDW", k=16)
            w.op("gpsimd", lambda e: e.dma_start(
                out=p2T_sb[:].rearrange("p (a j) -> p a j", j=C),
                in_=p2wT_d.rearrange("(a p) j -> p a j", p=128)), inc="sDW", k=16)
            wload = w.v("sDW")
            quant_load(0)
            quant_load(1)

            att2_done = {}
            gelu_done = {}
            p2mm_done = {}
            p1_last = {}
            ao_done = {}
            zs_done = {}
            y1tt_done = {}
            y1st_done = {}
            lnb_done = {}
            chain1_done = {}
            xload_v = {}

            def xld(tc):
                x2slot = tc % 2
                if tc >= 2:
                    w.wait("sync", "sS", lnb_done[tc - 2])
                    w.wait("sync", "sV", y1tt_done[(tc - 1) * 4 - 1])
                xsem = "sXL" if x2slot == 0 else "sXL1"
                w.op("sync", lambda e, tc=tc, x2slot=x2slot: e.dma_start(
                    out=xring2[:, x2slot * 4096:(x2slot + 1) * 4096].rearrange(
                        "p (s d) -> p s d", d=C),
                    in_=x_d[b, tc * 512:(tc + 1) * 512, :].rearrange(
                        "(s p) d -> p s d", p=128)), inc=xsem, k=16)
                xload_v[tc] = (xsem, w.v(xsem))

            def attf(tc):
                for p in range(8):
                    if p >= 2:
                        w.wait("tensor", "sV", att2_done[p - 2])
                    elif tc >= 1:
                        w.wait("tensor", "sV", att2_done[6 + p])
                    w.op("tensor", lambda e, p=p, tc=tc: e.matmul(
                        psATT[:, (p % 2) * 512:(p % 2 + 1) * 512],
                        kvbd[:, p * 128:(p + 1) * 128],
                        q_fm[:, p * T + tc * 512:p * T + (tc + 1) * 512],
                        start=True, stop=True), inc="sT")
                    w.wait("vector", "sT", w.v("sT"))
                    if tc >= 1 and p == 0:
                        w.wait("vector", "sT", p1_last[tc - 1])
                    w.op("vector", lambda e, p=p, tc=tc: e.tensor_tensor(
                        out=att2[:, p * 512:(p + 1) * 512],
                        in0=psATT[:, (p % 2) * 512:(p % 2 + 1) * 512],
                        in1=ln1_fm[:, p * T + tc * 512:p * T + (tc + 1) * 512],
                        op=ALU.add), inc="sV")
                    att2_done[p] = w.v("sV")

            def p1f(tc):
                for jc in range(8):
                    qslot = jc % 2
                    w.wait("tensor", "sV", att2_done[7])
                    if tc == 0 and jc == 0:
                        w.wait("tensor", "sDW", wload)
                    if jc >= 2:
                        w.wait("tensor", "sS", gelu_done[jc - 2])
                    elif tc >= 1:
                        w.wait("tensor", "sS", gelu_done[6 + qslot])

                    def p1mm(e, jc=jc, qslot=qslot):
                        ii = None
                        for ci in range(8):
                            ii = e.matmul(
                                psP1[:, qslot * 512:(qslot + 1) * 512],
                                p1T_sb[:, ci * C + jc * 128:ci * C + (jc + 1) * 128],
                                att2[:, ci * 512:(ci + 1) * 512],
                                start=(ci == 0), stop=(ci == 7))
                        return ii
                    w.op("tensor", p1mm, inc="sT")
                    p1_last[tc] = w.v("sT")
                    w.wait("scalar", "sT", w.v("sT"))
                    if tc >= 1 and jc == 0:
                        w.wait("scalar", "sT", p2mm_done[tc * 4 - 1])
                    w.op("scalar", lambda e, jc=jc, qslot=qslot: e.activation(
                        out=hring[:, jc * 512:(jc + 1) * 512],
                        in_=psP1[:, qslot * 512:(qslot + 1) * 512],
                        func=AF.Gelu, bias=p1b_cols[:, jc:jc + 1]), inc="sS")
                    gelu_done[jc] = w.v("sS")

            def body(tc):
                x2slot = tc % 2
                xsem, xload = xload_v[tc]
                for s4 in range(4):
                    s = tc * 4 + s4
                    pslot = s4 % 2
                    lslot = s4 % 2
                    w.wait("scalar", xsem, xload)
                    if s >= 2:
                        w.wait("scalar", "sV", ao_done.get(s - 2, 0))
                    if s >= 1:
                        w.wait("scalar", "sV", ao_done.get(s - 1, 0))
                    w.op("scalar", lambda e, s=s, s4=s4,
                         x2slot=x2slot: e.activation(
                        out=ln1tm2[:],
                        in_=xring2[:, x2slot * 4096 + s4 * 1024:
                                   x2slot * 4096 + (s4 + 1) * 1024],
                        func=AF.Identity, scale=r1_sb[:, s:s + 1],
                        bias=nmr1_sb[:, s:s + 1]), inc="sS")
                    lnb_done[tc] = w.v("sS")

                    w.wait("tensor", "sS", gelu_done[7])
                    if s4 >= 2:
                        w.wait("tensor", "sV", ao_done[s - 2])

                    def p2mm(e, s4=s4, pslot=pslot):
                        ii = None
                        for jh in range(2):
                            reg = psP2[:, pslot * 1024 + jh * 512:
                                       pslot * 1024 + (jh + 1) * 512]
                            e.matmul(reg, ones_row[:],
                                     p2b_row[0:1, jh * 512:(jh + 1) * 512],
                                     start=True, stop=False)
                            for ci in range(8):
                                ii = e.matmul(
                                    reg,
                                    hring[:, ci * 512 + s4 * 128:
                                          ci * 512 + s4 * 128 + 128],
                                    p2T_sb[:, ci * C + jh * 512:
                                           ci * C + (jh + 1) * 512],
                                    start=False, stop=(ci == 7))
                        return ii
                    w.op("tensor", p2mm, inc="sT")
                    p2mm_done[s] = w.v("sT")
                    w.wait("vector", "sT", p2mm_done[s])
                    w.wait("vector", "sS", lnb_done[tc])
                    if s >= 4:
                        w.wait("vector", "sS", zs_done[s - 4])
                    w.op("vector", lambda e, s4=s4, pslot=pslot:
                         e.tensor_tensor(
                             out=aoring[:, s4 * 1024:(s4 + 1) * 1024],
                             in0=psP2[:, pslot * 1024:(pslot + 1) * 1024],
                             in1=ln1tm2[:],
                             op=ALU.add), inc="sV")
                    ao_done[s] = w.v("sV")
                    quant_chunk()
                    quant_chunk()
                    w.wait("scalar", "sV", ao_done[s])
                    if tc >= 1 and s4 == 0:
                        w.wait("scalar", "sV", chain1_done[tc - 1])
                    w.wait("scalar", "sS", w.v("sS"))
                    w.op("scalar", lambda e, s4=s4: e.activation(
                        out=junk2[:], in_=aoring[:, s4 * 1024:(s4 + 1) * 1024],
                        func=AF.Square, accum_out=ssqc[:, s4:s4 + 1]), inc="sS")
                    w.wait("vector", "sV", w.v("sV"))
                    w.op("vector", lambda e, s4=s4: e.tensor_reduce(
                        out=sxc[:, s4:s4 + 1],
                        in_=aoring[:, s4 * 1024:(s4 + 1) * 1024],
                        axis=AX.X, op=ALU.add), inc="sV")

                w.wait("vector", "sS", w.v("sS"))
                w.wait("vector", "sV", w.v("sV"))

                w.vops("vector", [
                    lambda e: e.tensor_scalar(out=negm[:], in0=sxc[:],
                                              scalar1=-1.0 / C, scalar2=None,
                                              op0=ALU.mult),
                    lambda e: e.tensor_tensor(out=msq[:], in0=negm[:],
                                              in1=negm[:], op=ALU.mult),
                    lambda e: e.scalar_tensor_tensor(
                        out=vvar[:], in0=ssqc[:], scalar=1.0 / C, in1=msq[:],
                        op0=ALU.mult, op1=ALU.subtract)])
                chain1_done[tc] = w.v("sV")
                w.wait("scalar", "sV", w.v("sV"))
                w.op("scalar", lambda e: e.activation(
                    out=srt[:], in_=vvar[:], func=AF.Sqrt, bias=ceps[:]), inc="sS")
                w.wait("vector", "sS", w.v("sS"))

                w.vops("vector", [
                    lambda e: e.reciprocal(rrc[:], srt[:]),
                    lambda e: e.tensor_tensor(out=nmrc[:], in0=negm[:],
                                              in1=rrc[:], op=ALU.mult)])
                stats_done = w.v("sV")

                for s4 in range(4):
                    s = tc * 4 + s4
                    zslot = s4 % 2
                    w.wait("scalar", "sV", stats_done)
                    if s >= 2:
                        w.wait("scalar", "sV", y1tt_done.get(s - 2, 0))
                    if s >= 1:
                        w.wait("scalar", "sV", y1tt_done.get(s - 1, 0))
                    w.op("scalar", lambda e, s4=s4: e.activation(
                        out=zring[:],
                        in_=aoring[:, s4 * 1024:(s4 + 1) * 1024],
                        func=AF.Identity, scale=rrc[:, s4:s4 + 1],
                        bias=nmrc[:, s4:s4 + 1]), inc="sS")
                    zs_done[s] = w.v("sS")
                    sy = "sY0" if zslot == 0 else "sY1"
                    w.wait("vector", xsem, xload)
                    w.wait("vector", "sS", zs_done[s])
                    if s >= 2:
                        w.wait("vector", sy, y1st_done.get(s - 2, 0))
                    w.op("vector", lambda e, s4=s4, zslot=zslot,
                         x2slot=x2slot: e.tensor_tensor(
                        out=y1ring[:, zslot * 1024:(zslot + 1) * 1024],
                        in0=xring2[:, x2slot * 4096 + s4 * 1024:
                                   x2slot * 4096 + (s4 + 1) * 1024],
                        in1=zring[:],
                        op=ALU.add), inc="sV")
                    y1tt_done[s] = w.v("sV")
                    w.wait("gpsimd", "sV", y1tt_done[s])
                    w.op("gpsimd", lambda e, s=s, zslot=zslot: e.dma_start(
                        out=y1s_d[b, s * 128:(s + 1) * 128, :],
                        in_=y1ring[:, zslot * 1024:(zslot + 1) * 1024]),
                        inc=sy, k=16)
                    y1st_done[s] = w.v(sy)
                    quant_chunk()
                    quant_chunk()

            xld(0)
            attf(0)
            p1f(0)
            for tc in range(NTC):
                if tc + 1 < NTC:
                    xld(tc + 1)
                    attf(tc + 1)
                body(tc)
                if tc + 1 < NTC:
                    p1f(tc + 1)

            w.wait("gpsimd", "sY0", w.v("sY0"))
            w.wait("gpsimd", "sY1", w.v("sY1"))
            w.wait("gpsimd", "sSO0", w.v("sSO0"))
            w.wait("gpsimd", "sSO1", w.v("sSO1"))
            if debug:
                w.op("gpsimd", lambda e: e.dma_start(out=dbg["d_att2"],
                                                     in_=att2[:]), inc="sDO", k=16)
                w.wait("gpsimd", "sDO", w.v("sDO"))

        if 'P2' in run_p:
            w.block(p2, f"p2_{b}")
        p2sc.close()
        absc.close()

        p3sc = ExitStack()
        wq1_sb = p3sc.enter_context(nc.sbuf_tensor(f"wq1{b}", [128, 8 * FF], F8))
        wq2_sb = p3sc.enter_context(nc.sbuf_tensor(f"wq2{b}", [128, 32 * C], F8))
        y1r3 = p3sc.enter_context(nc.sbuf_tensor(f"y1r3{b}", [128, 4 * 2048], F32))
        ln2tm = p3sc.enter_context(nc.sbuf_tensor(f"l2t{b}", [128, 2 * 1024], BF16))
        ln2fm = p3sc.enter_context(nc.sbuf_tensor(f"l2f{b}", [128, 2 * 2048], BF16))
        h1 = p3sc.enter_context(nc.sbuf_tensor(f"h1_{b}", [128, 2 * 32 * 256], BF16))
        ffnr = p3sc.enter_context(nc.sbuf_tensor(f"ffnr{b}", [128, 2 * 1024], F32))
        z3r = p3sc.enter_context(nc.sbuf_tensor(f"z3r{b}", [128, 1024], F32))
        outr = p3sc.enter_context(nc.sbuf_tensor(f"outr{b}", [128, 1024], F32))
        psF1 = p3sc.enter_context(nc.psum_tensor(f"psF1{b}", [128, 2048], F32))
        psF2 = p3sc.enter_context(nc.psum_tensor(f"psF2{b}", [128, 2048], F32))

        def p3(w, b=b, wq1_sb=wq1_sb, wq2_sb=wq2_sb, y1r3=y1r3, ln2tm=ln2tm,
               ln2fm=ln2fm, h1=h1, ffnr=ffnr, z3r=z3r, outr=outr, psF1=psF1,
               psF2=psF2):
            w.op("sync", lambda e: e.dma_start(
                out=wq1_sb[:].rearrange("p (a f) -> p a f", f=FF),
                in_=wq1s_d.rearrange("(a p) f -> p a f", p=128)), inc="sWQ", k=16)
            wload = w.v("sWQ")
            w.op("sync", lambda e: e.dma_start(
                out=wq2_sb[:].rearrange("p (a d) -> p a d", d=C),
                in_=wq2s_d.rearrange("(a p) d -> p a d", p=128)), inc="sGB", k=16)
            wload2 = w.v("sGB")

            evac_done = {}
            tr_done3 = {}
            relu_done = {}
            f1_last = {}
            f2_done = {}
            ffn_done = {}
            outtt_done = {}
            outst_done = {}
            zs_done = {}
            apl_done = {}
            chain1_done = {}
            chain3_done = {}

            yload_map = {}

            def yld3(tc3):
                yslot = tc3 % 4
                ysem = ("sXL", "sXL1", "sXL2", "sXL3")[yslot]
                if tc3 >= 4:
                    w.wait("sync", "sV", outtt_done.get(2 * (tc3 - 4) + 1, 0))
                    w.wait("sync", "sV", apl_done.get(2 * (tc3 - 4) + 1, 0))
                w.op("sync", lambda e, tc3=tc3, yslot=yslot: e.dma_start(
                    out=y1r3[:, yslot * 2048:(yslot + 1) * 2048].rearrange(
                        "p (s d) -> p s d", d=C),
                    in_=y1s_d[b, tc3 * 256:(tc3 + 1) * 256, :].rearrange(
                        "(s p) d -> p s d", p=128)), inc=ysem, k=16)
                yload_map[tc3] = w.v(ysem)

            def p3front(tc3):
                slot = tc3 % 2
                s_hi = 2 * tc3 + 1
                yslot = tc3 % 4
                ysem = ("sXL", "sXL1", "sXL2", "sXL3")[yslot]
                yload = yload_map[tc3]

                w.wait("scalar", ysem, yload)
                if tc3 >= 1:
                    w.wait("scalar", "sV", chain1_done[tc3 - 1])
                w.wait("vector", ysem, yload)
                for s2 in range(2):
                    ysub = y1r3[:, yslot * 2048 + s2 * 1024:
                                yslot * 2048 + (s2 + 1) * 1024]
                    w.wait("scalar", "sS", w.v("sS"))
                    w.op("scalar", lambda e, ysub=ysub, s2=s2: e.activation(
                        out=junk2[:], in_=ysub, func=AF.Square,
                        accum_out=ssqc[:, s2:s2 + 1]), inc="sS")
                    w.op("vector", lambda e, ysub=ysub, s2=s2: e.tensor_reduce(
                        out=sxc[:, s2:s2 + 1], in_=ysub, axis=AX.X, op=ALU.add),
                        inc="sV" if s2 == 1 else None)
                w.wait("vector", "sS", w.v("sS"))
                w.wait("vector", "sV", w.v("sV"))


                w.vops("vector", [
                    lambda e: e.tensor_scalar(out=negm[:, 0:2], in0=sxc[:, 0:2],
                                              scalar1=-1.0 / C, scalar2=None,
                                              op0=ALU.mult),
                    lambda e: e.tensor_tensor(out=msq[:, 0:2], in0=negm[:, 0:2],
                                              in1=negm[:, 0:2], op=ALU.mult),
                    lambda e: e.scalar_tensor_tensor(
                        out=vvar[:, 0:2], in0=ssqc[:, 0:2], scalar=1.0 / C,
                        in1=msq[:, 0:2], op0=ALU.mult, op1=ALU.subtract)])
                chain1_done[tc3] = w.v("sV")
                w.wait("scalar", "sV", w.v("sV"))
                w.op("scalar", lambda e: e.activation(
                    out=srt[:, 0:2], in_=vvar[:, 0:2], func=AF.Sqrt,
                    bias=ceps[:]), inc="sS")
                w.wait("vector", "sS", w.v("sS"))

                w.vops("vector", [
                    lambda e: e.reciprocal(rrc[:, 0:2], srt[:, 0:2]),
                    lambda e: e.tensor_tensor(out=nmrc[:, 0:2], in0=negm[:, 0:2],
                                              in1=rrc[:, 0:2], op=ALU.mult)])
                stats_done = w.v("sV")

                for s2 in range(2):
                    s = tc3 * 2 + s2
                    lsl = s % 2
                    w.wait("vector", "sV", stats_done)
                    if s >= 2:
                        w.wait("vector", *tr_done3[s - 2])
                    w.op("vector", lambda e, s2=s2, yslot=yslot, lsl=lsl:
                         e.scalar_tensor_tensor(
                             out=ln2tm[:, lsl * 1024:(lsl + 1) * 1024],
                             in0=y1r3[:, yslot * 2048 + s2 * 1024:
                                      yslot * 2048 + (s2 + 1) * 1024],
                             scalar=rrc[:, s2:s2 + 1],
                             in1=nmrc[:, s2:s2 + 1].broadcast_to([128, 1024]),
                             op0=ALU.mult, op1=ALU.add), inc="sV")
                    apl_done[s] = w.v("sV")
                    if tc3 >= 2 and s2 == 0:
                        w.wait("scalar", "sT", f1_last.get(tc3 - 2, 0))
                    trs = "sTR0" if lsl == 0 else "sTR1"
                    w.wait("scalar", "sV", apl_done[s])
                    w.op("scalar", lambda e, s2=s2, slot=slot, lsl=lsl: e.dma_start(
                        out=AP(ln2fm, slot * 2048 + s2 * 128,
                               [[2 * 2048, 128], [256, 8], [1, 128]]),
                        in_=ln2tm[:, lsl * 1024:(lsl + 1) * 1024],
                        transpose=True), inc=trs, k=16)
                    tr_done3[s] = (trs, w.v(trs))

            relu_last = {}

            def f1blk(tc3):
                slot = tc3 % 2
                hsl = tc3 % 2
                s_hi = 2 * tc3 + 1
                for fc in (range(32) if p3sub in ('full', 'f1') else []):
                    fslot = fc % 4
                    if fc == 0:
                        w.wait("tensor", *tr_done3[s_hi - 1])
                        w.wait("tensor", *tr_done3[s_hi])
                    if tc3 == 0 and fc == 0:
                        w.wait("tensor", "sWQ", wload)
                    if fc >= 4:
                        w.wait("tensor", "sS", relu_done[fc - 4])
                    elif tc3 >= 1:
                        w.wait("tensor", "sS", relu_done.get(28 + fslot, 0))

                    def f1mm(e, fc=fc, fslot=fslot, slot=slot):
                        ii = None
                        for ci in range(8):
                            ii = e.matmul(
                                psF1[:, fslot * 512:fslot * 512 + 256],
                                wq1_sb[:, ci * FF + fc * 128:
                                       ci * FF + (fc + 1) * 128],
                                ln2fm[:, slot * 2048 + ci * 256:
                                      slot * 2048 + (ci + 1) * 256],
                                start=(ci == 0), stop=(ci == 7))
                        return ii
                    w.op("tensor", f1mm, inc="sT")
                    if fc == 31:
                        f1_last[tc3] = w.v("sT")
                    w.wait("scalar", "sT", w.v("sT"))
                    if tc3 >= 2 and fc == 0:
                        w.wait("scalar", "sT", f2_done.get(2 * (tc3 - 2) + 1, 0))
                    w.op("scalar", lambda e, fc=fc, fslot=fslot, hsl=hsl:
                         e.activation(
                             out=h1[:, hsl * 8192 + fc * 256:
                                    hsl * 8192 + (fc + 1) * 256],
                             in_=psF1[:, fslot * 512:fslot * 512 + 256],
                             func=AF.Relu, bias=b1_cols[:, fc:fc + 1]),
                        inc="sS")
                    relu_done[fc] = w.v("sS")
                relu_last[tc3] = relu_done.get(31, 0)

            def f2blk(tc3):
                slot = tc3 % 2
                hsl = tc3 % 2
                yslot = tc3 % 4
                ysem = ("sXL", "sXL1", "sXL2", "sXL3")[yslot]
                yload = yload_map[tc3]

                if p3sub in ('ln2', 'f1'):
                    for s2 in range(2):
                        s = tc3 * 2 + s2
                        if s >= 1:
                            w.wait("vector", "sOS", outst_done.get(s - 1, 0))
                        w.wait("vector", "sXL", yload)
                        w.op("vector", lambda e, s2=s2: e.tensor_copy(
                            out=outr[:], in_=y1r3[:, s2 * 1024:(s2 + 1) * 1024]),
                            inc="sV")
                        outtt_done[s] = w.v("sV")
                        apl_done.setdefault(s, 0)
                        w.wait("gpsimd", "sV", outtt_done[s])
                        w.op("gpsimd", lambda e, s=s: e.dma_start(
                            out=out_d[b, s * 128:(s + 1) * 128, :], in_=outr[:]),
                            inc="sOS", k=16)
                        outst_done[s] = w.v("sOS")
                    return
                for s2 in range(2):
                    s = tc3 * 2 + s2
                    pslot = s2
                    w.wait("tensor", "sS", relu_last.get(tc3, 0))
                    if tc3 == 0 and s2 == 0:
                        w.wait("tensor", "sGB", wload2)
                    if s >= 2:
                        w.wait("tensor", "sV", ffn_done.get(s - 2, 0))

                    def f2mm(e, s2=s2, pslot=pslot, hsl=hsl):
                        ii = None
                        for jh in range(2):
                            reg = psF2[:, pslot * 1024 + jh * 512:
                                       pslot * 1024 + (jh + 1) * 512]
                            for fc in range(32):
                                ii = e.matmul(
                                    reg,
                                    h1[:, hsl * 8192 + fc * 256 + s2 * 128:
                                       hsl * 8192 + fc * 256 + s2 * 128 + 128],
                                    wq2_sb[:, fc * C + jh * 512:
                                           fc * C + (jh + 1) * 512],
                                    start=(fc == 0), stop=(fc == 31))
                        return ii
                    w.op("tensor", f2mm, inc="sT")
                    f2_done[s] = w.v("sT")
                    w.wait("vector", "sT", f2_done[s])
                    w.op("vector", lambda e, s2=s2, pslot=pslot:
                         e.tensor_tensor(
                             out=ffnr[:, s2 * 1024:(s2 + 1) * 1024],
                             in0=psF2[:, pslot * 1024:(pslot + 1) * 1024],
                             in1=B2tile[:], op=ALU.add), inc="sV")
                    ffn_done[s] = w.v("sV")
                    w.wait("scalar", "sV", ffn_done[s])
                    if tc3 >= 1 and s2 == 0:
                        w.wait("scalar", "sV", chain3_done[tc3 - 1])
                    w.wait("scalar", "sS", w.v("sS"))
                    w.op("scalar", lambda e, s2=s2: e.activation(
                        out=junk2[:], in_=ffnr[:, s2 * 1024:(s2 + 1) * 1024],
                        func=AF.Square, accum_out=ssqc[:, 2 + s2:3 + s2]),
                        inc="sS")
                    w.wait("vector", "sS", w.v("sS"))
                    w.wait("vector", "sV", w.v("sV"))
                    w.op("vector", lambda e, s2=s2: e.tensor_reduce(
                        out=sxc[:, 2 + s2:3 + s2],
                        in_=ffnr[:, s2 * 1024:(s2 + 1) * 1024],
                        axis=AX.X, op=ALU.add), inc="sV")

                w.wait("vector", "sS", w.v("sS"))
                w.wait("vector", "sV", w.v("sV"))


                w.vops("vector", [
                    lambda e: e.tensor_scalar(out=negm[:, 2:4], in0=sxc[:, 2:4],
                                              scalar1=-1.0 / C, scalar2=None,
                                              op0=ALU.mult),
                    lambda e: e.tensor_tensor(out=msq[:, 2:4], in0=negm[:, 2:4],
                                              in1=negm[:, 2:4], op=ALU.mult),
                    lambda e: e.scalar_tensor_tensor(
                        out=vvar[:, 2:4], in0=ssqc[:, 2:4], scalar=1.0 / C,
                        in1=msq[:, 2:4], op0=ALU.mult, op1=ALU.subtract)])
                chain3_done[tc3] = w.v("sV")
                w.wait("scalar", "sV", w.v("sV"))
                w.op("scalar", lambda e: e.activation(
                    out=srt[:, 2:4], in_=vvar[:, 2:4], func=AF.Sqrt,
                    bias=ceps[:]), inc="sS")
                w.wait("vector", "sS", w.v("sS"))

                w.vops("vector", [
                    lambda e: e.reciprocal(rrc[:, 2:4], srt[:, 2:4]),
                    lambda e: e.tensor_tensor(out=nmrc[:, 2:4], in0=negm[:, 2:4],
                                              in1=rrc[:, 2:4], op=ALU.mult)])
                stats3 = w.v("sV")

                for s2 in range(2):
                    s = tc3 * 2 + s2
                    w.wait("vector", "sV", stats3)
                    if s >= 1:
                        w.wait("vector", "sV", outtt_done.get(s - 1, 0))
                    w.op("vector", lambda e, s2=s2: e.scalar_tensor_tensor(
                        out=z3r[:], in0=ffnr[:, s2 * 1024:(s2 + 1) * 1024],
                        scalar=rrc[:, 2 + s2:3 + s2],
                        in1=nmrc[:, 2 + s2:3 + s2].broadcast_to([128, 1024]),
                        op0=ALU.mult, op1=ALU.add), inc="sV")
                    zs_done[s] = w.v("sV")
                    w.wait("vector", "sV", zs_done[s])
                    w.wait("vector", ysem, yload)
                    if s >= 1:
                        w.wait("vector", "sOS", outst_done.get(s - 1, 0))
                    w.op("vector", lambda e, s2=s2, yslot=yslot: e.tensor_tensor(
                        out=outr[:],
                        in0=y1r3[:, yslot * 2048 + s2 * 1024:
                                 yslot * 2048 + (s2 + 1) * 1024],
                        in1=z3r[:], op=ALU.add), inc="sV")
                    outtt_done[s] = w.v("sV")
                    w.wait("gpsimd", "sV", outtt_done[s])
                    w.op("gpsimd", lambda e, s=s: e.dma_start(
                        out=out_d[b, s * 128:(s + 1) * 128, :], in_=outr[:]),
                        inc="sOS", k=16)
                    outst_done[s] = w.v("sOS")

            yld3(0)
            yld3(1)
            yld3(2)
            p3front(0)
            f1blk(0)
            p3front(1)
            for tc3 in range(NTC3):
                if tc3 + 3 < NTC3:
                    yld3(tc3 + 3)
                if tc3 + 1 < NTC3:
                    f1blk(tc3 + 1)
                if tc3 + 2 < NTC3:
                    p3front(tc3 + 2)
                f2blk(tc3)

            w.wait("gpsimd", "sOS", w.v("sOS"))
            w.wait("gpsimd", "sSO0", w.v("sSO0"))
            w.wait("gpsimd", "sSO1", w.v("sSO1"))

        if 'P3' in run_p:
            w.block(p3, f"p3_{b}")
        p3sc.close()

    es.close()
    return nc


def host_inputs(inputs, core):
    x = np.ascontiguousarray(np.asarray(inputs["L0_attn_x"],
                                        np.float32)[2 * core:2 * core + 2])
    qkv_b = np.asarray(inputs["L0_attn_qkv_b"], np.float32)
    d = {
        "x": x,
        "qkvwT": np.ascontiguousarray(
            np.asarray(inputs["L0_attn_qkv_w"], np.float32).T),
        "qb_cols": np.ascontiguousarray(qkv_b[:C].reshape(8, 128).T),
        "kvb_row": np.ascontiguousarray(qkv_b[C:].reshape(1, 2 * C)),
        "p1wT": np.ascontiguousarray(np.asarray(inputs["L0_attn_p1_w"], np.float32).T),
        "p1b_cols": np.ascontiguousarray(
            np.asarray(inputs["L0_attn_p1_b"], np.float32).reshape(8, 128).T),
        "p2wT": np.ascontiguousarray(np.asarray(inputs["L0_attn_p2_w"], np.float32).T),
        "p2b_row": np.ascontiguousarray(
            np.asarray(inputs["L0_attn_p2_b"], np.float32).reshape(1, C)),
        "w1T": np.ascontiguousarray(np.asarray(inputs["L0_ffn_W1"], np.float32).T),
        "b1_cols": np.ascontiguousarray(
            np.asarray(inputs["L0_ffn_b1"], np.float32).reshape(32, 128).T),
        "w2T": np.ascontiguousarray(np.asarray(inputs["L0_ffn_W2"], np.float32).T),
        "b2_row": np.ascontiguousarray(
            np.asarray(inputs["L0_ffn_b2"], np.float32).reshape(1, C)),
        "g64": np.ascontiguousarray(np.concatenate([
            np.asarray(inputs["L0_attn_kn_g"], np.float32),
            np.asarray(inputs["L0_attn_vn_g"], np.float32),
            np.asarray(inputs["L0_attn_kn_b"], np.float32),
            np.asarray(inputs["L0_attn_vn_b"], np.float32)]).reshape(1, 256)),
        "tempr": np.ascontiguousarray(
            np.asarray(inputs["L0_attn_temperature"], np.float32).reshape(16, 1)),
    }
    ab = np.asarray(inputs["L0_attn_attn_bias"], np.float32).reshape(8, 2, HD, HD)
    abias = np.concatenate([ab[:, 0], ab[:, 1]], axis=1)
    d["abias"] = np.ascontiguousarray(abias.transpose(1, 0, 2).reshape(128, 512))
    return d


_NC_CACHE = {}


def kernel(**inputs):
    if "prog" not in _NC_CACHE:
        _NC_CACHE["prog"] = build_program(debug=False)
    nc = _NC_CACHE["prog"]
    in_maps = [host_inputs(inputs, core) for core in range(NCORES)]
    res = run_bass_kernel_spmd(nc, in_maps, core_ids=list(range(NCORES)))
    out = np.empty((16, T, C), np.float32)
    for core in range(NCORES):
        out[2 * core:2 * core + 2] = res.results[core]["out"]
    return out


# version 28
# speedup vs baseline: 1.1607x; 1.1607x over previous
import numpy as np
from contextlib import ExitStack

import concourse.bass as bass
import concourse.mybir as mybir
from concourse.bass_utils import run_bass_kernel_spmd

F32 = mybir.dt.float32
BF16 = mybir.dt.bfloat16
I32 = mybir.dt.int32
F8 = mybir.dt.float8e4
AF = mybir.ActivationFunctionType
ALU = mybir.AluOpType
AX = mybir.AxisListType
AP = bass.AP

NCORES = 8
NB = 2
T = 2048
C = 1024
H = 16
HD = 64
FF = 4096
NSUB = 16
NTC = 4
NTC3 = 8
EPS = 1e-5
WN = float(FF * C)


class Walk:

    def __init__(self, nc, semnames):
        self.nc = nc
        self.sems = {}
        self.cnt = {n: 0 for n in semnames}
        self.names = list(semnames)
        self.cur_name = None
        self.cur_eng = None

    def op(self, eng, fn, inc=None, k=1):
        if self.cur_name == eng:
            instr = fn(self.cur_eng)
            if inc is not None:
                instr.then_inc(self.sems[inc], k)
        if inc is not None:
            self.cnt[inc] += k

    def wait(self, eng, sem, val):
        if val <= 0:
            return
        if self.cur_name == eng:
            self.cur_eng.wait_ge(self.sems[sem], val)

    def v(self, sem):
        return self.cnt[sem]

    def vops(self, eng, fns, sem="sV"):
        for fn in fns:
            self.op(eng, fn, inc=sem)
            self.wait(eng, sem, self.v(sem))

    def block(self, fn, name=None):
        nc = self.nc
        base = dict(self.cnt)
        final = {}

        def mk(engname):
            def body(e):
                self.cnt = dict(base)
                self.cur_name = engname
                self.cur_eng = e
                fn(self)
                final.update(self.cnt)
            return body

        with nc.Block(name) as blk:
            blk.sync(mk("sync"))
            blk.scalar(mk("scalar"))
            blk.vector(mk("vector"))
            blk.tensor(mk("tensor"))
            blk.gpsimd(mk("gpsimd"))
        self.cnt = dict(final)


def build_program(debug=False, phases=('S','Q','P1','P2','P3'), p3sub='full'):
    nc = bass.Bass("TRN2", target_bir_lowering=False)

    x_d = nc.dram_tensor("x", [NB, T, C], F32, kind="ExternalInput").ap()
    qkvwT_d = nc.dram_tensor("qkvwT", [C, 3 * C], F32, kind="ExternalInput").ap()
    qbcols_d = nc.dram_tensor("qb_cols", [128, 8], F32, kind="ExternalInput").ap()
    kvbrow_d = nc.dram_tensor("kvb_row", [1, 2 * C], F32, kind="ExternalInput").ap()
    p1wT_d = nc.dram_tensor("p1wT", [C, C], F32, kind="ExternalInput").ap()
    p1bcols_d = nc.dram_tensor("p1b_cols", [128, 8], F32, kind="ExternalInput").ap()
    p2wT_d = nc.dram_tensor("p2wT", [C, C], F32, kind="ExternalInput").ap()
    p2brow_d = nc.dram_tensor("p2b_row", [1, C], F32, kind="ExternalInput").ap()
    w1T_d = nc.dram_tensor("w1T", [C, FF], F32, kind="ExternalInput").ap()
    b1cols_d = nc.dram_tensor("b1_cols", [128, 32], F32, kind="ExternalInput").ap()
    w2T_d = nc.dram_tensor("w2T", [FF, C], F32, kind="ExternalInput").ap()
    b2row_d = nc.dram_tensor("b2_row", [1, C], F32, kind="ExternalInput").ap()
    g64_d = nc.dram_tensor("g64", [1, 256], F32, kind="ExternalInput").ap()
    temp_d = nc.dram_tensor("tempr", [16, 1], F32, kind="ExternalInput").ap()
    abias_d = nc.dram_tensor("abias", [128, 512], F32, kind="ExternalInput").ap()
    out_d = nc.dram_tensor("out", [NB, T, C], F32, kind="ExternalOutput").ap()
    y1s_d = nc.dram_tensor("y1s", [NB, T, C], F32, kind="Internal").ap()
    wq1s_d = nc.dram_tensor("wq1s", [C, FF], F8, kind="Internal").ap()
    wq2s_d = nc.dram_tensor("wq2s", [FF, C], F8, kind="Internal").ap()
    gbscr_d = nc.dram_tensor("gbscr", [2, 2 * C], F32, kind="Internal").ap()
    smscr_d = nc.dram_tensor("smscr", [16, 1], F32, kind="Internal").ap()

    dbg = {}
    if debug:
        for nm, shp, dt in [("d_ln1fm", [128, 8 * T], F32),
                            ("d_q", [128, 8 * T], F32),
                            ("d_khat", [128, 3 * 2048], F32),
                            ("d_kvbd", [128, 1024], F32),
                            ("d_att2", [128, 8 * 512], F32),
                            ("d_gb", [128, 2 * 2048], F32),
                            ("d_invg", [128, 4], F32),
                            ("d_red16", [128, 16], F32),
                            ("d_g11", [1, 16], F32)]:
            dbg[nm] = nc.dram_tensor(nm, shp, dt, kind="ExternalOutput").ap()

    es = ExitStack()

    def sb(name, shape, dt):
        return es.enter_context(nc.sbuf_tensor(name, shape, dt))

    ident = sb("ident", [128, 128], BF16)
    colidx = sb("colidx", [128, 128], I32)
    rowidx = sb("rowidx", [128, 1], F32)
    ones_row = sb("ones_row", [1, 128], BF16)
    onescol_f = sb("onescol_f", [128, 1], F32)
    chalf = sb("chalf", [128, 1], F32)
    cnhalf = sb("cnhalf", [128, 1], F32)
    ceps = sb("ceps", [128, 1], F32)
    ceps2 = sb("ceps2", [128, 1], F32)
    qb_cols = sb("qb_cols_s", [128, 8], F32)
    p1b_cols = sb("p1b_cols_s", [128, 8], F32)
    b1_cols = sb("b1_cols_s", [128, 32], F32)
    kvb_row = sb("kvb_row_s", [1, 2 * C], BF16)
    p2b_row = sb("p2b_row_s", [1, C], BF16)
    B2tile = sb("B2tile", [128, C], F32)
    abias_sb = sb("abias_sb", [128, 512], F32)
    tempQ = sb("tempQ", [128, 8], F32)
    kvbd = sb("kvbd", [128, 8 * 128], BF16)
    invg_col = sb("invg_col", [128, 4], F32)
    r1_sb = sb("r1_sb", [128, NSUB], F32)
    nmr1_sb = sb("nmr1_sb", [128, NSUB], F32)
    sxc = sb("sxc", [128, 4], F32)
    ssqc = sb("ssqc", [128, 4], F32)
    negm = sb("negm", [128, 4], F32)
    msq = sb("msq", [128, 4], F32)
    vvar = sb("vvar", [128, 4], F32)
    srt = sb("srt", [128, 4], F32)
    rrc = sb("rrc", [128, 4], F32)
    nmrc = sb("nmrc", [128, 4], F32)
    khsum = sb("khsum", [128, 32], F32)
    khsq = sb("khsq", [128, 32], F32)
    khneg = sb("khneg", [128, 32], F32)
    khmsq = sb("khmsq", [128, 32], F32)
    khv = sb("khv", [128, 32], F32)
    khsrt = sb("khsrt", [128, 32], F32)
    khr = sb("khr", [128, 32], F32)
    junk2 = sb("junk2", [128, 1024], F32)

    qsc = ExitStack()
    g64 = qsc.enter_context(nc.sbuf_tensor("g64_s", [1, 256], F32))
    g_row = qsc.enter_context(nc.sbuf_tensor("g_row", [1, 2 * C], F32))
    b_row = qsc.enter_context(nc.sbuf_tensor("b_row", [1, 2 * C], F32))
    t16 = qsc.enter_context(nc.sbuf_tensor("t16", [16, 1], F32))
    red16 = qsc.enter_context(nc.sbuf_tensor("red16", [128, 16], F32))
    g11 = qsc.enter_context(nc.sbuf_tensor("g11", [1, 16], F32))

    semnames = ["sDX", "sDW", "sDO", "sT", "sV", "sS", "sP",
                "sXL", "sGB", "sTQ", "sWR0", "sWR1", "sIG", "sSO0", "sSO1",
                "sY0", "sY1", "sOS", "sWQ", "sGM", "sXL1", "sXL2", "sXL3",
                "sTR0", "sTR1"]
    w = Walk(nc, semnames)
    for n in semnames:
        w.sems[n] = es.enter_context(nc.semaphore(n))

    def startup(w):
        for dst, src in [(qb_cols[:], qbcols_d), (p1b_cols[:], p1bcols_d),
                         (b1_cols[:], b1cols_d), (abias_sb[:], abias_d),
                         (t16[:], temp_d), (g64[:], g64_d),
                         (B2tile[:], AP(b2row_d.tensor, 0, [[0, 128], [1, C]]))]:
            w.op("sync", lambda e, dst=dst, src=src: e.dma_start(out=dst, in_=src),
                 inc="sDX", k=16)
        w.op("gpsimd", lambda e: e.dma_start(out=kvb_row[:], in_=kvbrow_d),
             inc="sDW", k=16)
        w.op("gpsimd", lambda e: e.dma_start(out=p2b_row[:], in_=p2brow_d),
             inc="sDW", k=16)
        w.op("gpsimd", lambda e: e.iota(colidx[:], [[1, 128]], base=0,
                                        channel_multiplier=0), inc="sP")
        w.op("gpsimd", lambda e: e.iota(rowidx[:], [[1, 1]], base=0,
                                        channel_multiplier=1,
                                        allow_small_or_imprecise_dtypes=True),
             inc="sP")

        def msets(e):
            e.memset(ones_row[:], 1.0)
            e.memset(onescol_f[:], 1.0)
            e.memset(chalf[:], 0.5)
            e.memset(cnhalf[:], -0.5)
            e.memset(ceps[:], EPS)
            e.memset(ceps2[:], EPS * HD * HD)
            return e.memset(kvbd[:], 0.0)
        w.op("vector", msets, inc="sV")

        w.wait("vector", "sP", w.v("sP"))
        w.op("vector", lambda e: e.tensor_scalar(
            out=ident[:], in0=colidx[:], scalar1=rowidx[:], scalar2=None,
            op0=ALU.is_equal), inc="sV")

        w.wait("vector", "sDX", w.v("sDX"))

        def tile_gb(e):
            for half in range(2):
                e.tensor_copy(
                    out=g_row[0:1, half * C:(half + 1) * C].rearrange(
                        "p (a d) -> p a d", d=HD),
                    in_=g64[0:1, half * HD:(half + 1) * HD].unsqueeze(1)
                    .broadcast_to([1, H, HD]))
            for half in range(2):
                e.tensor_copy(
                    out=b_row[0:1, half * C:(half + 1) * C].rearrange(
                        "p (a d) -> p a d", d=HD),
                    in_=g64[0:1, 128 + half * HD: 128 + (half + 1) * HD]
                    .unsqueeze(1).broadcast_to([1, H, HD]))
            return e.tensor_scalar(out=t16[:], in0=t16[:], scalar1=1.0 / T,
                                   scalar2=None, op0=ALU.mult)
        w.op("vector", tile_gb, inc="sV")
        w.wait("gpsimd", "sV", w.v("sV"))
        w.op("gpsimd", lambda e: e.dma_start(out=gbscr_d[0:1, :], in_=g_row[:]),
             inc="sDO", k=16)
        w.op("gpsimd", lambda e: e.dma_start(out=gbscr_d[1:2, :], in_=b_row[:]),
             inc="sDO", k=16)
        w.op("gpsimd", lambda e: e.dma_start(out=smscr_d, in_=t16[:]),
             inc="sDO", k=16)
        w.wait("sync", "sDO", w.v("sDO"))
        with nc.allow_non_contiguous_dma(reason="tiny one-time tempQ broadcast"):
            w.op("sync", lambda e: e.dma_start(
                out=tempQ[0:64, :], in_=AP(smscr_d.tensor, 0, [[0, 64], [2, 8]])),
                inc="sTQ", k=16)
            w.op("sync", lambda e: e.dma_start(
                out=tempQ[64:128, :], in_=AP(smscr_d.tensor, 1, [[0, 64], [2, 8]])),
                inc="sTQ", k=16)
        w.wait("sync", "sTQ", w.v("sTQ"))
        w.wait("gpsimd", "sDW", w.v("sDW"))

    w.block(startup, "startup")
    run_p = phases

    qsc.close()

    def wchunk_src(gi):
        if gi < 32:
            return w1T_d[(gi // 4) * 128:(gi // 4 + 1) * 128,
                         (gi % 4) * 1024:(gi % 4 + 1) * 1024]
        r = gi - 32
        return w2T_d[r * 128:(r + 1) * 128, :]

    def wchunk_dst(gi):
        if gi < 32:
            return wq1s_d[(gi // 4) * 128:(gi // 4 + 1) * 128,
                          (gi % 4) * 1024:(gi % 4 + 1) * 1024]
        r = gi - 32
        return wq2s_d[r * 128:(r + 1) * 128, :]

    for b in (range(NB) if 'P1' in run_p else []):
        absc = ExitStack()
        ln1_fm = absc.enter_context(nc.sbuf_tensor(f"ln1fm{b}", [128, 8 * T], BF16))
        q_fm = absc.enter_context(nc.sbuf_tensor(f"qfm{b}", [128, 8 * T], BF16))
        p1sc = ExitStack()
        G64 = p1sc.enter_context(nc.sbuf_tensor(f"G64_{b}", [128, 2 * HD], F32))
        B64 = p1sc.enter_context(nc.sbuf_tensor(f"B64_{b}", [128, 2 * HD], F32))
        qkvwT_sb = p1sc.enter_context(
            nc.sbuf_tensor(f"qkvwT{b}", [128, 8 * 3072], BF16))
        xring = p1sc.enter_context(nc.sbuf_tensor(f"xring{b}", [128, 4096], F32))
        ln1tm = p1sc.enter_context(nc.sbuf_tensor(f"ln1tm{b}", [128, 2 * 1024], BF16))
        uu = p1sc.enter_context(nc.sbuf_tensor(f"uu{b}", [128, 2048], F32))
        uu2 = p1sc.enter_context(nc.sbuf_tensor(f"uu2{b}", [128, 2048], F32))
        khat = p1sc.enter_context(nc.sbuf_tensor(f"khat{b}", [128, 3 * 2048], BF16))
        psKV = p1sc.enter_context(nc.psum_tensor(f"psKV{b}", [128, 2048], F32))
        psQ = p1sc.enter_context(nc.psum_tensor(f"psQ{b}", [128, 1024], F32))
        psACC = p1sc.enter_context(nc.psum_tensor(f"psACC{b}", [128, 1024], F32))
        gw_ring = p1sc.enter_context(nc.sbuf_tensor(f"gwr{b}", [128, 2 * 1024], F32))
        gw_junk = p1sc.enter_context(nc.sbuf_tensor(f"gwj{b}", [128, 1024], F32))
        junk = p1sc.enter_context(nc.sbuf_tensor(f"junk{b}", [128, 2048], F32))
        red64 = p1sc.enter_context(nc.sbuf_tensor(f"red64{b}", [128, 64], F32))
        g11 = p1sc.enter_context(nc.sbuf_tensor(f"g11{b}", [1, 16], F32))

        def p1(w, b=b, ln1_fm=ln1_fm, q_fm=q_fm, G64=G64, B64=B64,
               qkvwT_sb=qkvwT_sb, xring=xring, ln1tm=ln1tm, uu=uu, uu2=uu2,
               khat=khat, psKV=psKV, psQ=psQ, psACC=psACC,
               gw_ring=gw_ring, gw_junk=gw_junk, red64=red64, g11=g11,
               junk=junk):
            gred_done = {}

            def gamma_chunk(gi):
                if b != 0:
                    return
                wr = "sWR0" if gi % 2 == 0 else "sWR1"
                if gi >= 2:
                    w.wait("sync", "sS", gred_done[gi - 2])
                w.op("sync", lambda e, gi=gi: e.dma_start(
                    out=gw_ring[:, (gi % 2) * 1024:(gi % 2 + 1) * 1024],
                    in_=wchunk_src(gi)), inc=wr, k=16)
                w.wait("scalar", wr, w.v(wr))
                w.wait("scalar", "sS", w.v("sS"))
                w.op("scalar", lambda e, gi=gi: e.activation(
                    out=gw_junk[:], in_=gw_ring[:, (gi % 2) * 1024:(gi % 2 + 1) * 1024],
                    func=AF.Abs, accum_out=red64[:, gi:gi + 1]), inc="sS")
                gred_done[gi] = w.v("sS")
            for a in range(8):
                w.op("gpsimd", lambda e, a=a: e.dma_start(
                    out=qkvwT_sb[:, a * 3072:(a + 1) * 3072],
                    in_=qkvwT_d[a * 128:(a + 1) * 128, :],
                    max_dma_last_dim=8192), inc="sDW", k=16)
            w.op("sync", lambda e: e.dma_start(
                out=G64[:, 0:HD], in_=AP(gbscr_d.tensor, 0, [[0, 128], [1, HD]])),
                inc="sGB", k=16)
            w.op("sync", lambda e: e.dma_start(
                out=G64[:, HD:2 * HD],
                in_=AP(gbscr_d.tensor, C, [[0, 128], [1, HD]])), inc="sGB", k=16)
            w.op("sync", lambda e: e.dma_start(
                out=B64[:, 0:HD],
                in_=AP(gbscr_d.tensor, 2 * C, [[0, 128], [1, HD]])), inc="sGB", k=16)
            w.op("sync", lambda e: e.dma_start(
                out=B64[:, HD:2 * HD],
                in_=AP(gbscr_d.tensor, 3 * C, [[0, 128], [1, HD]])), inc="sGB", k=16)
            wload = w.v("sDW")
            gbload = w.v("sGB")

            evac_done = {}
            pass1g = {}
            hp_done = {}
            sq_done_g = {}
            khat_done = {}
            kvacc_done = {}
            tr_done = {}
            qevac_done = {}
            applyln_done = {}
            xcons_done = {}
            chain1_done = {}
            ksq_red = {}
            stats_done_map = {}

            def p1ldstats(tc):
                if tc >= 1:
                    w.wait("sync", "sS", applyln_done[tc - 1])
                    w.wait("sync", "sV", xcons_done[tc - 1])
                w.op("sync", lambda e, tc=tc: e.dma_start(
                    out=xring[:].rearrange("p (s d) -> p s d", d=C),
                    in_=x_d[b, tc * 512:(tc + 1) * 512, :].rearrange(
                        "(s p) d -> p s d", p=128)), inc="sXL", k=16)
                xload = w.v("sXL")

                w.wait("scalar", "sXL", xload)
                if tc >= 1:
                    w.wait("scalar", "sV", chain1_done[tc - 1])
                w.wait("vector", "sXL", xload)
                for s4 in range(4):
                    xs = xring[:, s4 * 1024:(s4 + 1) * 1024]
                    w.wait("scalar", "sS", w.v("sS"))
                    w.op("scalar", lambda e, xs=xs, s4=s4: e.activation(
                        out=junk2[:], in_=xs, func=AF.Square,
                        accum_out=ssqc[:, s4:s4 + 1]), inc="sS")
                    w.op("vector", lambda e, xs=xs, s4=s4: e.tensor_reduce(
                        out=sxc[:, s4:s4 + 1], in_=xs, axis=AX.X, op=ALU.add),
                        inc="sV" if s4 == 3 else None)
                sq_done = w.v("sS")
                xred_done = w.v("sV")
                w.wait("vector", "sS", sq_done)
                w.wait("vector", "sV", xred_done)

                w.vops("vector", [
                    lambda e: e.tensor_scalar(out=negm[:], in0=sxc[:],
                                              scalar1=-1.0 / C, scalar2=None,
                                              op0=ALU.mult),
                    lambda e: e.tensor_tensor(out=msq[:], in0=negm[:],
                                              in1=negm[:], op=ALU.mult),
                    lambda e: e.scalar_tensor_tensor(
                        out=vvar[:], in0=ssqc[:], scalar=1.0 / C, in1=msq[:],
                        op0=ALU.mult, op1=ALU.subtract)])
                chain1_done[tc] = w.v("sV")
                w.wait("scalar", "sV", w.v("sV"))
                w.op("scalar", lambda e: e.activation(
                    out=srt[:], in_=vvar[:], func=AF.Sqrt, bias=ceps[:]), inc="sS")
                w.wait("vector", "sS", w.v("sS"))

                w.vops("vector", [
                    lambda e: e.reciprocal(rrc[:], srt[:]),
                    lambda e: e.tensor_tensor(out=nmrc[:], in0=negm[:],
                                              in1=rrc[:], op=ALU.mult),
                    lambda e, tc=tc: e.tensor_copy(
                        out=r1_sb[:, tc * 4:(tc + 1) * 4], in_=rrc[:]),
                    lambda e, tc=tc: e.tensor_copy(
                        out=nmr1_sb[:, tc * 4:(tc + 1) * 4], in_=nmrc[:])])
                stats_done_map[tc] = w.v("sV")
                xcons_done[tc] = max(stats_done_map[tc], xred_done)

            def p1fronts(tc):
                for s4 in range(4):
                    s = tc * 4 + s4
                    lslot = s % 2
                    xs = xring[:, s4 * 1024:(s4 + 1) * 1024]
                    w.wait("scalar", "sV", stats_done_map[tc])
                    if s >= 2:
                        w.wait("scalar", *tr_done[s - 2])
                    w.op("scalar", lambda e, xs=xs, s4=s4, lslot=lslot: e.activation(
                        out=ln1tm[:, lslot * 1024:(lslot + 1) * 1024], in_=xs,
                        func=AF.Identity, scale=rrc[:, s4:s4 + 1],
                        bias=nmrc[:, s4:s4 + 1]), inc="sS")
                    apl = w.v("sS")
                    if s4 == 3:
                        applyln_done[tc] = apl
                    trs = "sTR0" if lslot == 0 else "sTR1"
                    w.wait("scalar", "sS", apl)
                    w.op("scalar", lambda e, s=s, lslot=lslot: e.dma_start(
                        out=AP(ln1_fm, s * 128, [[8 * T, 128], [T, 8], [1, 128]]),
                        in_=ln1tm[:, lslot * 1024:(lslot + 1) * 1024],
                        transpose=True), inc=trs, k=16)
                    tr_done[s] = (trs, w.v(trs))

            def p1body(tc):
                for s4 in range(4):
                    s = tc * 4 + s4
                    lslot = s % 2
                    if s >= 3:
                        w.wait("tensor", "sP", khat_done[s - 3])

                        def kvacc(e, sp=s - 3):
                            ii = None
                            lsp = sp % 3
                            for p in range(8):
                                ii = e.matmul(
                                    psACC[:, p * 128:(p + 1) * 128],
                                    khat[:, lsp * 2048 + p * 128:
                                         lsp * 2048 + (p + 1) * 128],
                                    khat[:, lsp * 2048 + 1024 + p * 128:
                                         lsp * 2048 + 1024 + (p + 1) * 128],
                                    start=(sp == 0 and p in (0, 4)), stop=False,
                                    skip_group_check=True)
                            return ii
                        w.op("tensor", kvacc, inc="sT")
                        kvacc_done[s - 3] = w.v("sT")

                    for g in range(4):
                        bank = g
                        jcol = 1024 + g * 512
                        half = g // 2
                        if g == 0:
                            w.wait("tensor", *tr_done[s])
                            if s == 0:
                                w.wait("tensor", "sDW", wload)
                        if s >= 1:
                            w.wait("tensor", "sV", pass1g[(s - 1, g)])
                        if s >= 1:
                            w.wait("vector", "sP", hp_done.get((s - 1, g // 2), 0))

                        def kvproj(e, s=s, jcol=jcol, bank=bank):
                            reg = psKV[:, bank * 512:(bank + 1) * 512]
                            e.matmul(reg, ones_row[:],
                                     kvb_row[0:1, jcol - 1024:jcol - 512],
                                     start=True, stop=False)
                            ii = None
                            for ci in range(8):
                                ii = e.matmul(
                                    reg,
                                    ln1_fm[:, ci * T + s * 128:
                                           ci * T + (s + 1) * 128],
                                    qkvwT_sb[:, ci * 3072 + jcol:
                                             ci * 3072 + jcol + 512],
                                    start=False, stop=(ci == 7))
                            return ii
                        w.op("tensor", kvproj, inc="sT")
                        projg = w.v("sT")

                        w.wait("vector", "sT", projg)
                        w.op("vector", lambda e, bank=bank, g=g: e.tensor_reduce(
                            out=khneg[:, g * 8:(g + 1) * 8],
                            in_=psKV[:, bank * 512:(bank + 1) * 512].rearrange(
                                "p (h d) -> p h d", d=HD),
                            axis=AX.X, op=ALU.add, negate=True), inc="sV")
                        w.wait("vector", "sV", w.v("sV"))
                        if s == 0 and g == 0:
                            w.wait("vector", "sGB", gbload)
                        w.op("vector", lambda e, bank=bank, g=g:
                             e.scalar_tensor_tensor(
                                 out=uu[:, g * 512:(g + 1) * 512].rearrange(
                                     "p (h d) -> p h d", d=HD),
                                 in0=psKV[:, bank * 512:(bank + 1) * 512].rearrange(
                                     "p (h d) -> p h d", d=HD),
                                 scalar=float(HD),
                                 in1=khneg[:, g * 8:(g + 1) * 8].unsqueeze(2)
                                 .broadcast_to([128, 8, HD]),
                                 op0=ALU.mult, op1=ALU.add), inc="sV")
                        pass1g[(s, g)] = w.v("sV")
                        w.wait("scalar", "sV", pass1g[(s, g)])
                        w.wait("scalar", "sS", w.v("sS"))
                        if (s - 1, g) in ksq_red:
                            w.wait("scalar", "sV", ksq_red[(s - 1, g)])
                        w.op("scalar", lambda e, g=g: e.activation(
                            out=junk[:, g * 512:(g + 1) * 512],
                            in_=uu[:, g * 512:(g + 1) * 512],
                            func=AF.Square), inc="sS")
                        sq_done_g[(s, g)] = w.v("sS")

                    for g in range(4):
                        bank = g % 2
                        w.wait("vector", "sS", sq_done_g[(s, g)])
                        w.op("vector", lambda e, g=g: e.tensor_reduce(
                            out=khsq[:, g * 8:(g + 1) * 8],
                            in_=junk[:, g * 512:(g + 1) * 512].rearrange(
                                "p (h d) -> p h d", d=HD),
                            axis=AX.X, op=ALU.add), inc="sV")
                        ksq_red[(s, g)] = w.v("sV")

                    for half in range(2):
                        hc = slice(half * 16, (half + 1) * 16)
                        w.wait("vector", "sV", ksq_red[(s, half * 2 + 1)])
                        w.vops("vector", [
                            lambda e, hc=hc: e.tensor_scalar(
                                out=khv[:, hc], in0=khsq[:, hc],
                                scalar1=1.0 / HD, scalar2=None,
                                op0=ALU.mult)])
                        w.wait("scalar", "sV", w.v("sV"))
                        w.op("scalar", lambda e, hc=hc: e.activation(
                            out=khsrt[:, hc], in_=khv[:, hc], func=AF.Sqrt,
                            bias=ceps2[:]), inc="sS")
                        w.wait("vector", "sS", w.v("sS"))
                        w.vops("vector", [
                            lambda e, hc=hc: e.reciprocal(khr[:, hc],
                                                          khsrt[:, hc])])
                        hr = slice(half * 1024, (half + 1) * 1024)
                        w.op("vector", lambda e, hc=hc, hr=hr: e.tensor_tensor(
                            out=uu2[:, hr].rearrange("p (h d) -> p h d", d=HD),
                            in0=uu[:, hr].rearrange("p (h d) -> p h d", d=HD),
                            in1=khr[:, hc].unsqueeze(2)
                            .broadcast_to([128, 16, HD]),
                            op=ALU.mult), inc="sV")
                        w.wait("gpsimd", "sV", w.v("sV"))
                        w.op("gpsimd", lambda e, half=half, hr=hr: e.tensor_tensor(
                            out=uu[:, hr].rearrange("p (h d) -> p h d", d=HD),
                            in0=uu2[:, hr].rearrange("p (h d) -> p h d", d=HD),
                            in1=G64[:, half * HD:(half + 1) * HD].unsqueeze(1)
                            .broadcast_to([128, 16, HD]),
                            op=ALU.mult), inc="sP")
                        w.wait("gpsimd", "sP", w.v("sP"))
                        if s >= 3 and half == 0:
                            w.wait("gpsimd", "sT", kvacc_done[s - 3])
                        w.op("gpsimd", lambda e, half=half, lslot=s % 3, hr=hr:
                             e.tensor_tensor(
                                 out=khat[:, lslot * 2048 + half * 1024:
                                          lslot * 2048 + (half + 1) * 1024]
                                 .rearrange("p (h d) -> p h d", d=HD),
                                 in0=uu[:, hr].rearrange("p (h d) -> p h d", d=HD),
                                 in1=B64[:, half * HD:(half + 1) * HD].unsqueeze(1)
                                 .broadcast_to([128, 16, HD]),
                                 op=ALU.add), inc="sP")
                        w.wait("gpsimd", "sP", w.v("sP"))
                        khat_done[s] = w.v("sP")
                        hp_done[(s, half)] = w.v("sP")

                    for u in range(4):
                        gamma_chunk(s * 4 + u)
                    if tc >= 1:
                        p1q_one(tc - 1, s4 * 2)
                        p1q_one(tc - 1, s4 * 2 + 1)

            def p1q_one(tc, jc):
                qslot = jc % 2
                if jc == 0:
                    w.wait("tensor", *tr_done[tc * 4 + 2])
                    w.wait("tensor", *tr_done[tc * 4 + 3])
                if jc >= 2:
                    w.wait("tensor", "sV", qevac_done[jc - 2])
                elif tc >= 1:
                    w.wait("tensor", "sV", qevac_done[6 + qslot])

                def qproj(e, jc=jc, qslot=qslot, tc=tc):
                    ii = None
                    for ci in range(8):
                        ii = e.matmul(
                            psQ[:, qslot * 512:(qslot + 1) * 512],
                            qkvwT_sb[:, ci * 3072 + jc * 128:
                                     ci * 3072 + (jc + 1) * 128],
                            ln1_fm[:, ci * T + tc * 512:ci * T + (tc + 1) * 512],
                            start=(ci == 0), stop=(ci == 7))
                    return ii
                w.op("tensor", qproj, inc="sT")
                w.wait("vector", "sT", w.v("sT"))
                w.op("vector", lambda e, jc=jc, qslot=qslot, tc=tc: e.tensor_scalar(
                    out=q_fm[:, jc * T + tc * 512:jc * T + (tc + 1) * 512],
                    in0=psQ[:, qslot * 512:(qslot + 1) * 512],
                    scalar1=qb_cols[:, jc:jc + 1], scalar2=None,
                    op0=ALU.add), inc="sV")
                qevac_done[jc] = w.v("sV")

            p1ldstats(0)
            for tc in range(NTC):
                p1fronts(tc)
                if tc + 1 < NTC:
                    p1ldstats(tc + 1)
                p1body(tc)
            for jc in range(8):
                p1q_one(NTC - 1, jc)

            for sp in (NSUB - 3, NSUB - 2, NSUB - 1):
                w.wait("tensor", "sP", khat_done[sp])

                def kvacc_last(e, sp=sp):
                    ii = None
                    lsp = sp % 3
                    for p in range(8):
                        ii = e.matmul(
                            psACC[:, p * 128:(p + 1) * 128],
                            khat[:, lsp * 2048 + p * 128:
                                 lsp * 2048 + (p + 1) * 128],
                            khat[:, lsp * 2048 + 1024 + p * 128:
                                 lsp * 2048 + 1024 + (p + 1) * 128],
                            start=False, stop=(sp == NSUB - 1),
                            skip_group_check=True)
                    return ii
                w.op("tensor", kvacc_last, inc="sT")
                kvacc_done[sp] = w.v("sT")

            if b == 0:
                w.wait("tensor", "sS", gred_done[63])
                w.wait("tensor", "sV", qevac_done[7])

                def gtotmm(e):
                    e.matmul(psQ[0:1, 0:32], onescol_f[:],
                             red64[:, 0:32], start=True, stop=True)
                    return e.matmul(psQ[0:1, 32:64], onescol_f[:],
                                    red64[:, 32:64], start=True, stop=True,
                                    skip_group_check=True)
                w.op("tensor", gtotmm, inc="sT")
                w.wait("vector", "sT", w.v("sT"))
                w.vops("vector", [
                    lambda e: e.tensor_reduce(out=g11[0:1, 2:3],
                                              in_=psQ[0:1, 0:32], axis=AX.X,
                                              op=ALU.add),
                    lambda e: e.tensor_reduce(out=g11[0:1, 3:4],
                                              in_=psQ[0:1, 32:64], axis=AX.X,
                                              op=ALU.add),
                    lambda e: e.tensor_scalar(
                        out=g11[0:1, 4:6], in0=g11[0:1, 2:4], scalar1=1.0 / WN,
                        scalar2=EPS, op0=ALU.mult, op1=ALU.add),
                    lambda e: e.reciprocal(g11[0:1, 6:8], g11[0:1, 4:6]),
                    lambda e: e.tensor_scalar(
                        out=g11[0:1, 8:10], in0=g11[0:1, 6:8], scalar1=-1.0,
                        scalar2=None, op0=ALU.mult)])
                w.wait("gpsimd", "sV", w.v("sV"))
                w.op("gpsimd", lambda e: e.dma_start(out=smscr_d[0:4, 0:1],
                                                     in_=g11[0:1, 6:10]),
                     inc="sGM", k=16)
                w.wait("sync", "sGM", w.v("sGM"))
                w.op("sync", lambda e: e.dma_start(
                    out=invg_col[:, 0:4],
                    in_=AP(smscr_d.tensor, 0, [[0, 128], [1, 4]])), inc="sIG", k=16)
                w.wait("sync", "sIG", w.v("sIG"))

            w.wait("vector", "sT", kvacc_done[NSUB - 1])

            def kvfin(e):
                ii = None
                for h in range(H):
                    p, r = h // 2, h % 2
                    base = r * 64
                    ii = e.scalar_tensor_tensor(
                        out=kvbd[base:base + 64,
                                 p * 128 + r * 64: p * 128 + r * 64 + 64],
                        in0=psACC[base:base + 64,
                                  p * 128 + r * 64: p * 128 + r * 64 + 64],
                        scalar=tempQ[base:base + 64, p:p + 1],
                        in1=abias_sb[base:base + 64, p * 64:(p + 1) * 64],
                        op0=ALU.mult, op1=ALU.add)
                return ii
            w.op("vector", kvfin, inc="sV")

            if debug:
                w.wait("gpsimd", "sV", w.v("sV"))
                w.wait("gpsimd", "sS", w.v("sS"))
                w.wait("gpsimd", "sT", w.v("sT"))
                for nm, src in [("d_kvbd", kvbd[:]), ("d_ln1fm", ln1_fm[:]),
                                ("d_q", q_fm[:]), ("d_khat", khat[:])]:
                    w.op("gpsimd", lambda e, nm=nm, src=src: e.dma_start(
                        out=dbg[nm], in_=src), inc="sDO", k=16)
                w.wait("gpsimd", "sDO", w.v("sDO"))

        if 'P1' in run_p:
            w.block(p1, f"p1_{b}")
        p1sc.close()

        p2sc = ExitStack()
        p1T_sb = p2sc.enter_context(nc.sbuf_tensor(f"p1T{b}", [128, 8 * C], BF16))
        p2T_sb = p2sc.enter_context(nc.sbuf_tensor(f"p2T{b}", [128, 8 * C], BF16))
        xring2 = p2sc.enter_context(nc.sbuf_tensor(f"xr2{b}", [128, 2 * 4096], F32))
        att2 = p2sc.enter_context(nc.sbuf_tensor(f"att2{b}", [128, 8 * 512], BF16))
        hring = p2sc.enter_context(nc.sbuf_tensor(f"hring{b}", [128, 8 * 512], BF16))
        aoring = p2sc.enter_context(nc.sbuf_tensor(f"aor{b}", [128, 4 * 1024], BF16))
        zring = p2sc.enter_context(nc.sbuf_tensor(f"zr{b}", [128, 1024], F32))
        ln1tm2 = p2sc.enter_context(nc.sbuf_tensor(f"l1b{b}", [128, 1024], BF16))
        y1ring = p2sc.enter_context(nc.sbuf_tensor(f"y1r{b}", [128, 2 * 1024], F32))
        psATT = p2sc.enter_context(nc.psum_tensor(f"psATT{b}", [128, 1024], F32))
        psP1 = p2sc.enter_context(nc.psum_tensor(f"psP1{b}", [128, 1024], F32))
        psP2 = p2sc.enter_context(nc.psum_tensor(f"psP2{b}", [128, 2048], F32))
        qw_ring = p2sc.enter_context(nc.sbuf_tensor(f"qwr{b}", [128, 2 * 1024], F32))
        qbuf = p2sc.enter_context(nc.sbuf_tensor(f"qbuf{b}", [128, 4 * 1024], F8))
        qbufB = p2sc.enter_context(nc.sbuf_tensor(f"qbufB{b}", [128, 2 * 1024], F8))
        qsA = p2sc.enter_context(nc.sbuf_tensor(f"qsA{b}", [128, 1024], F32))

        def p2(w, b=b, ln1_fm=ln1_fm, q_fm=q_fm, p1T_sb=p1T_sb, p2T_sb=p2T_sb,
               xring2=xring2, att2=att2, hring=hring, aoring=aoring, zring=zring,
               ln1tm2=ln1tm2, y1ring=y1ring, psATT=psATT, psP1=psP1, psP2=psP2,
               qw_ring=qw_ring, qbuf=qbuf, qbufB=qbufB, qsA=qsA):
            qtt_done = {}
            qst_done = {}
            qm_done = {}
            qld = {}
            qctr = [0]

            def quant_load(gi):
                if b != 0 or gi >= 64:
                    return
                wr = "sWR0" if gi % 2 == 0 else "sWR1"
                slot = gi % 2
                if gi >= 2:
                    w.wait("sync", *qtt_done[gi - 2])
                w.op("sync", lambda e, gi=gi, slot=slot: e.dma_start(
                    out=qw_ring[:, slot * 1024:(slot + 1) * 1024],
                    in_=wchunk_src(gi)), inc=wr, k=16)
                qld[gi] = (wr, w.v(wr))

            def quant_chunk(_ignored=None):
                if b != 0:
                    return
                gi = qctr[0]
                if gi >= 64:
                    return
                qctr[0] += 1
                quant_comp(gi)
                quant_load(gi + 2)

            def quant_comp(gi):
                so = "sSO0" if gi % 2 == 0 else "sSO1"
                gc = 0 if gi < 32 else 1
                slot = gi % 2
                if gi % 2 == 0:
                    w.wait("vector", *qld[gi])
                    w.wait("vector", "sV", w.v("sV"))
                    if gi >= 2:
                        w.wait("vector", *qtt_done[gi - 2])
                    w.op("vector", lambda e, slot=slot, gc=gc: e.tensor_scalar(
                        out=qbuf[:, 0:1024],
                        in0=qw_ring[:, slot * 1024:(slot + 1) * 1024],
                        scalar1=invg_col[:, gc:gc + 1], scalar2=0.5,
                        op0=ALU.mult, op1=ALU.is_ge), inc="sV")
                    w.wait("vector", "sV", w.v("sV"))
                    w.op("vector", lambda e, slot=slot, gc=gc: e.tensor_scalar(
                        out=qbuf[:, 1024:2048],
                        in0=qw_ring[:, slot * 1024:(slot + 1) * 1024],
                        scalar1=invg_col[:, 2 + gc:3 + gc], scalar2=0.5,
                        op0=ALU.mult, op1=ALU.is_ge), inc="sV")
                    w.wait("vector", "sV", w.v("sV"))
                    qts_done = w.v("sV")
                    w.wait("gpsimd", "sV", qts_done)
                    op = ALU.subtract
                    src0, src1 = qbuf[:, 0:1024], qbuf[:, 1024:2048]
                else:
                    w.wait("scalar", *qld[gi])
                    w.wait("scalar", "sS", w.v("sS"))
                    if gi >= 2:
                        w.wait("scalar", "sV", qm_done.get(gi - 2, 0))
                    w.op("scalar", lambda e, slot=slot, gc=gc: e.activation(
                        out=qsA[:],
                        in_=qw_ring[:, slot * 1024:(slot + 1) * 1024],
                        func=AF.Abs, scale=invg_col[:, gc:gc + 1]), inc="sS")
                    qa_done = w.v("sS")
                    if gi >= 2:
                        w.wait("scalar", *qtt_done[gi - 2])
                    w.op("scalar", lambda e, slot=slot: e.activation(
                        out=qbufB[:, 1024:2048],
                        in_=qw_ring[:, slot * 1024:(slot + 1) * 1024],
                        func=AF.Sign), inc="sS")
                    qsg_done = w.v("sS")
                    w.wait("vector", "sS", qa_done)
                    if gi >= 2:
                        w.wait("vector", *qtt_done[gi - 2])
                    w.op("vector", lambda e: e.tensor_scalar(
                        out=qbufB[:, 0:1024], in0=qsA[:],
                        scalar1=0.5, scalar2=None, op0=ALU.is_ge), inc="sV")
                    qm_done[gi] = w.v("sV")
                    w.wait("gpsimd", "sV", qm_done[gi])
                    w.wait("gpsimd", "sS", qsg_done)
                    op = ALU.mult
                    src0, src1 = qbufB[:, 0:1024], qbufB[:, 1024:2048]
                if gi >= 2:
                    w.wait("gpsimd", so, qst_done[gi - 2])
                w.op("gpsimd", lambda e, slot=slot, op=op, src0=src0, src1=src1:
                     e.tensor_tensor(
                         out=qbuf[:, (2 + slot) * 1024:(3 + slot) * 1024],
                         in0=src0, in1=src1, op=op), inc="sP")
                qtt_done[gi] = ("sP", w.v("sP"))
                w.wait("gpsimd", "sP", w.v("sP"))
                w.op("gpsimd", lambda e, gi=gi, slot=slot: e.dma_start(
                    out=wchunk_dst(gi),
                    in_=qbuf[:, (2 + slot) * 1024:(3 + slot) * 1024]), inc=so, k=16)
                qst_done[gi] = w.v(so)
            w.op("gpsimd", lambda e: e.dma_start(
                out=p1T_sb[:].rearrange("p (a j) -> p a j", j=C),
                in_=p1wT_d.rearrange("(a p) j -> p a j", p=128)), inc="sDW", k=16)
            w.op("gpsimd", lambda e: e.dma_start(
                out=p2T_sb[:].rearrange("p (a j) -> p a j", j=C),
                in_=p2wT_d.rearrange("(a p) j -> p a j", p=128)), inc="sDW", k=16)
            wload = w.v("sDW")
            quant_load(0)
            quant_load(1)

            att2_done = {}
            gelu_done = {}
            p2mm_done = {}
            p1_last = {}
            ao_done = {}
            zs_done = {}
            y1tt_done = {}
            y1st_done = {}
            lnb_done = {}
            chain1_done = {}
            xload_v = {}

            def xld(tc):
                x2slot = tc % 2
                if tc >= 2:
                    w.wait("sync", "sS", lnb_done[tc - 2])
                    w.wait("sync", "sV", y1tt_done[(tc - 1) * 4 - 1])
                xsem = "sXL" if x2slot == 0 else "sXL1"
                w.op("sync", lambda e, tc=tc, x2slot=x2slot: e.dma_start(
                    out=xring2[:, x2slot * 4096:(x2slot + 1) * 4096].rearrange(
                        "p (s d) -> p s d", d=C),
                    in_=x_d[b, tc * 512:(tc + 1) * 512, :].rearrange(
                        "(s p) d -> p s d", p=128)), inc=xsem, k=16)
                xload_v[tc] = (xsem, w.v(xsem))

            def attf(tc):
                for p in range(8):
                    if p >= 2:
                        w.wait("tensor", "sV", att2_done[p - 2])
                    elif tc >= 1:
                        w.wait("tensor", "sV", att2_done[6 + p])
                    w.op("tensor", lambda e, p=p, tc=tc: e.matmul(
                        psATT[:, (p % 2) * 512:(p % 2 + 1) * 512],
                        kvbd[:, p * 128:(p + 1) * 128],
                        q_fm[:, p * T + tc * 512:p * T + (tc + 1) * 512],
                        start=True, stop=True), inc="sT")
                    w.wait("vector", "sT", w.v("sT"))
                    if tc >= 1 and p == 0:
                        w.wait("vector", "sT", p1_last[tc - 1])
                    w.op("vector", lambda e, p=p, tc=tc: e.tensor_tensor(
                        out=att2[:, p * 512:(p + 1) * 512],
                        in0=psATT[:, (p % 2) * 512:(p % 2 + 1) * 512],
                        in1=ln1_fm[:, p * T + tc * 512:p * T + (tc + 1) * 512],
                        op=ALU.add), inc="sV")
                    att2_done[p] = w.v("sV")

            def p1f(tc):
                for jc in range(8):
                    qslot = jc % 2
                    w.wait("tensor", "sV", att2_done[7])
                    if tc == 0 and jc == 0:
                        w.wait("tensor", "sDW", wload)
                    if jc >= 2:
                        w.wait("tensor", "sS", gelu_done[jc - 2])
                    elif tc >= 1:
                        w.wait("tensor", "sS", gelu_done[6 + qslot])

                    def p1mm(e, jc=jc, qslot=qslot):
                        ii = None
                        for ci in range(8):
                            ii = e.matmul(
                                psP1[:, qslot * 512:(qslot + 1) * 512],
                                p1T_sb[:, ci * C + jc * 128:ci * C + (jc + 1) * 128],
                                att2[:, ci * 512:(ci + 1) * 512],
                                start=(ci == 0), stop=(ci == 7))
                        return ii
                    w.op("tensor", p1mm, inc="sT")
                    p1_last[tc] = w.v("sT")
                    w.wait("scalar", "sT", w.v("sT"))
                    if tc >= 1 and jc == 0:
                        w.wait("scalar", "sT", p2mm_done[tc * 4 - 1])
                    w.op("scalar", lambda e, jc=jc, qslot=qslot: e.activation(
                        out=hring[:, jc * 512:(jc + 1) * 512],
                        in_=psP1[:, qslot * 512:(qslot + 1) * 512],
                        func=AF.Gelu, bias=p1b_cols[:, jc:jc + 1]), inc="sS")
                    gelu_done[jc] = w.v("sS")

            def body(tc):
                x2slot = tc % 2
                xsem, xload = xload_v[tc]
                for s4 in range(4):
                    s = tc * 4 + s4
                    pslot = s4 % 2
                    lslot = s4 % 2
                    w.wait("scalar", xsem, xload)
                    if s >= 2:
                        w.wait("scalar", "sV", ao_done.get(s - 2, 0))
                    if s >= 1:
                        w.wait("scalar", "sV", ao_done.get(s - 1, 0))
                    w.op("scalar", lambda e, s=s, s4=s4,
                         x2slot=x2slot: e.activation(
                        out=ln1tm2[:],
                        in_=xring2[:, x2slot * 4096 + s4 * 1024:
                                   x2slot * 4096 + (s4 + 1) * 1024],
                        func=AF.Identity, scale=r1_sb[:, s:s + 1],
                        bias=nmr1_sb[:, s:s + 1]), inc="sS")
                    lnb_done[tc] = w.v("sS")

                    w.wait("tensor", "sS", gelu_done[7])
                    if s4 >= 2:
                        w.wait("tensor", "sV", ao_done[s - 2])

                    def p2mm(e, s4=s4, pslot=pslot):
                        ii = None
                        for jh in range(2):
                            reg = psP2[:, pslot * 1024 + jh * 512:
                                       pslot * 1024 + (jh + 1) * 512]
                            e.matmul(reg, ones_row[:],
                                     p2b_row[0:1, jh * 512:(jh + 1) * 512],
                                     start=True, stop=False)
                            for ci in range(8):
                                ii = e.matmul(
                                    reg,
                                    hring[:, ci * 512 + s4 * 128:
                                          ci * 512 + s4 * 128 + 128],
                                    p2T_sb[:, ci * C + jh * 512:
                                           ci * C + (jh + 1) * 512],
                                    start=False, stop=(ci == 7))
                        return ii
                    w.op("tensor", p2mm, inc="sT")
                    p2mm_done[s] = w.v("sT")
                    w.wait("vector", "sT", p2mm_done[s])
                    w.wait("vector", "sS", lnb_done[tc])
                    if s >= 4:
                        w.wait("vector", "sS", zs_done[s - 4])
                    w.op("vector", lambda e, s4=s4, pslot=pslot:
                         e.tensor_tensor(
                             out=aoring[:, s4 * 1024:(s4 + 1) * 1024],
                             in0=psP2[:, pslot * 1024:(pslot + 1) * 1024],
                             in1=ln1tm2[:],
                             op=ALU.add), inc="sV")
                    ao_done[s] = w.v("sV")
                    quant_chunk()
                    quant_chunk()
                    w.wait("scalar", "sV", ao_done[s])
                    if tc >= 1 and s4 == 0:
                        w.wait("scalar", "sV", chain1_done[tc - 1])
                    w.wait("scalar", "sS", w.v("sS"))
                    w.op("scalar", lambda e, s4=s4: e.activation(
                        out=junk2[:], in_=aoring[:, s4 * 1024:(s4 + 1) * 1024],
                        func=AF.Square, accum_out=ssqc[:, s4:s4 + 1]), inc="sS")
                    w.wait("vector", "sV", w.v("sV"))
                    w.op("vector", lambda e, s4=s4: e.tensor_reduce(
                        out=sxc[:, s4:s4 + 1],
                        in_=aoring[:, s4 * 1024:(s4 + 1) * 1024],
                        axis=AX.X, op=ALU.add), inc="sV")

                w.wait("vector", "sS", w.v("sS"))
                w.wait("vector", "sV", w.v("sV"))

                w.vops("vector", [
                    lambda e: e.tensor_scalar(out=negm[:], in0=sxc[:],
                                              scalar1=-1.0 / C, scalar2=None,
                                              op0=ALU.mult),
                    lambda e: e.tensor_tensor(out=msq[:], in0=negm[:],
                                              in1=negm[:], op=ALU.mult),
                    lambda e: e.scalar_tensor_tensor(
                        out=vvar[:], in0=ssqc[:], scalar=1.0 / C, in1=msq[:],
                        op0=ALU.mult, op1=ALU.subtract)])
                chain1_done[tc] = w.v("sV")
                w.wait("scalar", "sV", w.v("sV"))
                w.op("scalar", lambda e: e.activation(
                    out=srt[:], in_=vvar[:], func=AF.Sqrt, bias=ceps[:]), inc="sS")
                w.wait("vector", "sS", w.v("sS"))

                w.vops("vector", [
                    lambda e: e.reciprocal(rrc[:], srt[:]),
                    lambda e: e.tensor_tensor(out=nmrc[:], in0=negm[:],
                                              in1=rrc[:], op=ALU.mult)])
                stats_done = w.v("sV")

                for s4 in range(4):
                    s = tc * 4 + s4
                    zslot = s4 % 2
                    w.wait("scalar", "sV", stats_done)
                    if s >= 2:
                        w.wait("scalar", "sV", y1tt_done.get(s - 2, 0))
                    if s >= 1:
                        w.wait("scalar", "sV", y1tt_done.get(s - 1, 0))
                    w.op("scalar", lambda e, s4=s4: e.activation(
                        out=zring[:],
                        in_=aoring[:, s4 * 1024:(s4 + 1) * 1024],
                        func=AF.Identity, scale=rrc[:, s4:s4 + 1],
                        bias=nmrc[:, s4:s4 + 1]), inc="sS")
                    zs_done[s] = w.v("sS")
                    sy = "sY0" if zslot == 0 else "sY1"
                    w.wait("vector", xsem, xload)
                    w.wait("vector", "sS", zs_done[s])
                    if s >= 2:
                        w.wait("vector", sy, y1st_done.get(s - 2, 0))
                    w.op("vector", lambda e, s4=s4, zslot=zslot,
                         x2slot=x2slot: e.tensor_tensor(
                        out=y1ring[:, zslot * 1024:(zslot + 1) * 1024],
                        in0=xring2[:, x2slot * 4096 + s4 * 1024:
                                   x2slot * 4096 + (s4 + 1) * 1024],
                        in1=zring[:],
                        op=ALU.add), inc="sV")
                    y1tt_done[s] = w.v("sV")
                    w.wait("gpsimd", "sV", y1tt_done[s])
                    w.op("gpsimd", lambda e, s=s, zslot=zslot: e.dma_start(
                        out=y1s_d[b, s * 128:(s + 1) * 128, :],
                        in_=y1ring[:, zslot * 1024:(zslot + 1) * 1024]),
                        inc=sy, k=16)
                    y1st_done[s] = w.v(sy)
                    quant_chunk()
                    quant_chunk()

            xld(0)
            attf(0)
            p1f(0)
            for tc in range(NTC):
                if tc + 1 < NTC:
                    xld(tc + 1)
                    attf(tc + 1)
                body(tc)
                if tc + 1 < NTC:
                    p1f(tc + 1)

            w.wait("gpsimd", "sY0", w.v("sY0"))
            w.wait("gpsimd", "sY1", w.v("sY1"))
            w.wait("gpsimd", "sSO0", w.v("sSO0"))
            w.wait("gpsimd", "sSO1", w.v("sSO1"))
            if debug:
                w.op("gpsimd", lambda e: e.dma_start(out=dbg["d_att2"],
                                                     in_=att2[:]), inc="sDO", k=16)
                w.wait("gpsimd", "sDO", w.v("sDO"))

        if 'P2' in run_p:
            w.block(p2, f"p2_{b}")
        p2sc.close()
        absc.close()

        p3sc = ExitStack()
        wq1_sb = p3sc.enter_context(nc.sbuf_tensor(f"wq1{b}", [128, 8 * FF], F8))
        wq2_sb = p3sc.enter_context(nc.sbuf_tensor(f"wq2{b}", [128, 32 * C], F8))
        y1r3 = p3sc.enter_context(nc.sbuf_tensor(f"y1r3{b}", [128, 4 * 2048], F32))
        ln2tm = p3sc.enter_context(nc.sbuf_tensor(f"l2t{b}", [128, 2 * 1024], BF16))
        ln2fm = p3sc.enter_context(nc.sbuf_tensor(f"l2f{b}", [128, 2 * 2048], BF16))
        h1 = p3sc.enter_context(nc.sbuf_tensor(f"h1_{b}", [128, 2 * 32 * 256], BF16))
        ffnr = p3sc.enter_context(nc.sbuf_tensor(f"ffnr{b}", [128, 2 * 1024], F32))
        z3r = p3sc.enter_context(nc.sbuf_tensor(f"z3r{b}", [128, 1024], F32))
        outr = p3sc.enter_context(nc.sbuf_tensor(f"outr{b}", [128, 1024], F32))
        psF1 = p3sc.enter_context(nc.psum_tensor(f"psF1{b}", [128, 2048], F32))
        psF2 = p3sc.enter_context(nc.psum_tensor(f"psF2{b}", [128, 2048], F32))

        def p3(w, b=b, wq1_sb=wq1_sb, wq2_sb=wq2_sb, y1r3=y1r3, ln2tm=ln2tm,
               ln2fm=ln2fm, h1=h1, ffnr=ffnr, z3r=z3r, outr=outr, psF1=psF1,
               psF2=psF2):
            w.op("sync", lambda e: e.dma_start(
                out=wq1_sb[:].rearrange("p (a f) -> p a f", f=FF),
                in_=wq1s_d.rearrange("(a p) f -> p a f", p=128)), inc="sWQ", k=16)
            wload = w.v("sWQ")
            w.op("sync", lambda e: e.dma_start(
                out=wq2_sb[:].rearrange("p (a d) -> p a d", d=C),
                in_=wq2s_d.rearrange("(a p) d -> p a d", p=128)), inc="sGB", k=16)
            wload2 = w.v("sGB")

            evac_done = {}
            tr_done3 = {}
            relu_done = {}
            f1_last = {}
            f2_done = {}
            ffn_done = {}
            outtt_done = {}
            outst_done = {}
            zs_done = {}
            apl_done = {}
            chain1_done = {}
            chain3_done = {}

            yload_map = {}

            def yld3(tc3):
                yslot = tc3 % 4
                ysem = ("sXL", "sXL1", "sXL2", "sXL3")[yslot]
                if tc3 >= 4:
                    w.wait("sync", "sV", outtt_done.get(2 * (tc3 - 4) + 1, 0))
                    w.wait("sync", "sV", apl_done.get(2 * (tc3 - 4) + 1, 0))
                w.op("sync", lambda e, tc3=tc3, yslot=yslot: e.dma_start(
                    out=y1r3[:, yslot * 2048:(yslot + 1) * 2048].rearrange(
                        "p (s d) -> p s d", d=C),
                    in_=y1s_d[b, tc3 * 256:(tc3 + 1) * 256, :].rearrange(
                        "(s p) d -> p s d", p=128)), inc=ysem, k=16)
                yload_map[tc3] = w.v(ysem)

            def p3front(tc3):
                slot = tc3 % 2
                s_hi = 2 * tc3 + 1
                yslot = tc3 % 4
                ysem = ("sXL", "sXL1", "sXL2", "sXL3")[yslot]
                yload = yload_map[tc3]

                w.wait("scalar", ysem, yload)
                if tc3 >= 1:
                    w.wait("scalar", "sV", chain1_done[tc3 - 1])
                w.wait("vector", ysem, yload)
                for s2 in range(2):
                    ysub = y1r3[:, yslot * 2048 + s2 * 1024:
                                yslot * 2048 + (s2 + 1) * 1024]
                    w.wait("scalar", "sS", w.v("sS"))
                    w.op("scalar", lambda e, ysub=ysub, s2=s2: e.activation(
                        out=junk2[:], in_=ysub, func=AF.Square,
                        accum_out=ssqc[:, s2:s2 + 1]), inc="sS")
                    w.op("vector", lambda e, ysub=ysub, s2=s2: e.tensor_reduce(
                        out=sxc[:, s2:s2 + 1], in_=ysub, axis=AX.X, op=ALU.add),
                        inc="sV" if s2 == 1 else None)
                w.wait("vector", "sS", w.v("sS"))
                w.wait("vector", "sV", w.v("sV"))


                w.vops("vector", [
                    lambda e: e.tensor_scalar(out=negm[:, 0:2], in0=sxc[:, 0:2],
                                              scalar1=-1.0 / C, scalar2=None,
                                              op0=ALU.mult),
                    lambda e: e.tensor_tensor(out=msq[:, 0:2], in0=negm[:, 0:2],
                                              in1=negm[:, 0:2], op=ALU.mult),
                    lambda e: e.scalar_tensor_tensor(
                        out=vvar[:, 0:2], in0=ssqc[:, 0:2], scalar=1.0 / C,
                        in1=msq[:, 0:2], op0=ALU.mult, op1=ALU.subtract)])
                chain1_done[tc3] = w.v("sV")
                w.wait("scalar", "sV", w.v("sV"))
                w.op("scalar", lambda e: e.activation(
                    out=srt[:, 0:2], in_=vvar[:, 0:2], func=AF.Sqrt,
                    bias=ceps[:]), inc="sS")
                w.wait("vector", "sS", w.v("sS"))

                w.vops("vector", [
                    lambda e: e.reciprocal(rrc[:, 0:2], srt[:, 0:2]),
                    lambda e: e.tensor_tensor(out=nmrc[:, 0:2], in0=negm[:, 0:2],
                                              in1=rrc[:, 0:2], op=ALU.mult)])
                stats_done = w.v("sV")

                for s2 in range(2):
                    s = tc3 * 2 + s2
                    lsl = s % 2
                    w.wait("vector", "sV", stats_done)
                    if s >= 2:
                        w.wait("vector", *tr_done3[s - 2])
                    w.op("vector", lambda e, s2=s2, yslot=yslot, lsl=lsl:
                         e.scalar_tensor_tensor(
                             out=ln2tm[:, lsl * 1024:(lsl + 1) * 1024],
                             in0=y1r3[:, yslot * 2048 + s2 * 1024:
                                      yslot * 2048 + (s2 + 1) * 1024],
                             scalar=rrc[:, s2:s2 + 1],
                             in1=nmrc[:, s2:s2 + 1].broadcast_to([128, 1024]),
                             op0=ALU.mult, op1=ALU.add), inc="sV")
                    apl_done[s] = w.v("sV")
                    if tc3 >= 2 and s2 == 0:
                        w.wait("scalar", "sT", f1_last.get(tc3 - 2, 0))
                    trs = "sTR0" if lsl == 0 else "sTR1"
                    w.wait("scalar", "sV", apl_done[s])
                    w.op("scalar", lambda e, s2=s2, slot=slot, lsl=lsl: e.dma_start(
                        out=AP(ln2fm, slot * 2048 + s2 * 128,
                               [[2 * 2048, 128], [256, 8], [1, 128]]),
                        in_=ln2tm[:, lsl * 1024:(lsl + 1) * 1024],
                        transpose=True), inc=trs, k=16)
                    tr_done3[s] = (trs, w.v(trs))

            relu_last = {}

            def f1blk(tc3):
                slot = tc3 % 2
                hsl = tc3 % 2
                s_hi = 2 * tc3 + 1
                for fc in (range(32) if p3sub in ('full', 'f1') else []):
                    fslot = fc % 4
                    if fc == 0:
                        w.wait("tensor", *tr_done3[s_hi - 1])
                        w.wait("tensor", *tr_done3[s_hi])
                    if tc3 == 0 and fc == 0:
                        w.wait("tensor", "sWQ", wload)
                    if fc >= 4:
                        w.wait("tensor", "sS", relu_done[fc - 4])
                    elif tc3 >= 1:
                        w.wait("tensor", "sS", relu_done.get(28 + fslot, 0))

                    def f1mm(e, fc=fc, fslot=fslot, slot=slot):
                        ii = None
                        for ci in range(8):
                            ii = e.matmul(
                                psF1[:, fslot * 512:fslot * 512 + 256],
                                wq1_sb[:, ci * FF + fc * 128:
                                       ci * FF + (fc + 1) * 128],
                                ln2fm[:, slot * 2048 + ci * 256:
                                      slot * 2048 + (ci + 1) * 256],
                                start=(ci == 0), stop=(ci == 7))
                        return ii
                    w.op("tensor", f1mm, inc="sT")
                    if fc == 31:
                        f1_last[tc3] = w.v("sT")
                    w.wait("scalar", "sT", w.v("sT"))
                    if tc3 >= 2 and fc == 0:
                        w.wait("scalar", "sT", f2_done.get(2 * (tc3 - 2) + 1, 0))
                    w.op("scalar", lambda e, fc=fc, fslot=fslot, hsl=hsl:
                         e.activation(
                             out=h1[:, hsl * 8192 + fc * 256:
                                    hsl * 8192 + (fc + 1) * 256],
                             in_=psF1[:, fslot * 512:fslot * 512 + 256],
                             func=AF.Relu, bias=b1_cols[:, fc:fc + 1]),
                        inc="sS")
                    relu_done[fc] = w.v("sS")
                relu_last[tc3] = relu_done.get(31, 0)

            def f2blk(tc3):
                slot = tc3 % 2
                hsl = tc3 % 2
                yslot = tc3 % 4
                ysem = ("sXL", "sXL1", "sXL2", "sXL3")[yslot]
                yload = yload_map[tc3]

                if p3sub in ('ln2', 'f1'):
                    for s2 in range(2):
                        s = tc3 * 2 + s2
                        if s >= 1:
                            w.wait("vector", "sOS", outst_done.get(s - 1, 0))
                        w.wait("vector", "sXL", yload)
                        w.op("vector", lambda e, s2=s2: e.tensor_copy(
                            out=outr[:], in_=y1r3[:, s2 * 1024:(s2 + 1) * 1024]),
                            inc="sV")
                        outtt_done[s] = w.v("sV")
                        apl_done.setdefault(s, 0)
                        w.wait("gpsimd", "sV", outtt_done[s])
                        w.op("gpsimd", lambda e, s=s: e.dma_start(
                            out=out_d[b, s * 128:(s + 1) * 128, :], in_=outr[:]),
                            inc="sOS", k=16)
                        outst_done[s] = w.v("sOS")
                    return
                for s2 in range(2):
                    s = tc3 * 2 + s2
                    pslot = s2
                    w.wait("tensor", "sS", relu_last.get(tc3, 0))
                    if tc3 == 0 and s2 == 0:
                        w.wait("tensor", "sGB", wload2)
                    if s >= 2:
                        w.wait("tensor", "sV", ffn_done.get(s - 2, 0))

                    def f2mm(e, s2=s2, pslot=pslot, hsl=hsl):
                        ii = None
                        for jh in range(2):
                            reg = psF2[:, pslot * 1024 + jh * 512:
                                       pslot * 1024 + (jh + 1) * 512]
                            for fc in range(32):
                                ii = e.matmul(
                                    reg,
                                    h1[:, hsl * 8192 + fc * 256 + s2 * 128:
                                       hsl * 8192 + fc * 256 + s2 * 128 + 128],
                                    wq2_sb[:, fc * C + jh * 512:
                                           fc * C + (jh + 1) * 512],
                                    start=(fc == 0), stop=(fc == 31))
                        return ii
                    w.op("tensor", f2mm, inc="sT")
                    f2_done[s] = w.v("sT")
                    w.wait("vector", "sT", f2_done[s])
                    w.op("vector", lambda e, s2=s2, pslot=pslot:
                         e.tensor_tensor(
                             out=ffnr[:, s2 * 1024:(s2 + 1) * 1024],
                             in0=psF2[:, pslot * 1024:(pslot + 1) * 1024],
                             in1=B2tile[:], op=ALU.add), inc="sV")
                    ffn_done[s] = w.v("sV")
                    w.wait("scalar", "sV", ffn_done[s])
                    if tc3 >= 1 and s2 == 0:
                        w.wait("scalar", "sV", chain3_done[tc3 - 1])
                    w.wait("scalar", "sS", w.v("sS"))
                    w.op("scalar", lambda e, s2=s2: e.activation(
                        out=junk2[:], in_=ffnr[:, s2 * 1024:(s2 + 1) * 1024],
                        func=AF.Square, accum_out=ssqc[:, 2 + s2:3 + s2]),
                        inc="sS")
                    w.wait("vector", "sS", w.v("sS"))
                    w.wait("vector", "sV", w.v("sV"))
                    w.op("vector", lambda e, s2=s2: e.tensor_reduce(
                        out=sxc[:, 2 + s2:3 + s2],
                        in_=ffnr[:, s2 * 1024:(s2 + 1) * 1024],
                        axis=AX.X, op=ALU.add), inc="sV")

                w.wait("vector", "sS", w.v("sS"))
                w.wait("vector", "sV", w.v("sV"))


                w.vops("vector", [
                    lambda e: e.tensor_scalar(out=negm[:, 2:4], in0=sxc[:, 2:4],
                                              scalar1=-1.0 / C, scalar2=None,
                                              op0=ALU.mult),
                    lambda e: e.tensor_tensor(out=msq[:, 2:4], in0=negm[:, 2:4],
                                              in1=negm[:, 2:4], op=ALU.mult),
                    lambda e: e.scalar_tensor_tensor(
                        out=vvar[:, 2:4], in0=ssqc[:, 2:4], scalar=1.0 / C,
                        in1=msq[:, 2:4], op0=ALU.mult, op1=ALU.subtract)])
                chain3_done[tc3] = w.v("sV")
                w.wait("scalar", "sV", w.v("sV"))
                w.op("scalar", lambda e: e.activation(
                    out=srt[:, 2:4], in_=vvar[:, 2:4], func=AF.Sqrt,
                    bias=ceps[:]), inc="sS")
                w.wait("vector", "sS", w.v("sS"))

                w.vops("vector", [
                    lambda e: e.reciprocal(rrc[:, 2:4], srt[:, 2:4]),
                    lambda e: e.tensor_tensor(out=nmrc[:, 2:4], in0=negm[:, 2:4],
                                              in1=rrc[:, 2:4], op=ALU.mult)])
                stats3 = w.v("sV")

                for s2 in range(2):
                    s = tc3 * 2 + s2
                    w.wait("vector", "sV", stats3)
                    if s >= 1:
                        w.wait("vector", "sV", outtt_done.get(s - 1, 0))
                    w.op("vector", lambda e, s2=s2: e.scalar_tensor_tensor(
                        out=z3r[:], in0=ffnr[:, s2 * 1024:(s2 + 1) * 1024],
                        scalar=rrc[:, 2 + s2:3 + s2],
                        in1=nmrc[:, 2 + s2:3 + s2].broadcast_to([128, 1024]),
                        op0=ALU.mult, op1=ALU.add), inc="sV")
                    zs_done[s] = w.v("sV")
                    w.wait("vector", "sV", zs_done[s])
                    w.wait("vector", ysem, yload)
                    if s >= 1:
                        w.wait("vector", "sOS", outst_done.get(s - 1, 0))
                    w.op("vector", lambda e, s2=s2, yslot=yslot: e.tensor_tensor(
                        out=outr[:],
                        in0=y1r3[:, yslot * 2048 + s2 * 1024:
                                 yslot * 2048 + (s2 + 1) * 1024],
                        in1=z3r[:], op=ALU.add), inc="sV")
                    outtt_done[s] = w.v("sV")
                    w.wait("gpsimd", "sV", outtt_done[s])
                    w.op("gpsimd", lambda e, s=s: e.dma_start(
                        out=out_d[b, s * 128:(s + 1) * 128, :], in_=outr[:]),
                        inc="sOS", k=16)
                    outst_done[s] = w.v("sOS")

            yld3(0)
            yld3(1)
            yld3(2)
            p3front(0)
            f1blk(0)
            p3front(1)
            for tc3 in range(NTC3):
                if tc3 + 3 < NTC3:
                    yld3(tc3 + 3)
                if tc3 + 1 < NTC3:
                    f1blk(tc3 + 1)
                if tc3 + 2 < NTC3:
                    p3front(tc3 + 2)
                f2blk(tc3)

            w.wait("gpsimd", "sOS", w.v("sOS"))
            w.wait("gpsimd", "sSO0", w.v("sSO0"))
            w.wait("gpsimd", "sSO1", w.v("sSO1"))

        if 'P3' in run_p:
            w.block(p3, f"p3_{b}")
        p3sc.close()

    es.close()
    return nc


def host_inputs(inputs, core):
    x = np.ascontiguousarray(np.asarray(inputs["L0_attn_x"],
                                        np.float32)[2 * core:2 * core + 2])
    qkv_b = np.asarray(inputs["L0_attn_qkv_b"], np.float32)
    d = {
        "x": x,
        "qkvwT": np.ascontiguousarray(
            np.asarray(inputs["L0_attn_qkv_w"], np.float32).T),
        "qb_cols": np.ascontiguousarray(qkv_b[:C].reshape(8, 128).T),
        "kvb_row": np.ascontiguousarray(qkv_b[C:].reshape(1, 2 * C)),
        "p1wT": np.ascontiguousarray(np.asarray(inputs["L0_attn_p1_w"], np.float32).T),
        "p1b_cols": np.ascontiguousarray(
            np.asarray(inputs["L0_attn_p1_b"], np.float32).reshape(8, 128).T),
        "p2wT": np.ascontiguousarray(np.asarray(inputs["L0_attn_p2_w"], np.float32).T),
        "p2b_row": np.ascontiguousarray(
            np.asarray(inputs["L0_attn_p2_b"], np.float32).reshape(1, C)),
        "w1T": np.ascontiguousarray(np.asarray(inputs["L0_ffn_W1"], np.float32).T),
        "b1_cols": np.ascontiguousarray(
            np.asarray(inputs["L0_ffn_b1"], np.float32).reshape(32, 128).T),
        "w2T": np.ascontiguousarray(np.asarray(inputs["L0_ffn_W2"], np.float32).T),
        "b2_row": np.ascontiguousarray(
            np.asarray(inputs["L0_ffn_b2"], np.float32).reshape(1, C)),
        "g64": np.ascontiguousarray(np.concatenate([
            np.asarray(inputs["L0_attn_kn_g"], np.float32),
            np.asarray(inputs["L0_attn_vn_g"], np.float32),
            np.asarray(inputs["L0_attn_kn_b"], np.float32),
            np.asarray(inputs["L0_attn_vn_b"], np.float32)]).reshape(1, 256)),
        "tempr": np.ascontiguousarray(
            np.asarray(inputs["L0_attn_temperature"], np.float32).reshape(16, 1)),
    }
    ab = np.asarray(inputs["L0_attn_attn_bias"], np.float32).reshape(8, 2, HD, HD)
    abias = np.concatenate([ab[:, 0], ab[:, 1]], axis=1)
    d["abias"] = np.ascontiguousarray(abias.transpose(1, 0, 2).reshape(128, 512))
    return d


_NC_CACHE = {}


def kernel(**inputs):
    if "prog" not in _NC_CACHE:
        _NC_CACHE["prog"] = build_program(debug=False)
    nc = _NC_CACHE["prog"]
    in_maps = [host_inputs(inputs, core) for core in range(NCORES)]
    res = run_bass_kernel_spmd(nc, in_maps, core_ids=list(range(NCORES)))
    out = np.empty((16, T, C), np.float32)
    for core in range(NCORES):
        out[2 * core:2 * core + 2] = res.results[core]["out"]
    return out


# version 30
# speedup vs baseline: 1.1726x; 1.0102x over previous
import numpy as np
from contextlib import ExitStack

import concourse.bass as bass
import concourse.mybir as mybir
from concourse.bass_utils import run_bass_kernel_spmd

F32 = mybir.dt.float32
BF16 = mybir.dt.bfloat16
I32 = mybir.dt.int32
F8 = mybir.dt.float8e4
AF = mybir.ActivationFunctionType
ALU = mybir.AluOpType
AX = mybir.AxisListType
AP = bass.AP

NCORES = 8
NB = 2
T = 2048
C = 1024
H = 16
HD = 64
FF = 4096
NSUB = 16
NTC = 4
NTC3 = 8
EPS = 1e-5
WN = float(FF * C)


class Walk:

    def __init__(self, nc, semnames):
        self.nc = nc
        self.sems = {}
        self.cnt = {n: 0 for n in semnames}
        self.names = list(semnames)
        self.cur_name = None
        self.cur_eng = None

    def op(self, eng, fn, inc=None, k=1):
        if self.cur_name == eng:
            instr = fn(self.cur_eng)
            if inc is not None:
                instr.then_inc(self.sems[inc], k)
        if inc is not None:
            self.cnt[inc] += k

    def wait(self, eng, sem, val):
        if val <= 0:
            return
        if self.cur_name == eng:
            self.cur_eng.wait_ge(self.sems[sem], val)

    def v(self, sem):
        return self.cnt[sem]

    def vops(self, eng, fns, sem="sV"):
        for fn in fns:
            self.op(eng, fn, inc=sem)
            self.wait(eng, sem, self.v(sem))

    def block(self, fn, name=None):
        nc = self.nc
        base = dict(self.cnt)
        final = {}

        def mk(engname):
            def body(e):
                self.cnt = dict(base)
                self.cur_name = engname
                self.cur_eng = e
                fn(self)
                final.update(self.cnt)
            return body

        with nc.Block(name) as blk:
            blk.sync(mk("sync"))
            blk.scalar(mk("scalar"))
            blk.vector(mk("vector"))
            blk.tensor(mk("tensor"))
            blk.gpsimd(mk("gpsimd"))
        self.cnt = dict(final)


def build_program(debug=False, phases=('S','Q','P1','P2','P3'), p3sub='full'):
    nc = bass.Bass("TRN2", target_bir_lowering=False)

    x_d = nc.dram_tensor("x", [NB, T, C], F32, kind="ExternalInput").ap()
    qkvwT_d = nc.dram_tensor("qkvwT", [C, 3 * C], F32, kind="ExternalInput").ap()
    qbcols_d = nc.dram_tensor("qb_cols", [128, 8], F32, kind="ExternalInput").ap()
    kvbrow_d = nc.dram_tensor("kvb_row", [1, 2 * C], F32, kind="ExternalInput").ap()
    p1wT_d = nc.dram_tensor("p1wT", [C, C], F32, kind="ExternalInput").ap()
    p1bcols_d = nc.dram_tensor("p1b_cols", [128, 8], F32, kind="ExternalInput").ap()
    p2wT_d = nc.dram_tensor("p2wT", [C, C], F32, kind="ExternalInput").ap()
    p2brow_d = nc.dram_tensor("p2b_row", [1, C], F32, kind="ExternalInput").ap()
    w1T_d = nc.dram_tensor("w1T", [C, FF], F32, kind="ExternalInput").ap()
    b1cols_d = nc.dram_tensor("b1_cols", [128, 32], F32, kind="ExternalInput").ap()
    w2T_d = nc.dram_tensor("w2T", [FF, C], F32, kind="ExternalInput").ap()
    b2row_d = nc.dram_tensor("b2_row", [1, C], F32, kind="ExternalInput").ap()
    g64_d = nc.dram_tensor("g64", [1, 256], F32, kind="ExternalInput").ap()
    temp_d = nc.dram_tensor("tempr", [16, 1], F32, kind="ExternalInput").ap()
    abias_d = nc.dram_tensor("abias", [128, 512], F32, kind="ExternalInput").ap()
    out_d = nc.dram_tensor("out", [NB, T, C], F32, kind="ExternalOutput").ap()
    y1s_d = nc.dram_tensor("y1s", [NB, T, C], F32, kind="Internal").ap()
    wq1s_d = nc.dram_tensor("wq1s", [C, FF], F8, kind="Internal").ap()
    wq2s_d = nc.dram_tensor("wq2s", [FF, C], F8, kind="Internal").ap()
    gbscr_d = nc.dram_tensor("gbscr", [2, 2 * C], F32, kind="Internal").ap()
    smscr_d = nc.dram_tensor("smscr", [16, 1], F32, kind="Internal").ap()

    dbg = {}
    if debug:
        for nm, shp, dt in [("d_ln1fm", [128, 8 * T], F32),
                            ("d_q", [128, 8 * T], F32),
                            ("d_khat", [128, 3 * 2048], F32),
                            ("d_kvbd", [128, 1024], F32),
                            ("d_att2", [128, 8 * 512], F32),
                            ("d_gb", [128, 2 * 2048], F32),
                            ("d_invg", [128, 4], F32),
                            ("d_red16", [128, 16], F32),
                            ("d_g11", [1, 16], F32)]:
            dbg[nm] = nc.dram_tensor(nm, shp, dt, kind="ExternalOutput").ap()

    es = ExitStack()

    def sb(name, shape, dt):
        return es.enter_context(nc.sbuf_tensor(name, shape, dt))

    ident = sb("ident", [128, 128], BF16)
    colidx = sb("colidx", [128, 128], I32)
    rowidx = sb("rowidx", [128, 1], F32)
    ones_row = sb("ones_row", [1, 128], BF16)
    onescol_f = sb("onescol_f", [128, 1], F32)
    chalf = sb("chalf", [128, 1], F32)
    cnhalf = sb("cnhalf", [128, 1], F32)
    ceps = sb("ceps", [128, 1], F32)
    ceps2 = sb("ceps2", [128, 1], F32)
    qb_cols = sb("qb_cols_s", [128, 8], F32)
    p1b_cols = sb("p1b_cols_s", [128, 8], F32)
    b1_cols = sb("b1_cols_s", [128, 32], F32)
    kvb_row = sb("kvb_row_s", [1, 2 * C], BF16)
    p2b_row = sb("p2b_row_s", [1, C], BF16)
    B2tile = sb("B2tile", [128, C], F32)
    abias_sb = sb("abias_sb", [128, 512], F32)
    tempQ = sb("tempQ", [128, 8], F32)
    kvbd = sb("kvbd", [128, 8 * 128], BF16)
    invg_col = sb("invg_col", [128, 4], F32)
    r1_sb = sb("r1_sb", [128, NSUB], F32)
    nmr1_sb = sb("nmr1_sb", [128, NSUB], F32)
    sxc = sb("sxc", [128, 4], F32)
    ssqc = sb("ssqc", [128, 4], F32)
    negm = sb("negm", [128, 4], F32)
    msq = sb("msq", [128, 4], F32)
    vvar = sb("vvar", [128, 4], F32)
    srt = sb("srt", [128, 4], F32)
    rrc = sb("rrc", [128, 4], F32)
    nmrc = sb("nmrc", [128, 4], F32)
    khsum = sb("khsum", [128, 32], F32)
    khsq = sb("khsq", [128, 32], F32)
    khneg = sb("khneg", [128, 32], F32)
    khmsq = sb("khmsq", [128, 32], F32)
    khv = sb("khv", [128, 32], F32)
    khsrt = sb("khsrt", [128, 32], F32)
    khr = sb("khr", [128, 32], F32)
    junk2 = sb("junk2", [128, 1024], F32)

    qsc = ExitStack()
    g64 = qsc.enter_context(nc.sbuf_tensor("g64_s", [1, 256], F32))
    g_row = qsc.enter_context(nc.sbuf_tensor("g_row", [1, 2 * C], F32))
    b_row = qsc.enter_context(nc.sbuf_tensor("b_row", [1, 2 * C], F32))
    t16 = qsc.enter_context(nc.sbuf_tensor("t16", [16, 1], F32))
    red16 = qsc.enter_context(nc.sbuf_tensor("red16", [128, 16], F32))
    g11 = qsc.enter_context(nc.sbuf_tensor("g11", [1, 16], F32))

    semnames = ["sDX", "sDW", "sDO", "sT", "sV", "sS", "sP",
                "sXL", "sGB", "sTQ", "sWR0", "sWR1", "sIG", "sSO0", "sSO1",
                "sY0", "sY1", "sOS", "sWQ", "sGM", "sXL1", "sXL2", "sXL3",
                "sTR0", "sTR1"]
    w = Walk(nc, semnames)
    for n in semnames:
        w.sems[n] = es.enter_context(nc.semaphore(n))

    def startup(w):
        for dst, src in [(qb_cols[:], qbcols_d), (p1b_cols[:], p1bcols_d),
                         (b1_cols[:], b1cols_d), (abias_sb[:], abias_d),
                         (t16[:], temp_d), (g64[:], g64_d),
                         (B2tile[:], AP(b2row_d.tensor, 0, [[0, 128], [1, C]]))]:
            w.op("sync", lambda e, dst=dst, src=src: e.dma_start(out=dst, in_=src),
                 inc="sDX", k=16)
        w.op("gpsimd", lambda e: e.dma_start(out=kvb_row[:], in_=kvbrow_d),
             inc="sDW", k=16)
        w.op("gpsimd", lambda e: e.dma_start(out=p2b_row[:], in_=p2brow_d),
             inc="sDW", k=16)
        w.op("gpsimd", lambda e: e.iota(colidx[:], [[1, 128]], base=0,
                                        channel_multiplier=0), inc="sP")
        w.op("gpsimd", lambda e: e.iota(rowidx[:], [[1, 1]], base=0,
                                        channel_multiplier=1,
                                        allow_small_or_imprecise_dtypes=True),
             inc="sP")

        def msets(e):
            e.memset(ones_row[:], 1.0)
            e.memset(onescol_f[:], 1.0)
            e.memset(chalf[:], 0.5)
            e.memset(cnhalf[:], -0.5)
            e.memset(ceps[:], EPS)
            e.memset(ceps2[:], EPS * HD * HD)
            return e.memset(kvbd[:], 0.0)
        w.op("vector", msets, inc="sV")

        w.wait("vector", "sP", w.v("sP"))
        w.op("vector", lambda e: e.tensor_scalar(
            out=ident[:], in0=colidx[:], scalar1=rowidx[:], scalar2=None,
            op0=ALU.is_equal), inc="sV")

        w.wait("vector", "sDX", w.v("sDX"))

        def tile_gb(e):
            for half in range(2):
                e.tensor_copy(
                    out=g_row[0:1, half * C:(half + 1) * C].rearrange(
                        "p (a d) -> p a d", d=HD),
                    in_=g64[0:1, half * HD:(half + 1) * HD].unsqueeze(1)
                    .broadcast_to([1, H, HD]))
            for half in range(2):
                e.tensor_copy(
                    out=b_row[0:1, half * C:(half + 1) * C].rearrange(
                        "p (a d) -> p a d", d=HD),
                    in_=g64[0:1, 128 + half * HD: 128 + (half + 1) * HD]
                    .unsqueeze(1).broadcast_to([1, H, HD]))
            return e.tensor_scalar(out=t16[:], in0=t16[:], scalar1=1.0 / T,
                                   scalar2=None, op0=ALU.mult)
        w.op("vector", tile_gb, inc="sV")
        w.wait("gpsimd", "sV", w.v("sV"))
        w.op("gpsimd", lambda e: e.dma_start(out=gbscr_d[0:1, :], in_=g_row[:]),
             inc="sDO", k=16)
        w.op("gpsimd", lambda e: e.dma_start(out=gbscr_d[1:2, :], in_=b_row[:]),
             inc="sDO", k=16)
        w.op("gpsimd", lambda e: e.dma_start(out=smscr_d, in_=t16[:]),
             inc="sDO", k=16)
        w.wait("sync", "sDO", w.v("sDO"))
        with nc.allow_non_contiguous_dma(reason="tiny one-time tempQ broadcast"):
            w.op("sync", lambda e: e.dma_start(
                out=tempQ[0:64, :], in_=AP(smscr_d.tensor, 0, [[0, 64], [2, 8]])),
                inc="sTQ", k=16)
            w.op("sync", lambda e: e.dma_start(
                out=tempQ[64:128, :], in_=AP(smscr_d.tensor, 1, [[0, 64], [2, 8]])),
                inc="sTQ", k=16)
        w.wait("sync", "sTQ", w.v("sTQ"))
        w.wait("gpsimd", "sDW", w.v("sDW"))

    w.block(startup, "startup")
    run_p = phases

    qsc.close()

    def wchunk_src(gi):
        if gi < 32:
            return w1T_d[(gi // 4) * 128:(gi // 4 + 1) * 128,
                         (gi % 4) * 1024:(gi % 4 + 1) * 1024]
        r = gi - 32
        return w2T_d[r * 128:(r + 1) * 128, :]

    def wchunk_dst(gi):
        if gi < 32:
            return wq1s_d[(gi // 4) * 128:(gi // 4 + 1) * 128,
                          (gi % 4) * 1024:(gi % 4 + 1) * 1024]
        r = gi - 32
        return wq2s_d[r * 128:(r + 1) * 128, :]

    for b in (range(NB) if 'P1' in run_p else []):
        absc = ExitStack()
        ln1_fm = absc.enter_context(nc.sbuf_tensor(f"ln1fm{b}", [128, 8 * T], BF16))
        q_fm = absc.enter_context(nc.sbuf_tensor(f"qfm{b}", [128, 8 * T], BF16))
        p1sc = ExitStack()
        G64 = p1sc.enter_context(nc.sbuf_tensor(f"G64_{b}", [128, 2 * HD], F32))
        B64 = p1sc.enter_context(nc.sbuf_tensor(f"B64_{b}", [128, 2 * HD], F32))
        qkvwT_sb = p1sc.enter_context(
            nc.sbuf_tensor(f"qkvwT{b}", [128, 8 * 3072], BF16))
        xring = p1sc.enter_context(nc.sbuf_tensor(f"xring{b}", [128, 4096], F32))
        ln1tm = p1sc.enter_context(nc.sbuf_tensor(f"ln1tm{b}", [128, 2 * 1024], BF16))
        uu = p1sc.enter_context(nc.sbuf_tensor(f"uu{b}", [128, 2048], F32))
        uu2 = p1sc.enter_context(nc.sbuf_tensor(f"uu2{b}", [128, 2048], F32))
        khat = p1sc.enter_context(nc.sbuf_tensor(f"khat{b}", [128, 3 * 2048], BF16))
        psKV = p1sc.enter_context(nc.psum_tensor(f"psKV{b}", [128, 2048], F32))
        psQ = p1sc.enter_context(nc.psum_tensor(f"psQ{b}", [128, 1024], F32))
        psACC = p1sc.enter_context(nc.psum_tensor(f"psACC{b}", [128, 1024], F32))
        gw_ring = p1sc.enter_context(nc.sbuf_tensor(f"gwr{b}", [128, 2 * 1024], F32))
        gw_junk = p1sc.enter_context(nc.sbuf_tensor(f"gwj{b}", [128, 1024], F32))
        junk = p1sc.enter_context(nc.sbuf_tensor(f"junk{b}", [128, 2048], F32))
        red64 = p1sc.enter_context(nc.sbuf_tensor(f"red64{b}", [128, 64], F32))
        g11 = p1sc.enter_context(nc.sbuf_tensor(f"g11{b}", [1, 16], F32))

        def p1(w, b=b, ln1_fm=ln1_fm, q_fm=q_fm, G64=G64, B64=B64,
               qkvwT_sb=qkvwT_sb, xring=xring, ln1tm=ln1tm, uu=uu, uu2=uu2,
               khat=khat, psKV=psKV, psQ=psQ, psACC=psACC,
               gw_ring=gw_ring, gw_junk=gw_junk, red64=red64, g11=g11,
               junk=junk):
            gred_done = {}

            def gamma_chunk(gi):
                if b != 0:
                    return
                wr = "sWR0" if gi % 2 == 0 else "sWR1"
                if gi >= 2:
                    w.wait("sync", "sS", gred_done[gi - 2])
                w.op("sync", lambda e, gi=gi: e.dma_start(
                    out=gw_ring[:, (gi % 2) * 1024:(gi % 2 + 1) * 1024],
                    in_=wchunk_src(gi)), inc=wr, k=16)
                w.wait("scalar", wr, w.v(wr))
                w.wait("scalar", "sS", w.v("sS"))
                w.op("scalar", lambda e, gi=gi: e.activation(
                    out=gw_junk[:], in_=gw_ring[:, (gi % 2) * 1024:(gi % 2 + 1) * 1024],
                    func=AF.Abs, accum_out=red64[:, gi:gi + 1]), inc="sS")
                gred_done[gi] = w.v("sS")
            for a in range(8):
                w.op("gpsimd", lambda e, a=a: e.dma_start(
                    out=qkvwT_sb[:, a * 3072:(a + 1) * 3072],
                    in_=qkvwT_d[a * 128:(a + 1) * 128, :],
                    max_dma_last_dim=8192), inc="sDW", k=16)
            w.op("sync", lambda e: e.dma_start(
                out=G64[:, 0:HD], in_=AP(gbscr_d.tensor, 0, [[0, 128], [1, HD]])),
                inc="sGB", k=16)
            w.op("sync", lambda e: e.dma_start(
                out=G64[:, HD:2 * HD],
                in_=AP(gbscr_d.tensor, C, [[0, 128], [1, HD]])), inc="sGB", k=16)
            w.op("sync", lambda e: e.dma_start(
                out=B64[:, 0:HD],
                in_=AP(gbscr_d.tensor, 2 * C, [[0, 128], [1, HD]])), inc="sGB", k=16)
            w.op("sync", lambda e: e.dma_start(
                out=B64[:, HD:2 * HD],
                in_=AP(gbscr_d.tensor, 3 * C, [[0, 128], [1, HD]])), inc="sGB", k=16)
            wload = w.v("sDW")
            gbload = w.v("sGB")

            evac_done = {}
            pass1g = {}
            hp_done = {}
            sq_done_g = {}
            khat_done = {}
            kvacc_done = {}
            tr_done = {}
            qevac_done = {}
            applyln_done = {}
            xcons_done = {}
            chain1_done = {}
            ksq_red = {}
            stats_done_map = {}

            def p1ldstats(tc):
                if tc >= 1:
                    w.wait("sync", "sS", applyln_done[tc - 1])
                    w.wait("sync", "sV", xcons_done[tc - 1])
                w.op("sync", lambda e, tc=tc: e.dma_start(
                    out=xring[:].rearrange("p (s d) -> p s d", d=C),
                    in_=x_d[b, tc * 512:(tc + 1) * 512, :].rearrange(
                        "(s p) d -> p s d", p=128)), inc="sXL", k=16)
                xload = w.v("sXL")

                w.wait("scalar", "sXL", xload)
                if tc >= 1:
                    w.wait("scalar", "sV", chain1_done[tc - 1])
                w.wait("vector", "sXL", xload)
                for s4 in range(4):
                    xs = xring[:, s4 * 1024:(s4 + 1) * 1024]
                    w.wait("scalar", "sS", w.v("sS"))
                    w.op("scalar", lambda e, xs=xs, s4=s4: e.activation(
                        out=junk2[:], in_=xs, func=AF.Square,
                        accum_out=ssqc[:, s4:s4 + 1]), inc="sS")
                    w.op("vector", lambda e, xs=xs, s4=s4: e.tensor_reduce(
                        out=sxc[:, s4:s4 + 1], in_=xs, axis=AX.X, op=ALU.add),
                        inc="sV" if s4 == 3 else None)
                sq_done = w.v("sS")
                xred_done = w.v("sV")
                w.wait("vector", "sS", sq_done)
                w.wait("vector", "sV", xred_done)

                w.vops("vector", [
                    lambda e: e.tensor_scalar(out=negm[:], in0=sxc[:],
                                              scalar1=-1.0 / C, scalar2=None,
                                              op0=ALU.mult),
                    lambda e: e.tensor_tensor(out=msq[:], in0=negm[:],
                                              in1=negm[:], op=ALU.mult),
                    lambda e: e.scalar_tensor_tensor(
                        out=vvar[:], in0=ssqc[:], scalar=1.0 / C, in1=msq[:],
                        op0=ALU.mult, op1=ALU.subtract)])
                chain1_done[tc] = w.v("sV")
                w.wait("scalar", "sV", w.v("sV"))
                w.op("scalar", lambda e: e.activation(
                    out=srt[:], in_=vvar[:], func=AF.Sqrt, bias=ceps[:]), inc="sS")
                w.wait("vector", "sS", w.v("sS"))

                w.vops("vector", [
                    lambda e: e.reciprocal(rrc[:], srt[:]),
                    lambda e: e.tensor_tensor(out=nmrc[:], in0=negm[:],
                                              in1=rrc[:], op=ALU.mult),
                    lambda e, tc=tc: e.tensor_copy(
                        out=r1_sb[:, tc * 4:(tc + 1) * 4], in_=rrc[:]),
                    lambda e, tc=tc: e.tensor_copy(
                        out=nmr1_sb[:, tc * 4:(tc + 1) * 4], in_=nmrc[:])])
                stats_done_map[tc] = w.v("sV")
                xcons_done[tc] = max(stats_done_map[tc], xred_done)

            def p1fronts(tc):
                for s4 in range(4):
                    s = tc * 4 + s4
                    lslot = s % 2
                    xs = xring[:, s4 * 1024:(s4 + 1) * 1024]
                    w.wait("scalar", "sV", stats_done_map[tc])
                    if s >= 2:
                        w.wait("scalar", *tr_done[s - 2])
                    w.op("scalar", lambda e, xs=xs, s4=s4, lslot=lslot: e.activation(
                        out=ln1tm[:, lslot * 1024:(lslot + 1) * 1024], in_=xs,
                        func=AF.Identity, scale=rrc[:, s4:s4 + 1],
                        bias=nmrc[:, s4:s4 + 1]), inc="sS")
                    apl = w.v("sS")
                    if s4 == 3:
                        applyln_done[tc] = apl
                    trs = "sTR0" if lslot == 0 else "sTR1"
                    w.wait("scalar", "sS", apl)
                    w.op("scalar", lambda e, s=s, lslot=lslot: e.dma_start(
                        out=AP(ln1_fm, s * 128, [[8 * T, 128], [T, 8], [1, 128]]),
                        in_=ln1tm[:, lslot * 1024:(lslot + 1) * 1024],
                        transpose=True), inc=trs, k=16)
                    tr_done[s] = (trs, w.v(trs))

            def p1body(tc):
                for s4 in range(4):
                    s = tc * 4 + s4
                    lslot = s % 2
                    if s >= 3:
                        w.wait("tensor", "sP", khat_done[s - 3])

                        def kvacc(e, sp=s - 3):
                            ii = None
                            lsp = sp % 3
                            for p in range(8):
                                ii = e.matmul(
                                    psACC[:, p * 128:(p + 1) * 128],
                                    khat[:, lsp * 2048 + p * 128:
                                         lsp * 2048 + (p + 1) * 128],
                                    khat[:, lsp * 2048 + 1024 + p * 128:
                                         lsp * 2048 + 1024 + (p + 1) * 128],
                                    start=(sp == 0 and p in (0, 4)), stop=False,
                                    skip_group_check=True)
                            return ii
                        w.op("tensor", kvacc, inc="sT")
                        kvacc_done[s - 3] = w.v("sT")

                    for g in range(4):
                        bank = g
                        jcol = 1024 + g * 512
                        half = g // 2
                        if g == 0:
                            w.wait("tensor", *tr_done[s])
                            if s == 0:
                                w.wait("tensor", "sDW", wload)
                        if s >= 1:
                            w.wait("tensor", "sV", pass1g[(s - 1, g)])
                        if s >= 1:
                            w.wait("vector", "sP", hp_done.get((s - 1, g // 2), 0))

                        def kvproj(e, s=s, jcol=jcol, bank=bank):
                            reg = psKV[:, bank * 512:(bank + 1) * 512]
                            e.matmul(reg, ones_row[:],
                                     kvb_row[0:1, jcol - 1024:jcol - 512],
                                     start=True, stop=False)
                            ii = None
                            for ci in range(8):
                                ii = e.matmul(
                                    reg,
                                    ln1_fm[:, ci * T + s * 128:
                                           ci * T + (s + 1) * 128],
                                    qkvwT_sb[:, ci * 3072 + jcol:
                                             ci * 3072 + jcol + 512],
                                    start=False, stop=(ci == 7))
                            return ii
                        w.op("tensor", kvproj, inc="sT")
                        projg = w.v("sT")

                        w.wait("vector", "sT", projg)
                        w.op("vector", lambda e, bank=bank, g=g: e.tensor_reduce(
                            out=khneg[:, g * 8:(g + 1) * 8],
                            in_=psKV[:, bank * 512:(bank + 1) * 512].rearrange(
                                "p (h d) -> p h d", d=HD),
                            axis=AX.X, op=ALU.add, negate=True), inc="sV")
                        w.wait("vector", "sV", w.v("sV"))
                        if s == 0 and g == 0:
                            w.wait("vector", "sGB", gbload)
                        w.op("vector", lambda e, bank=bank, g=g:
                             e.scalar_tensor_tensor(
                                 out=uu[:, g * 512:(g + 1) * 512].rearrange(
                                     "p (h d) -> p h d", d=HD),
                                 in0=psKV[:, bank * 512:(bank + 1) * 512].rearrange(
                                     "p (h d) -> p h d", d=HD),
                                 scalar=float(HD),
                                 in1=khneg[:, g * 8:(g + 1) * 8].unsqueeze(2)
                                 .broadcast_to([128, 8, HD]),
                                 op0=ALU.mult, op1=ALU.add), inc="sV")
                        pass1g[(s, g)] = w.v("sV")
                        w.wait("scalar", "sV", pass1g[(s, g)])
                        w.wait("scalar", "sS", w.v("sS"))
                        if (s - 1, g) in ksq_red:
                            w.wait("scalar", "sV", ksq_red[(s - 1, g)])
                        w.op("scalar", lambda e, g=g: e.activation(
                            out=junk[:, g * 512:(g + 1) * 512],
                            in_=uu[:, g * 512:(g + 1) * 512],
                            func=AF.Square), inc="sS")
                        sq_done_g[(s, g)] = w.v("sS")

                    for g in range(4):
                        bank = g % 2
                        w.wait("vector", "sS", sq_done_g[(s, g)])
                        w.op("vector", lambda e, g=g: e.tensor_reduce(
                            out=khsq[:, g * 8:(g + 1) * 8],
                            in_=junk[:, g * 512:(g + 1) * 512].rearrange(
                                "p (h d) -> p h d", d=HD),
                            axis=AX.X, op=ALU.add), inc="sV")
                        ksq_red[(s, g)] = w.v("sV")

                    for half in range(2):
                        hc = slice(half * 16, (half + 1) * 16)
                        w.wait("vector", "sV", ksq_red[(s, half * 2 + 1)])
                        w.vops("vector", [
                            lambda e, hc=hc: e.tensor_scalar(
                                out=khv[:, hc], in0=khsq[:, hc],
                                scalar1=1.0 / HD, scalar2=None,
                                op0=ALU.mult)])
                        w.wait("scalar", "sV", w.v("sV"))
                        w.op("scalar", lambda e, hc=hc: e.activation(
                            out=khsrt[:, hc], in_=khv[:, hc], func=AF.Sqrt,
                            bias=ceps2[:]), inc="sS")
                        w.wait("vector", "sS", w.v("sS"))
                        w.vops("vector", [
                            lambda e, hc=hc: e.reciprocal(khr[:, hc],
                                                          khsrt[:, hc])])
                        hr = slice(half * 1024, (half + 1) * 1024)
                        w.op("vector", lambda e, hc=hc, hr=hr: e.tensor_tensor(
                            out=uu2[:, hr].rearrange("p (h d) -> p h d", d=HD),
                            in0=uu[:, hr].rearrange("p (h d) -> p h d", d=HD),
                            in1=khr[:, hc].unsqueeze(2)
                            .broadcast_to([128, 16, HD]),
                            op=ALU.mult), inc="sV")
                        w.wait("gpsimd", "sV", w.v("sV"))
                        w.op("gpsimd", lambda e, half=half, hr=hr: e.tensor_tensor(
                            out=uu[:, hr].rearrange("p (h d) -> p h d", d=HD),
                            in0=uu2[:, hr].rearrange("p (h d) -> p h d", d=HD),
                            in1=G64[:, half * HD:(half + 1) * HD].unsqueeze(1)
                            .broadcast_to([128, 16, HD]),
                            op=ALU.mult), inc="sP")
                        w.wait("gpsimd", "sP", w.v("sP"))
                        if s >= 3 and half == 0:
                            w.wait("gpsimd", "sT", kvacc_done[s - 3])
                        w.op("gpsimd", lambda e, half=half, lslot=s % 3, hr=hr:
                             e.tensor_tensor(
                                 out=khat[:, lslot * 2048 + half * 1024:
                                          lslot * 2048 + (half + 1) * 1024]
                                 .rearrange("p (h d) -> p h d", d=HD),
                                 in0=uu[:, hr].rearrange("p (h d) -> p h d", d=HD),
                                 in1=B64[:, half * HD:(half + 1) * HD].unsqueeze(1)
                                 .broadcast_to([128, 16, HD]),
                                 op=ALU.add), inc="sP")
                        w.wait("gpsimd", "sP", w.v("sP"))
                        khat_done[s] = w.v("sP")
                        hp_done[(s, half)] = w.v("sP")

                    for u in range(4):
                        gamma_chunk(s * 4 + u)
                    if tc >= 1:
                        p1q_one(tc - 1, s4 * 2)
                        p1q_one(tc - 1, s4 * 2 + 1)

            def p1q_one(tc, jc):
                qslot = jc % 2
                if jc == 0:
                    w.wait("tensor", *tr_done[tc * 4 + 2])
                    w.wait("tensor", *tr_done[tc * 4 + 3])
                if jc >= 2:
                    w.wait("tensor", "sV", qevac_done[jc - 2])
                elif tc >= 1:
                    w.wait("tensor", "sV", qevac_done[6 + qslot])

                def qproj(e, jc=jc, qslot=qslot, tc=tc):
                    ii = None
                    for ci in range(8):
                        ii = e.matmul(
                            psQ[:, qslot * 512:(qslot + 1) * 512],
                            qkvwT_sb[:, ci * 3072 + jc * 128:
                                     ci * 3072 + (jc + 1) * 128],
                            ln1_fm[:, ci * T + tc * 512:ci * T + (tc + 1) * 512],
                            start=(ci == 0), stop=(ci == 7))
                    return ii
                w.op("tensor", qproj, inc="sT")
                w.wait("vector", "sT", w.v("sT"))
                w.op("vector", lambda e, jc=jc, qslot=qslot, tc=tc: e.tensor_scalar(
                    out=q_fm[:, jc * T + tc * 512:jc * T + (tc + 1) * 512],
                    in0=psQ[:, qslot * 512:(qslot + 1) * 512],
                    scalar1=qb_cols[:, jc:jc + 1], scalar2=None,
                    op0=ALU.add), inc="sV")
                qevac_done[jc] = w.v("sV")

            p1ldstats(0)
            for tc in range(NTC):
                p1fronts(tc)
                if tc + 1 < NTC:
                    p1ldstats(tc + 1)
                p1body(tc)
            for jc in range(8):
                p1q_one(NTC - 1, jc)

            for sp in (NSUB - 3, NSUB - 2, NSUB - 1):
                w.wait("tensor", "sP", khat_done[sp])

                def kvacc_last(e, sp=sp):
                    ii = None
                    lsp = sp % 3
                    for p in range(8):
                        ii = e.matmul(
                            psACC[:, p * 128:(p + 1) * 128],
                            khat[:, lsp * 2048 + p * 128:
                                 lsp * 2048 + (p + 1) * 128],
                            khat[:, lsp * 2048 + 1024 + p * 128:
                                 lsp * 2048 + 1024 + (p + 1) * 128],
                            start=False, stop=(sp == NSUB - 1),
                            skip_group_check=True)
                    return ii
                w.op("tensor", kvacc_last, inc="sT")
                kvacc_done[sp] = w.v("sT")

            if b == 0:
                w.wait("tensor", "sS", gred_done[63])
                w.wait("tensor", "sV", qevac_done[7])

                def gtotmm(e):
                    e.matmul(psQ[0:1, 0:32], onescol_f[:],
                             red64[:, 0:32], start=True, stop=True)
                    return e.matmul(psQ[0:1, 32:64], onescol_f[:],
                                    red64[:, 32:64], start=True, stop=True,
                                    skip_group_check=True)
                w.op("tensor", gtotmm, inc="sT")
                w.wait("vector", "sT", w.v("sT"))
                w.vops("vector", [
                    lambda e: e.tensor_reduce(out=g11[0:1, 2:3],
                                              in_=psQ[0:1, 0:32], axis=AX.X,
                                              op=ALU.add),
                    lambda e: e.tensor_reduce(out=g11[0:1, 3:4],
                                              in_=psQ[0:1, 32:64], axis=AX.X,
                                              op=ALU.add),
                    lambda e: e.tensor_scalar(
                        out=g11[0:1, 4:6], in0=g11[0:1, 2:4], scalar1=1.0 / WN,
                        scalar2=EPS, op0=ALU.mult, op1=ALU.add),
                    lambda e: e.reciprocal(g11[0:1, 6:8], g11[0:1, 4:6]),
                    lambda e: e.tensor_scalar(
                        out=g11[0:1, 8:10], in0=g11[0:1, 6:8], scalar1=-1.0,
                        scalar2=None, op0=ALU.mult)])
                w.wait("gpsimd", "sV", w.v("sV"))
                w.op("gpsimd", lambda e: e.dma_start(out=smscr_d[0:4, 0:1],
                                                     in_=g11[0:1, 6:10]),
                     inc="sGM", k=16)
                w.wait("sync", "sGM", w.v("sGM"))
                w.op("sync", lambda e: e.dma_start(
                    out=invg_col[:, 0:4],
                    in_=AP(smscr_d.tensor, 0, [[0, 128], [1, 4]])), inc="sIG", k=16)
                w.wait("sync", "sIG", w.v("sIG"))

            w.wait("vector", "sT", kvacc_done[NSUB - 1])

            def kvfin(e):
                ii = None
                for h in range(H):
                    p, r = h // 2, h % 2
                    base = r * 64
                    ii = e.scalar_tensor_tensor(
                        out=kvbd[base:base + 64,
                                 p * 128 + r * 64: p * 128 + r * 64 + 64],
                        in0=psACC[base:base + 64,
                                  p * 128 + r * 64: p * 128 + r * 64 + 64],
                        scalar=tempQ[base:base + 64, p:p + 1],
                        in1=abias_sb[base:base + 64, p * 64:(p + 1) * 64],
                        op0=ALU.mult, op1=ALU.add)
                return ii
            w.op("vector", kvfin, inc="sV")

            if debug:
                w.wait("gpsimd", "sV", w.v("sV"))
                w.wait("gpsimd", "sS", w.v("sS"))
                w.wait("gpsimd", "sT", w.v("sT"))
                for nm, src in [("d_kvbd", kvbd[:]), ("d_ln1fm", ln1_fm[:]),
                                ("d_q", q_fm[:]), ("d_khat", khat[:])]:
                    w.op("gpsimd", lambda e, nm=nm, src=src: e.dma_start(
                        out=dbg[nm], in_=src), inc="sDO", k=16)
                w.wait("gpsimd", "sDO", w.v("sDO"))

        if 'P1' in run_p:
            w.block(p1, f"p1_{b}")
        p1sc.close()

        p2sc = ExitStack()
        p1T_sb = p2sc.enter_context(nc.sbuf_tensor(f"p1T{b}", [128, 8 * C], BF16))
        p2T_sb = p2sc.enter_context(nc.sbuf_tensor(f"p2T{b}", [128, 8 * C], BF16))
        xring2 = p2sc.enter_context(nc.sbuf_tensor(f"xr2{b}", [128, 2 * 4096], F32))
        att2 = p2sc.enter_context(nc.sbuf_tensor(f"att2{b}", [128, 8 * 512], BF16))
        hring = p2sc.enter_context(nc.sbuf_tensor(f"hring{b}", [128, 8 * 512], BF16))
        aoring = p2sc.enter_context(nc.sbuf_tensor(f"aor{b}", [128, 4 * 1024], BF16))
        zring = p2sc.enter_context(nc.sbuf_tensor(f"zr{b}", [128, 1024], F32))
        ln1tm2 = p2sc.enter_context(nc.sbuf_tensor(f"l1b{b}", [128, 1024], BF16))
        y1ring = p2sc.enter_context(nc.sbuf_tensor(f"y1r{b}", [128, 2 * 1024], F32))
        psATT = p2sc.enter_context(nc.psum_tensor(f"psATT{b}", [128, 1024], F32))
        psP1 = p2sc.enter_context(nc.psum_tensor(f"psP1{b}", [128, 1024], F32))
        psP2 = p2sc.enter_context(nc.psum_tensor(f"psP2{b}", [128, 2048], F32))
        qw_ring = p2sc.enter_context(nc.sbuf_tensor(f"qwr{b}", [128, 2 * 1024], F32))
        qbuf = p2sc.enter_context(nc.sbuf_tensor(f"qbuf{b}", [128, 4 * 1024], F8))
        qbufB = p2sc.enter_context(nc.sbuf_tensor(f"qbufB{b}", [128, 2 * 1024], F8))
        qsA = p2sc.enter_context(nc.sbuf_tensor(f"qsA{b}", [128, 1024], F32))

        def p2(w, b=b, ln1_fm=ln1_fm, q_fm=q_fm, p1T_sb=p1T_sb, p2T_sb=p2T_sb,
               xring2=xring2, att2=att2, hring=hring, aoring=aoring, zring=zring,
               ln1tm2=ln1tm2, y1ring=y1ring, psATT=psATT, psP1=psP1, psP2=psP2,
               qw_ring=qw_ring, qbuf=qbuf, qbufB=qbufB, qsA=qsA):
            qtt_done = {}
            qst_done = {}
            qm_done = {}
            qld = {}
            qctr = [0]

            def quant_load(gi):
                if b != 0 or gi >= 64:
                    return
                wr = "sWR0" if gi % 2 == 0 else "sWR1"
                slot = gi % 2
                if gi >= 2:
                    w.wait("sync", *qtt_done[gi - 2])
                w.op("sync", lambda e, gi=gi, slot=slot: e.dma_start(
                    out=qw_ring[:, slot * 1024:(slot + 1) * 1024],
                    in_=wchunk_src(gi)), inc=wr, k=16)
                qld[gi] = (wr, w.v(wr))

            def quant_chunk(_ignored=None):
                if b != 0:
                    return
                gi = qctr[0]
                if gi >= 64:
                    return
                qctr[0] += 1
                quant_comp(gi)
                quant_load(gi + 2)

            def quant_comp(gi):
                so = "sSO0" if gi % 2 == 0 else "sSO1"
                gc = 0 if gi < 32 else 1
                slot = gi % 2
                if gi % 2 == 0:
                    w.wait("vector", *qld[gi])
                    w.wait("vector", "sV", w.v("sV"))
                    if gi >= 2:
                        w.wait("vector", *qtt_done[gi - 2])
                    w.op("vector", lambda e, slot=slot, gc=gc: e.tensor_scalar(
                        out=qbuf[:, 0:1024],
                        in0=qw_ring[:, slot * 1024:(slot + 1) * 1024],
                        scalar1=invg_col[:, gc:gc + 1], scalar2=0.5,
                        op0=ALU.mult, op1=ALU.is_ge), inc="sV")
                    w.wait("vector", "sV", w.v("sV"))
                    w.op("vector", lambda e, slot=slot, gc=gc: e.tensor_scalar(
                        out=qbuf[:, 1024:2048],
                        in0=qw_ring[:, slot * 1024:(slot + 1) * 1024],
                        scalar1=invg_col[:, 2 + gc:3 + gc], scalar2=0.5,
                        op0=ALU.mult, op1=ALU.is_ge), inc="sV")
                    w.wait("vector", "sV", w.v("sV"))
                    qts_done = w.v("sV")
                    w.wait("gpsimd", "sV", qts_done)
                    op = ALU.subtract
                    src0, src1 = qbuf[:, 0:1024], qbuf[:, 1024:2048]
                else:
                    w.wait("scalar", *qld[gi])
                    w.wait("scalar", "sS", w.v("sS"))
                    if gi >= 2:
                        w.wait("scalar", "sV", qm_done.get(gi - 2, 0))
                    w.op("scalar", lambda e, slot=slot, gc=gc: e.activation(
                        out=qsA[:],
                        in_=qw_ring[:, slot * 1024:(slot + 1) * 1024],
                        func=AF.Abs, scale=invg_col[:, gc:gc + 1]), inc="sS")
                    qa_done = w.v("sS")
                    if gi >= 2:
                        w.wait("scalar", *qtt_done[gi - 2])
                    w.op("scalar", lambda e, slot=slot: e.activation(
                        out=qbufB[:, 1024:2048],
                        in_=qw_ring[:, slot * 1024:(slot + 1) * 1024],
                        func=AF.Sign), inc="sS")
                    qsg_done = w.v("sS")
                    w.wait("vector", "sS", qa_done)
                    if gi >= 2:
                        w.wait("vector", *qtt_done[gi - 2])
                    w.op("vector", lambda e: e.tensor_scalar(
                        out=qbufB[:, 0:1024], in0=qsA[:],
                        scalar1=0.5, scalar2=None, op0=ALU.is_ge), inc="sV")
                    qm_done[gi] = w.v("sV")
                    w.wait("gpsimd", "sV", qm_done[gi])
                    w.wait("gpsimd", "sS", qsg_done)
                    op = ALU.mult
                    src0, src1 = qbufB[:, 0:1024], qbufB[:, 1024:2048]
                if gi >= 2:
                    w.wait("gpsimd", so, qst_done[gi - 2])
                w.op("gpsimd", lambda e, slot=slot, op=op, src0=src0, src1=src1:
                     e.tensor_tensor(
                         out=qbuf[:, (2 + slot) * 1024:(3 + slot) * 1024],
                         in0=src0, in1=src1, op=op), inc="sP")
                qtt_done[gi] = ("sP", w.v("sP"))
                w.wait("gpsimd", "sP", w.v("sP"))
                w.op("gpsimd", lambda e, gi=gi, slot=slot: e.dma_start(
                    out=wchunk_dst(gi),
                    in_=qbuf[:, (2 + slot) * 1024:(3 + slot) * 1024]), inc=so, k=16)
                qst_done[gi] = w.v(so)
            w.op("gpsimd", lambda e: e.dma_start(
                out=p1T_sb[:].rearrange("p (a j) -> p a j", j=C),
                in_=p1wT_d.rearrange("(a p) j -> p a j", p=128)), inc="sDW", k=16)
            w.op("gpsimd", lambda e: e.dma_start(
                out=p2T_sb[:].rearrange("p (a j) -> p a j", j=C),
                in_=p2wT_d.rearrange("(a p) j -> p a j", p=128)), inc="sDW", k=16)
            wload = w.v("sDW")
            quant_load(0)
            quant_load(1)

            att2_done = {}
            gelu_done = {}
            p2mm_done = {}
            p1_last = {}
            ao_done = {}
            zs_done = {}
            y1tt_done = {}
            y1st_done = {}
            lnb_done = {}
            chain1_done = {}
            xload_v = {}

            def xld(tc):
                x2slot = tc % 2
                if tc >= 2:
                    w.wait("sync", "sS", lnb_done[tc - 2])
                    w.wait("sync", "sV", y1tt_done[(tc - 1) * 4 - 1])
                xsem = "sXL" if x2slot == 0 else "sXL1"
                w.op("sync", lambda e, tc=tc, x2slot=x2slot: e.dma_start(
                    out=xring2[:, x2slot * 4096:(x2slot + 1) * 4096].rearrange(
                        "p (s d) -> p s d", d=C),
                    in_=x_d[b, tc * 512:(tc + 1) * 512, :].rearrange(
                        "(s p) d -> p s d", p=128)), inc=xsem, k=16)
                xload_v[tc] = (xsem, w.v(xsem))

            def attf(tc):
                for p in range(8):
                    if p >= 2:
                        w.wait("tensor", "sV", att2_done[p - 2])
                    elif tc >= 1:
                        w.wait("tensor", "sV", att2_done[6 + p])
                    w.op("tensor", lambda e, p=p, tc=tc: e.matmul(
                        psATT[:, (p % 2) * 512:(p % 2 + 1) * 512],
                        kvbd[:, p * 128:(p + 1) * 128],
                        q_fm[:, p * T + tc * 512:p * T + (tc + 1) * 512],
                        start=True, stop=True), inc="sT")
                    w.wait("vector", "sT", w.v("sT"))
                    if tc >= 1 and p == 0:
                        w.wait("vector", "sT", p1_last[tc - 1])
                    w.op("vector", lambda e, p=p, tc=tc: e.tensor_tensor(
                        out=att2[:, p * 512:(p + 1) * 512],
                        in0=psATT[:, (p % 2) * 512:(p % 2 + 1) * 512],
                        in1=ln1_fm[:, p * T + tc * 512:p * T + (tc + 1) * 512],
                        op=ALU.add), inc="sV")
                    att2_done[p] = w.v("sV")

            def p1f(tc):
                for jc in range(8):
                    qslot = jc % 2
                    w.wait("tensor", "sV", att2_done[7])
                    if tc == 0 and jc == 0:
                        w.wait("tensor", "sDW", wload)
                    if jc >= 2:
                        w.wait("tensor", "sS", gelu_done[jc - 2])
                    elif tc >= 1:
                        w.wait("tensor", "sS", gelu_done[6 + qslot])

                    def p1mm(e, jc=jc, qslot=qslot):
                        ii = None
                        for ci in range(8):
                            ii = e.matmul(
                                psP1[:, qslot * 512:(qslot + 1) * 512],
                                p1T_sb[:, ci * C + jc * 128:ci * C + (jc + 1) * 128],
                                att2[:, ci * 512:(ci + 1) * 512],
                                start=(ci == 0), stop=(ci == 7))
                        return ii
                    w.op("tensor", p1mm, inc="sT")
                    p1_last[tc] = w.v("sT")
                    w.wait("scalar", "sT", w.v("sT"))
                    if tc >= 1 and jc == 0:
                        w.wait("scalar", "sT", p2mm_done[tc * 4 - 1])
                    w.op("scalar", lambda e, jc=jc, qslot=qslot: e.activation(
                        out=hring[:, jc * 512:(jc + 1) * 512],
                        in_=psP1[:, qslot * 512:(qslot + 1) * 512],
                        func=AF.Gelu, bias=p1b_cols[:, jc:jc + 1]), inc="sS")
                    gelu_done[jc] = w.v("sS")

            def body(tc):
                x2slot = tc % 2
                xsem, xload = xload_v[tc]
                for s4 in range(4):
                    s = tc * 4 + s4
                    pslot = s4 % 2
                    lslot = s4 % 2
                    w.wait("scalar", xsem, xload)
                    if s >= 2:
                        w.wait("scalar", "sV", ao_done.get(s - 2, 0))
                    if s >= 1:
                        w.wait("scalar", "sV", ao_done.get(s - 1, 0))
                    w.op("scalar", lambda e, s=s, s4=s4,
                         x2slot=x2slot: e.activation(
                        out=ln1tm2[:],
                        in_=xring2[:, x2slot * 4096 + s4 * 1024:
                                   x2slot * 4096 + (s4 + 1) * 1024],
                        func=AF.Identity, scale=r1_sb[:, s:s + 1],
                        bias=nmr1_sb[:, s:s + 1]), inc="sS")
                    lnb_done[tc] = w.v("sS")

                    w.wait("tensor", "sS", gelu_done[7])
                    if s4 >= 2:
                        w.wait("tensor", "sV", ao_done[s - 2])

                    def p2mm(e, s4=s4, pslot=pslot):
                        ii = None
                        for jh in range(2):
                            reg = psP2[:, pslot * 1024 + jh * 512:
                                       pslot * 1024 + (jh + 1) * 512]
                            e.matmul(reg, ones_row[:],
                                     p2b_row[0:1, jh * 512:(jh + 1) * 512],
                                     start=True, stop=False)
                            for ci in range(8):
                                ii = e.matmul(
                                    reg,
                                    hring[:, ci * 512 + s4 * 128:
                                          ci * 512 + s4 * 128 + 128],
                                    p2T_sb[:, ci * C + jh * 512:
                                           ci * C + (jh + 1) * 512],
                                    start=False, stop=(ci == 7))
                        return ii
                    w.op("tensor", p2mm, inc="sT")
                    p2mm_done[s] = w.v("sT")
                    w.wait("vector", "sT", p2mm_done[s])
                    w.wait("vector", "sS", lnb_done[tc])
                    if s >= 4:
                        w.wait("vector", "sS", zs_done[s - 4])
                    w.op("vector", lambda e, s4=s4, pslot=pslot:
                         e.tensor_tensor(
                             out=aoring[:, s4 * 1024:(s4 + 1) * 1024],
                             in0=psP2[:, pslot * 1024:(pslot + 1) * 1024],
                             in1=ln1tm2[:],
                             op=ALU.add), inc="sV")
                    ao_done[s] = w.v("sV")
                    quant_chunk()
                    quant_chunk()
                    w.wait("scalar", "sV", ao_done[s])
                    if tc >= 1 and s4 == 0:
                        w.wait("scalar", "sV", chain1_done[tc - 1])
                    w.wait("scalar", "sS", w.v("sS"))
                    w.op("scalar", lambda e, s4=s4: e.activation(
                        out=junk2[:], in_=aoring[:, s4 * 1024:(s4 + 1) * 1024],
                        func=AF.Square, accum_out=ssqc[:, s4:s4 + 1]), inc="sS")
                    w.wait("vector", "sV", w.v("sV"))
                    w.op("vector", lambda e, s4=s4: e.tensor_reduce(
                        out=sxc[:, s4:s4 + 1],
                        in_=aoring[:, s4 * 1024:(s4 + 1) * 1024],
                        axis=AX.X, op=ALU.add), inc="sV")

                w.wait("vector", "sS", w.v("sS"))
                w.wait("vector", "sV", w.v("sV"))

                w.vops("vector", [
                    lambda e: e.tensor_scalar(out=negm[:], in0=sxc[:],
                                              scalar1=-1.0 / C, scalar2=None,
                                              op0=ALU.mult),
                    lambda e: e.tensor_tensor(out=msq[:], in0=negm[:],
                                              in1=negm[:], op=ALU.mult),
                    lambda e: e.scalar_tensor_tensor(
                        out=vvar[:], in0=ssqc[:], scalar=1.0 / C, in1=msq[:],
                        op0=ALU.mult, op1=ALU.subtract)])
                chain1_done[tc] = w.v("sV")
                w.wait("scalar", "sV", w.v("sV"))
                w.op("scalar", lambda e: e.activation(
                    out=srt[:], in_=vvar[:], func=AF.Sqrt, bias=ceps[:]), inc="sS")
                w.wait("vector", "sS", w.v("sS"))

                w.vops("vector", [
                    lambda e: e.reciprocal(rrc[:], srt[:]),
                    lambda e: e.tensor_tensor(out=nmrc[:], in0=negm[:],
                                              in1=rrc[:], op=ALU.mult)])
                stats_done = w.v("sV")

                for s4 in range(4):
                    s = tc * 4 + s4
                    zslot = s4 % 2
                    w.wait("scalar", "sV", stats_done)
                    if s >= 2:
                        w.wait("scalar", "sV", y1tt_done.get(s - 2, 0))
                    if s >= 1:
                        w.wait("scalar", "sV", y1tt_done.get(s - 1, 0))
                    w.op("scalar", lambda e, s4=s4: e.activation(
                        out=zring[:],
                        in_=aoring[:, s4 * 1024:(s4 + 1) * 1024],
                        func=AF.Identity, scale=rrc[:, s4:s4 + 1],
                        bias=nmrc[:, s4:s4 + 1]), inc="sS")
                    zs_done[s] = w.v("sS")
                    sy = "sY0" if zslot == 0 else "sY1"
                    w.wait("vector", xsem, xload)
                    w.wait("vector", "sS", zs_done[s])
                    if s >= 2:
                        w.wait("vector", sy, y1st_done.get(s - 2, 0))
                    w.op("vector", lambda e, s4=s4, zslot=zslot,
                         x2slot=x2slot: e.tensor_tensor(
                        out=y1ring[:, zslot * 1024:(zslot + 1) * 1024],
                        in0=xring2[:, x2slot * 4096 + s4 * 1024:
                                   x2slot * 4096 + (s4 + 1) * 1024],
                        in1=zring[:],
                        op=ALU.add), inc="sV")
                    y1tt_done[s] = w.v("sV")
                    w.wait("gpsimd", "sV", y1tt_done[s])
                    w.op("gpsimd", lambda e, s=s, zslot=zslot: e.dma_start(
                        out=y1s_d[b, s * 128:(s + 1) * 128, :],
                        in_=y1ring[:, zslot * 1024:(zslot + 1) * 1024]),
                        inc=sy, k=16)
                    y1st_done[s] = w.v(sy)
                    quant_chunk()
                    quant_chunk()

            xld(0)
            attf(0)
            p1f(0)
            for tc in range(NTC):
                if tc + 1 < NTC:
                    xld(tc + 1)
                    attf(tc + 1)
                body(tc)
                if tc + 1 < NTC:
                    p1f(tc + 1)

            w.wait("gpsimd", "sY0", w.v("sY0"))
            w.wait("gpsimd", "sY1", w.v("sY1"))
            w.wait("gpsimd", "sSO0", w.v("sSO0"))
            w.wait("gpsimd", "sSO1", w.v("sSO1"))
            if debug:
                w.op("gpsimd", lambda e: e.dma_start(out=dbg["d_att2"],
                                                     in_=att2[:]), inc="sDO", k=16)
                w.wait("gpsimd", "sDO", w.v("sDO"))

        if 'P2' in run_p:
            w.block(p2, f"p2_{b}")
        p2sc.close()
        absc.close()

        p3sc = ExitStack()
        wq1_sb = p3sc.enter_context(nc.sbuf_tensor(f"wq1{b}", [128, 8 * FF], F8))
        wq2_sb = p3sc.enter_context(nc.sbuf_tensor(f"wq2{b}", [128, 32 * C], F8))
        y1r3 = p3sc.enter_context(nc.sbuf_tensor(f"y1r3{b}", [128, 4 * 2048], F32))
        ln2tm = p3sc.enter_context(nc.sbuf_tensor(f"l2t{b}", [128, 2 * 1024], BF16))
        ln2fm = p3sc.enter_context(nc.sbuf_tensor(f"l2f{b}", [128, 2 * 2048], BF16))
        h1 = p3sc.enter_context(nc.sbuf_tensor(f"h1_{b}", [128, 2 * 32 * 256], BF16))
        ffnr = p3sc.enter_context(nc.sbuf_tensor(f"ffnr{b}", [128, 2 * 1024], F32))
        z3r = p3sc.enter_context(nc.sbuf_tensor(f"z3r{b}", [128, 1024], F32))
        outr = p3sc.enter_context(nc.sbuf_tensor(f"outr{b}", [128, 1024], F32))
        psF1 = p3sc.enter_context(nc.psum_tensor(f"psF1{b}", [128, 2048], F32))
        psF2 = p3sc.enter_context(nc.psum_tensor(f"psF2{b}", [128, 2048], F32))

        def p3(w, b=b, wq1_sb=wq1_sb, wq2_sb=wq2_sb, y1r3=y1r3, ln2tm=ln2tm,
               ln2fm=ln2fm, h1=h1, ffnr=ffnr, z3r=z3r, outr=outr, psF1=psF1,
               psF2=psF2):
            def wqloads():
                w.op("sync", lambda e: e.dma_start(
                    out=wq1_sb[:].rearrange("p (a f) -> p a f", f=FF),
                    in_=wq1s_d.rearrange("(a p) f -> p a f", p=128)),
                    inc="sWQ", k=16)
                wl = w.v("sWQ")
                w.op("sync", lambda e: e.dma_start(
                    out=wq2_sb[:].rearrange("p (a d) -> p a d", d=C),
                    in_=wq2s_d.rearrange("(a p) d -> p a d", p=128)),
                    inc="sGB", k=16)
                return wl, w.v("sGB")

            evac_done = {}
            tr_done3 = {}
            relu_done = {}
            f1_last = {}
            f2_done = {}
            ffn_done = {}
            outtt_done = {}
            outst_done = {}
            zs_done = {}
            apl_done = {}
            chain1_done = {}
            chain3_done = {}

            yload_map = {}

            def yld3(tc3):
                yslot = tc3 % 4
                ysem = ("sXL", "sXL1", "sXL2", "sXL3")[yslot]
                if tc3 >= 4:
                    w.wait("sync", "sV", outtt_done.get(2 * (tc3 - 4) + 1, 0))
                    w.wait("sync", "sV", apl_done.get(2 * (tc3 - 4) + 1, 0))
                w.op("sync", lambda e, tc3=tc3, yslot=yslot: e.dma_start(
                    out=y1r3[:, yslot * 2048:(yslot + 1) * 2048].rearrange(
                        "p (s d) -> p s d", d=C),
                    in_=y1s_d[b, tc3 * 256:(tc3 + 1) * 256, :].rearrange(
                        "(s p) d -> p s d", p=128)), inc=ysem, k=16)
                yload_map[tc3] = w.v(ysem)

            def p3front(tc3):
                slot = tc3 % 2
                s_hi = 2 * tc3 + 1
                yslot = tc3 % 4
                ysem = ("sXL", "sXL1", "sXL2", "sXL3")[yslot]
                yload = yload_map[tc3]

                w.wait("scalar", ysem, yload)
                if tc3 >= 1:
                    w.wait("scalar", "sV", chain1_done[tc3 - 1])
                w.wait("vector", ysem, yload)
                for s2 in range(2):
                    ysub = y1r3[:, yslot * 2048 + s2 * 1024:
                                yslot * 2048 + (s2 + 1) * 1024]
                    w.wait("scalar", "sS", w.v("sS"))
                    w.op("scalar", lambda e, ysub=ysub, s2=s2: e.activation(
                        out=junk2[:], in_=ysub, func=AF.Square,
                        accum_out=ssqc[:, s2:s2 + 1]), inc="sS")
                    w.op("vector", lambda e, ysub=ysub, s2=s2: e.tensor_reduce(
                        out=sxc[:, s2:s2 + 1], in_=ysub, axis=AX.X, op=ALU.add),
                        inc="sV" if s2 == 1 else None)
                w.wait("vector", "sS", w.v("sS"))
                w.wait("vector", "sV", w.v("sV"))


                w.vops("vector", [
                    lambda e: e.tensor_scalar(out=negm[:, 0:2], in0=sxc[:, 0:2],
                                              scalar1=-1.0 / C, scalar2=None,
                                              op0=ALU.mult),
                    lambda e: e.tensor_tensor(out=msq[:, 0:2], in0=negm[:, 0:2],
                                              in1=negm[:, 0:2], op=ALU.mult),
                    lambda e: e.scalar_tensor_tensor(
                        out=vvar[:, 0:2], in0=ssqc[:, 0:2], scalar=1.0 / C,
                        in1=msq[:, 0:2], op0=ALU.mult, op1=ALU.subtract)])
                chain1_done[tc3] = w.v("sV")
                w.wait("scalar", "sV", w.v("sV"))
                w.op("scalar", lambda e: e.activation(
                    out=srt[:, 0:2], in_=vvar[:, 0:2], func=AF.Sqrt,
                    bias=ceps[:]), inc="sS")
                w.wait("vector", "sS", w.v("sS"))

                w.vops("vector", [
                    lambda e: e.reciprocal(rrc[:, 0:2], srt[:, 0:2]),
                    lambda e: e.tensor_tensor(out=nmrc[:, 0:2], in0=negm[:, 0:2],
                                              in1=rrc[:, 0:2], op=ALU.mult)])
                stats_done = w.v("sV")

                for s2 in range(2):
                    s = tc3 * 2 + s2
                    lsl = s % 2
                    w.wait("vector", "sV", stats_done)
                    if s >= 2:
                        w.wait("vector", *tr_done3[s - 2])
                    w.op("vector", lambda e, s2=s2, yslot=yslot, lsl=lsl:
                         e.scalar_tensor_tensor(
                             out=ln2tm[:, lsl * 1024:(lsl + 1) * 1024],
                             in0=y1r3[:, yslot * 2048 + s2 * 1024:
                                      yslot * 2048 + (s2 + 1) * 1024],
                             scalar=rrc[:, s2:s2 + 1],
                             in1=nmrc[:, s2:s2 + 1].broadcast_to([128, 1024]),
                             op0=ALU.mult, op1=ALU.add), inc="sV")
                    apl_done[s] = w.v("sV")
                    if tc3 >= 2 and s2 == 0:
                        w.wait("scalar", "sT", f1_last.get(tc3 - 2, 0))
                    trs = "sTR0" if lsl == 0 else "sTR1"
                    w.wait("scalar", "sV", apl_done[s])
                    w.op("scalar", lambda e, s2=s2, slot=slot, lsl=lsl: e.dma_start(
                        out=AP(ln2fm, slot * 2048 + s2 * 128,
                               [[2 * 2048, 128], [256, 8], [1, 128]]),
                        in_=ln2tm[:, lsl * 1024:(lsl + 1) * 1024],
                        transpose=True), inc=trs, k=16)
                    tr_done3[s] = (trs, w.v(trs))

            relu_last = {}

            def f1blk(tc3):
                slot = tc3 % 2
                hsl = tc3 % 2
                s_hi = 2 * tc3 + 1
                for fc in (range(32) if p3sub in ('full', 'f1') else []):
                    fslot = fc % 4
                    if fc == 0:
                        w.wait("tensor", *tr_done3[s_hi - 1])
                        w.wait("tensor", *tr_done3[s_hi])
                    if tc3 == 0 and fc == 0:
                        w.wait("tensor", "sWQ", wload)
                    if fc >= 4:
                        w.wait("tensor", "sS", relu_done[fc - 4])
                    elif tc3 >= 1:
                        w.wait("tensor", "sS", relu_done.get(28 + fslot, 0))

                    def f1mm(e, fc=fc, fslot=fslot, slot=slot):
                        ii = None
                        for ci in range(8):
                            ii = e.matmul(
                                psF1[:, fslot * 512:fslot * 512 + 256],
                                wq1_sb[:, ci * FF + fc * 128:
                                       ci * FF + (fc + 1) * 128],
                                ln2fm[:, slot * 2048 + ci * 256:
                                      slot * 2048 + (ci + 1) * 256],
                                start=(ci == 0), stop=(ci == 7))
                        return ii
                    w.op("tensor", f1mm, inc="sT")
                    if fc == 31:
                        f1_last[tc3] = w.v("sT")
                    w.wait("scalar", "sT", w.v("sT"))
                    if tc3 >= 2 and fc == 0:
                        w.wait("scalar", "sT", f2_done.get(2 * (tc3 - 2) + 1, 0))
                    w.op("scalar", lambda e, fc=fc, fslot=fslot, hsl=hsl:
                         e.activation(
                             out=h1[:, hsl * 8192 + fc * 256:
                                    hsl * 8192 + (fc + 1) * 256],
                             in_=psF1[:, fslot * 512:fslot * 512 + 256],
                             func=AF.Relu, bias=b1_cols[:, fc:fc + 1]),
                        inc="sS")
                    relu_done[fc] = w.v("sS")
                relu_last[tc3] = relu_done.get(31, 0)

            def f2blk(tc3):
                slot = tc3 % 2
                hsl = tc3 % 2
                yslot = tc3 % 4
                ysem = ("sXL", "sXL1", "sXL2", "sXL3")[yslot]
                yload = yload_map[tc3]

                if p3sub in ('ln2', 'f1'):
                    for s2 in range(2):
                        s = tc3 * 2 + s2
                        if s >= 1:
                            w.wait("vector", "sOS", outst_done.get(s - 1, 0))
                        w.wait("vector", "sXL", yload)
                        w.op("vector", lambda e, s2=s2: e.tensor_copy(
                            out=outr[:], in_=y1r3[:, s2 * 1024:(s2 + 1) * 1024]),
                            inc="sV")
                        outtt_done[s] = w.v("sV")
                        apl_done.setdefault(s, 0)
                        w.wait("gpsimd", "sV", outtt_done[s])
                        w.op("gpsimd", lambda e, s=s: e.dma_start(
                            out=out_d[b, s * 128:(s + 1) * 128, :], in_=outr[:]),
                            inc="sOS", k=16)
                        outst_done[s] = w.v("sOS")
                    return
                for s2 in range(2):
                    s = tc3 * 2 + s2
                    pslot = s2
                    w.wait("tensor", "sS", relu_last.get(tc3, 0))
                    if tc3 == 0 and s2 == 0:
                        w.wait("tensor", "sGB", wload2)
                    if s >= 2:
                        w.wait("tensor", "sV", ffn_done.get(s - 2, 0))

                    def f2mm(e, s2=s2, pslot=pslot, hsl=hsl):
                        ii = None
                        for jh in range(2):
                            reg = psF2[:, pslot * 1024 + jh * 512:
                                       pslot * 1024 + (jh + 1) * 512]
                            for fc in range(32):
                                ii = e.matmul(
                                    reg,
                                    h1[:, hsl * 8192 + fc * 256 + s2 * 128:
                                       hsl * 8192 + fc * 256 + s2 * 128 + 128],
                                    wq2_sb[:, fc * C + jh * 512:
                                           fc * C + (jh + 1) * 512],
                                    start=(fc == 0), stop=(fc == 31))
                        return ii
                    w.op("tensor", f2mm, inc="sT")
                    f2_done[s] = w.v("sT")
                    w.wait("vector", "sT", f2_done[s])
                    w.op("vector", lambda e, s2=s2, pslot=pslot:
                         e.tensor_tensor(
                             out=ffnr[:, s2 * 1024:(s2 + 1) * 1024],
                             in0=psF2[:, pslot * 1024:(pslot + 1) * 1024],
                             in1=B2tile[:], op=ALU.add), inc="sV")
                    ffn_done[s] = w.v("sV")
                    w.wait("scalar", "sV", ffn_done[s])
                    if tc3 >= 1 and s2 == 0:
                        w.wait("scalar", "sV", chain3_done[tc3 - 1])
                    w.wait("scalar", "sS", w.v("sS"))
                    w.op("scalar", lambda e, s2=s2: e.activation(
                        out=junk2[:], in_=ffnr[:, s2 * 1024:(s2 + 1) * 1024],
                        func=AF.Square, accum_out=ssqc[:, 2 + s2:3 + s2]),
                        inc="sS")
                    w.wait("vector", "sS", w.v("sS"))
                    w.wait("vector", "sV", w.v("sV"))
                    w.op("vector", lambda e, s2=s2: e.tensor_reduce(
                        out=sxc[:, 2 + s2:3 + s2],
                        in_=ffnr[:, s2 * 1024:(s2 + 1) * 1024],
                        axis=AX.X, op=ALU.add), inc="sV")

                w.wait("vector", "sS", w.v("sS"))
                w.wait("vector", "sV", w.v("sV"))


                w.vops("vector", [
                    lambda e: e.tensor_scalar(out=negm[:, 2:4], in0=sxc[:, 2:4],
                                              scalar1=-1.0 / C, scalar2=None,
                                              op0=ALU.mult),
                    lambda e: e.tensor_tensor(out=msq[:, 2:4], in0=negm[:, 2:4],
                                              in1=negm[:, 2:4], op=ALU.mult),
                    lambda e: e.scalar_tensor_tensor(
                        out=vvar[:, 2:4], in0=ssqc[:, 2:4], scalar=1.0 / C,
                        in1=msq[:, 2:4], op0=ALU.mult, op1=ALU.subtract)])
                chain3_done[tc3] = w.v("sV")
                w.wait("scalar", "sV", w.v("sV"))
                w.op("scalar", lambda e: e.activation(
                    out=srt[:, 2:4], in_=vvar[:, 2:4], func=AF.Sqrt,
                    bias=ceps[:]), inc="sS")
                w.wait("vector", "sS", w.v("sS"))

                w.vops("vector", [
                    lambda e: e.reciprocal(rrc[:, 2:4], srt[:, 2:4]),
                    lambda e: e.tensor_tensor(out=nmrc[:, 2:4], in0=negm[:, 2:4],
                                              in1=rrc[:, 2:4], op=ALU.mult)])
                stats3 = w.v("sV")

                for s2 in range(2):
                    s = tc3 * 2 + s2
                    w.wait("vector", "sV", stats3)
                    if s >= 1:
                        w.wait("vector", "sV", outtt_done.get(s - 1, 0))
                    w.op("vector", lambda e, s2=s2: e.scalar_tensor_tensor(
                        out=z3r[:], in0=ffnr[:, s2 * 1024:(s2 + 1) * 1024],
                        scalar=rrc[:, 2 + s2:3 + s2],
                        in1=nmrc[:, 2 + s2:3 + s2].broadcast_to([128, 1024]),
                        op0=ALU.mult, op1=ALU.add), inc="sV")
                    zs_done[s] = w.v("sV")
                    w.wait("vector", "sV", zs_done[s])
                    w.wait("vector", ysem, yload)
                    if s >= 1:
                        w.wait("vector", "sOS", outst_done.get(s - 1, 0))
                    w.op("vector", lambda e, s2=s2, yslot=yslot: e.tensor_tensor(
                        out=outr[:],
                        in0=y1r3[:, yslot * 2048 + s2 * 1024:
                                 yslot * 2048 + (s2 + 1) * 1024],
                        in1=z3r[:], op=ALU.add), inc="sV")
                    outtt_done[s] = w.v("sV")
                    w.wait("gpsimd", "sV", outtt_done[s])
                    w.op("gpsimd", lambda e, s=s: e.dma_start(
                        out=out_d[b, s * 128:(s + 1) * 128, :], in_=outr[:]),
                        inc="sOS", k=16)
                    outst_done[s] = w.v("sOS")

            yld3(0)
            wload, wload2 = wqloads()
            yld3(1)
            yld3(2)
            p3front(0)
            f1blk(0)
            p3front(1)
            for tc3 in range(NTC3):
                if tc3 + 3 < NTC3:
                    yld3(tc3 + 3)
                if tc3 + 1 < NTC3:
                    f1blk(tc3 + 1)
                if tc3 + 2 < NTC3:
                    p3front(tc3 + 2)
                f2blk(tc3)

            w.wait("gpsimd", "sOS", w.v("sOS"))
            w.wait("gpsimd", "sSO0", w.v("sSO0"))
            w.wait("gpsimd", "sSO1", w.v("sSO1"))

        if 'P3' in run_p:
            w.block(p3, f"p3_{b}")
        p3sc.close()

    es.close()
    return nc


def host_inputs(inputs, core):
    x = np.ascontiguousarray(np.asarray(inputs["L0_attn_x"],
                                        np.float32)[2 * core:2 * core + 2])
    qkv_b = np.asarray(inputs["L0_attn_qkv_b"], np.float32)
    d = {
        "x": x,
        "qkvwT": np.ascontiguousarray(
            np.asarray(inputs["L0_attn_qkv_w"], np.float32).T),
        "qb_cols": np.ascontiguousarray(qkv_b[:C].reshape(8, 128).T),
        "kvb_row": np.ascontiguousarray(qkv_b[C:].reshape(1, 2 * C)),
        "p1wT": np.ascontiguousarray(np.asarray(inputs["L0_attn_p1_w"], np.float32).T),
        "p1b_cols": np.ascontiguousarray(
            np.asarray(inputs["L0_attn_p1_b"], np.float32).reshape(8, 128).T),
        "p2wT": np.ascontiguousarray(np.asarray(inputs["L0_attn_p2_w"], np.float32).T),
        "p2b_row": np.ascontiguousarray(
            np.asarray(inputs["L0_attn_p2_b"], np.float32).reshape(1, C)),
        "w1T": np.ascontiguousarray(np.asarray(inputs["L0_ffn_W1"], np.float32).T),
        "b1_cols": np.ascontiguousarray(
            np.asarray(inputs["L0_ffn_b1"], np.float32).reshape(32, 128).T),
        "w2T": np.ascontiguousarray(np.asarray(inputs["L0_ffn_W2"], np.float32).T),
        "b2_row": np.ascontiguousarray(
            np.asarray(inputs["L0_ffn_b2"], np.float32).reshape(1, C)),
        "g64": np.ascontiguousarray(np.concatenate([
            np.asarray(inputs["L0_attn_kn_g"], np.float32),
            np.asarray(inputs["L0_attn_vn_g"], np.float32),
            np.asarray(inputs["L0_attn_kn_b"], np.float32),
            np.asarray(inputs["L0_attn_vn_b"], np.float32)]).reshape(1, 256)),
        "tempr": np.ascontiguousarray(
            np.asarray(inputs["L0_attn_temperature"], np.float32).reshape(16, 1)),
    }
    ab = np.asarray(inputs["L0_attn_attn_bias"], np.float32).reshape(8, 2, HD, HD)
    abias = np.concatenate([ab[:, 0], ab[:, 1]], axis=1)
    d["abias"] = np.ascontiguousarray(abias.transpose(1, 0, 2).reshape(128, 512))
    return d


_NC_CACHE = {}


def kernel(**inputs):
    if "prog" not in _NC_CACHE:
        _NC_CACHE["prog"] = build_program(debug=False)
    nc = _NC_CACHE["prog"]
    in_maps = [host_inputs(inputs, core) for core in range(NCORES)]
    res = run_bass_kernel_spmd(nc, in_maps, core_ids=list(range(NCORES)))
    out = np.empty((16, T, C), np.float32)
    for core in range(NCORES):
        out[2 * core:2 * core + 2] = res.results[core]["out"]
    return out


# version 31
# speedup vs baseline: 1.1738x; 1.0011x over previous
import numpy as np
from contextlib import ExitStack

import concourse.bass as bass
import concourse.mybir as mybir
from concourse.bass_utils import run_bass_kernel_spmd

F32 = mybir.dt.float32
BF16 = mybir.dt.bfloat16
I32 = mybir.dt.int32
F8 = mybir.dt.float8e4
AF = mybir.ActivationFunctionType
ALU = mybir.AluOpType
AX = mybir.AxisListType
AP = bass.AP

NCORES = 8
NB = 2
T = 2048
C = 1024
H = 16
HD = 64
FF = 4096
NSUB = 16
NTC = 4
NTC3 = 8
EPS = 1e-5
WN = float(FF * C)


class Walk:

    def __init__(self, nc, semnames):
        self.nc = nc
        self.sems = {}
        self.cnt = {n: 0 for n in semnames}
        self.names = list(semnames)
        self.cur_name = None
        self.cur_eng = None

    def op(self, eng, fn, inc=None, k=1):
        if self.cur_name == eng:
            instr = fn(self.cur_eng)
            if inc is not None:
                instr.then_inc(self.sems[inc], k)
        if inc is not None:
            self.cnt[inc] += k

    def wait(self, eng, sem, val):
        if val <= 0:
            return
        if self.cur_name == eng:
            self.cur_eng.wait_ge(self.sems[sem], val)

    def v(self, sem):
        return self.cnt[sem]

    def vops(self, eng, fns, sem="sV"):
        for fn in fns:
            self.op(eng, fn, inc=sem)
            self.wait(eng, sem, self.v(sem))

    def block(self, fn, name=None):
        nc = self.nc
        base = dict(self.cnt)
        final = {}

        def mk(engname):
            def body(e):
                self.cnt = dict(base)
                self.cur_name = engname
                self.cur_eng = e
                fn(self)
                final.update(self.cnt)
            return body

        with nc.Block(name) as blk:
            blk.sync(mk("sync"))
            blk.scalar(mk("scalar"))
            blk.vector(mk("vector"))
            blk.tensor(mk("tensor"))
            blk.gpsimd(mk("gpsimd"))
        self.cnt = dict(final)


def build_program(debug=False, phases=('S','Q','P1','P2','P3'), p3sub='full'):
    nc = bass.Bass("TRN2", target_bir_lowering=False)

    x_d = nc.dram_tensor("x", [NB, T, C], F32, kind="ExternalInput").ap()
    qkvwT_d = nc.dram_tensor("qkvwT", [C, 3 * C], F32, kind="ExternalInput").ap()
    qbcols_d = nc.dram_tensor("qb_cols", [128, 8], F32, kind="ExternalInput").ap()
    kvbrow_d = nc.dram_tensor("kvb_row", [1, 2 * C], F32, kind="ExternalInput").ap()
    p1wT_d = nc.dram_tensor("p1wT", [C, C], F32, kind="ExternalInput").ap()
    p1bcols_d = nc.dram_tensor("p1b_cols", [128, 8], F32, kind="ExternalInput").ap()
    p2wT_d = nc.dram_tensor("p2wT", [C, C], F32, kind="ExternalInput").ap()
    p2brow_d = nc.dram_tensor("p2b_row", [1, C], F32, kind="ExternalInput").ap()
    w1T_d = nc.dram_tensor("w1T", [C, FF], F32, kind="ExternalInput").ap()
    b1cols_d = nc.dram_tensor("b1_cols", [128, 32], F32, kind="ExternalInput").ap()
    w2T_d = nc.dram_tensor("w2T", [FF, C], F32, kind="ExternalInput").ap()
    b2row_d = nc.dram_tensor("b2_row", [1, C], F32, kind="ExternalInput").ap()
    g64_d = nc.dram_tensor("g64", [1, 256], F32, kind="ExternalInput").ap()
    temp_d = nc.dram_tensor("tempr", [16, 1], F32, kind="ExternalInput").ap()
    abias_d = nc.dram_tensor("abias", [128, 512], F32, kind="ExternalInput").ap()
    out_d = nc.dram_tensor("out", [NB, T, C], F32, kind="ExternalOutput").ap()
    y1s_d = nc.dram_tensor("y1s", [NB, T, C], F32, kind="Internal").ap()
    wq1s_d = nc.dram_tensor("wq1s", [C, FF], F8, kind="Internal").ap()
    wq2s_d = nc.dram_tensor("wq2s", [FF, C], F8, kind="Internal").ap()
    gbscr_d = nc.dram_tensor("gbscr", [2, 2 * C], F32, kind="Internal").ap()
    smscr_d = nc.dram_tensor("smscr", [16, 1], F32, kind="Internal").ap()

    dbg = {}
    if debug:
        for nm, shp, dt in [("d_ln1fm", [128, 8 * T], F32),
                            ("d_q", [128, 8 * T], F32),
                            ("d_khat", [128, 3 * 2048], F32),
                            ("d_kvbd", [128, 1024], F32),
                            ("d_att2", [128, 8 * 512], F32),
                            ("d_gb", [128, 2 * 2048], F32),
                            ("d_invg", [128, 4], F32),
                            ("d_red16", [128, 16], F32),
                            ("d_g11", [1, 16], F32)]:
            dbg[nm] = nc.dram_tensor(nm, shp, dt, kind="ExternalOutput").ap()

    es = ExitStack()

    def sb(name, shape, dt):
        return es.enter_context(nc.sbuf_tensor(name, shape, dt))

    ident = sb("ident", [128, 128], BF16)
    colidx = sb("colidx", [128, 128], I32)
    rowidx = sb("rowidx", [128, 1], F32)
    ones_row = sb("ones_row", [1, 128], BF16)
    onescol_f = sb("onescol_f", [128, 1], F32)
    chalf = sb("chalf", [128, 1], F32)
    cnhalf = sb("cnhalf", [128, 1], F32)
    ceps = sb("ceps", [128, 1], F32)
    ceps2 = sb("ceps2", [128, 1], F32)
    qb_cols = sb("qb_cols_s", [128, 8], F32)
    p1b_cols = sb("p1b_cols_s", [128, 8], F32)
    b1_cols = sb("b1_cols_s", [128, 32], F32)
    kvb_row = sb("kvb_row_s", [1, 2 * C], BF16)
    p2b_row = sb("p2b_row_s", [1, C], BF16)
    B2tile = sb("B2tile", [128, C], F32)
    abias_sb = sb("abias_sb", [128, 512], F32)
    tempQ = sb("tempQ", [128, 8], F32)
    kvbd = sb("kvbd", [128, 8 * 128], BF16)
    invg_col = sb("invg_col", [128, 4], F32)
    r1_sb = sb("r1_sb", [128, NSUB], F32)
    nmr1_sb = sb("nmr1_sb", [128, NSUB], F32)
    sxc = sb("sxc", [128, 4], F32)
    ssqc = sb("ssqc", [128, 4], F32)
    negm = sb("negm", [128, 4], F32)
    msq = sb("msq", [128, 4], F32)
    vvar = sb("vvar", [128, 4], F32)
    srt = sb("srt", [128, 4], F32)
    rrc = sb("rrc", [128, 4], F32)
    nmrc = sb("nmrc", [128, 4], F32)
    khsum = sb("khsum", [128, 32], F32)
    khsq = sb("khsq", [128, 32], F32)
    khneg = sb("khneg", [128, 32], F32)
    khmsq = sb("khmsq", [128, 32], F32)
    khv = sb("khv", [128, 32], F32)
    khsrt = sb("khsrt", [128, 32], F32)
    khr = sb("khr", [128, 32], F32)
    junk2 = sb("junk2", [128, 1024], F32)

    qsc = ExitStack()
    g64 = qsc.enter_context(nc.sbuf_tensor("g64_s", [1, 256], F32))
    g_row = qsc.enter_context(nc.sbuf_tensor("g_row", [1, 2 * C], F32))
    b_row = qsc.enter_context(nc.sbuf_tensor("b_row", [1, 2 * C], F32))
    t16 = qsc.enter_context(nc.sbuf_tensor("t16", [16, 1], F32))
    red16 = qsc.enter_context(nc.sbuf_tensor("red16", [128, 16], F32))
    g11 = qsc.enter_context(nc.sbuf_tensor("g11", [1, 16], F32))

    semnames = ["sDX", "sDW", "sDO", "sT", "sV", "sS", "sP",
                "sXL", "sGB", "sTQ", "sWR0", "sWR1", "sIG", "sSO0", "sSO1",
                "sY0", "sY1", "sOS", "sWQ", "sGM", "sXL1", "sXL2", "sXL3",
                "sTR0", "sTR1"]
    w = Walk(nc, semnames)
    for n in semnames:
        w.sems[n] = es.enter_context(nc.semaphore(n))

    def startup(w):
        for dst, src in [(qb_cols[:], qbcols_d), (p1b_cols[:], p1bcols_d),
                         (b1_cols[:], b1cols_d), (abias_sb[:], abias_d),
                         (t16[:], temp_d), (g64[:], g64_d),
                         (B2tile[:], AP(b2row_d.tensor, 0, [[0, 128], [1, C]]))]:
            w.op("sync", lambda e, dst=dst, src=src: e.dma_start(out=dst, in_=src),
                 inc="sDX", k=16)
        w.op("gpsimd", lambda e: e.dma_start(out=kvb_row[:], in_=kvbrow_d),
             inc="sDW", k=16)
        w.op("gpsimd", lambda e: e.dma_start(out=p2b_row[:], in_=p2brow_d),
             inc="sDW", k=16)
        w.op("gpsimd", lambda e: e.iota(colidx[:], [[1, 128]], base=0,
                                        channel_multiplier=0), inc="sP")
        w.op("gpsimd", lambda e: e.iota(rowidx[:], [[1, 1]], base=0,
                                        channel_multiplier=1,
                                        allow_small_or_imprecise_dtypes=True),
             inc="sP")

        def msets(e):
            e.memset(ones_row[:], 1.0)
            e.memset(onescol_f[:], 1.0)
            e.memset(chalf[:], 0.5)
            e.memset(cnhalf[:], -0.5)
            e.memset(ceps[:], EPS)
            e.memset(ceps2[:], EPS * HD * HD)
            return e.memset(kvbd[:], 0.0)
        w.op("vector", msets, inc="sV")

        w.wait("vector", "sP", w.v("sP"))
        w.op("vector", lambda e: e.tensor_scalar(
            out=ident[:], in0=colidx[:], scalar1=rowidx[:], scalar2=None,
            op0=ALU.is_equal), inc="sV")

        w.wait("vector", "sDX", w.v("sDX"))

        def tile_gb(e):
            for half in range(2):
                e.tensor_copy(
                    out=g_row[0:1, half * C:(half + 1) * C].rearrange(
                        "p (a d) -> p a d", d=HD),
                    in_=g64[0:1, half * HD:(half + 1) * HD].unsqueeze(1)
                    .broadcast_to([1, H, HD]))
            for half in range(2):
                e.tensor_copy(
                    out=b_row[0:1, half * C:(half + 1) * C].rearrange(
                        "p (a d) -> p a d", d=HD),
                    in_=g64[0:1, 128 + half * HD: 128 + (half + 1) * HD]
                    .unsqueeze(1).broadcast_to([1, H, HD]))
            return e.tensor_scalar(out=t16[:], in0=t16[:], scalar1=1.0 / T,
                                   scalar2=None, op0=ALU.mult)
        w.op("vector", tile_gb, inc="sV")
        w.wait("gpsimd", "sV", w.v("sV"))
        w.op("gpsimd", lambda e: e.dma_start(out=gbscr_d[0:1, :], in_=g_row[:]),
             inc="sDO", k=16)
        w.op("gpsimd", lambda e: e.dma_start(out=gbscr_d[1:2, :], in_=b_row[:]),
             inc="sDO", k=16)
        w.op("gpsimd", lambda e: e.dma_start(out=smscr_d, in_=t16[:]),
             inc="sDO", k=16)
        w.wait("sync", "sDO", w.v("sDO"))
        with nc.allow_non_contiguous_dma(reason="tiny one-time tempQ broadcast"):
            w.op("sync", lambda e: e.dma_start(
                out=tempQ[0:64, :], in_=AP(smscr_d.tensor, 0, [[0, 64], [2, 8]])),
                inc="sTQ", k=16)
            w.op("sync", lambda e: e.dma_start(
                out=tempQ[64:128, :], in_=AP(smscr_d.tensor, 1, [[0, 64], [2, 8]])),
                inc="sTQ", k=16)
        w.wait("sync", "sTQ", w.v("sTQ"))
        w.wait("gpsimd", "sDW", w.v("sDW"))

    w.block(startup, "startup")
    run_p = phases

    qsc.close()

    def wchunk_src(gi):
        if gi < 32:
            return w1T_d[(gi // 4) * 128:(gi // 4 + 1) * 128,
                         (gi % 4) * 1024:(gi % 4 + 1) * 1024]
        r = gi - 32
        return w2T_d[r * 128:(r + 1) * 128, :]

    def wchunk_dst(gi):
        if gi < 32:
            return wq1s_d[(gi // 4) * 128:(gi // 4 + 1) * 128,
                          (gi % 4) * 1024:(gi % 4 + 1) * 1024]
        r = gi - 32
        return wq2s_d[r * 128:(r + 1) * 128, :]

    for b in (range(NB) if 'P1' in run_p else []):
        absc = ExitStack()
        ln1_fm = absc.enter_context(nc.sbuf_tensor(f"ln1fm{b}", [128, 8 * T], BF16))
        q_fm = absc.enter_context(nc.sbuf_tensor(f"qfm{b}", [128, 8 * T], BF16))
        p1sc = ExitStack()
        G64 = p1sc.enter_context(nc.sbuf_tensor(f"G64_{b}", [128, 2 * HD], F32))
        B64 = p1sc.enter_context(nc.sbuf_tensor(f"B64_{b}", [128, 2 * HD], F32))
        qkvwT_sb = p1sc.enter_context(
            nc.sbuf_tensor(f"qkvwT{b}", [128, 8 * 3072], BF16))
        xring = p1sc.enter_context(nc.sbuf_tensor(f"xring{b}", [128, 4096], F32))
        ln1tm = p1sc.enter_context(nc.sbuf_tensor(f"ln1tm{b}", [128, 2 * 1024], BF16))
        uu = p1sc.enter_context(nc.sbuf_tensor(f"uu{b}", [128, 2048], F32))
        uu2 = p1sc.enter_context(nc.sbuf_tensor(f"uu2{b}", [128, 2048], F32))
        khat = p1sc.enter_context(nc.sbuf_tensor(f"khat{b}", [128, 3 * 2048], BF16))
        psKV = p1sc.enter_context(nc.psum_tensor(f"psKV{b}", [128, 2048], F32))
        psQ = p1sc.enter_context(nc.psum_tensor(f"psQ{b}", [128, 1024], F32))
        psACC = p1sc.enter_context(nc.psum_tensor(f"psACC{b}", [128, 1024], F32))
        gw_ring = p1sc.enter_context(nc.sbuf_tensor(f"gwr{b}", [128, 2 * 1024], F32))
        gw_junk = p1sc.enter_context(nc.sbuf_tensor(f"gwj{b}", [128, 1024], F32))
        junk = p1sc.enter_context(nc.sbuf_tensor(f"junk{b}", [128, 2048], F32))
        red64 = p1sc.enter_context(nc.sbuf_tensor(f"red64{b}", [128, 64], F32))
        g11 = p1sc.enter_context(nc.sbuf_tensor(f"g11{b}", [1, 16], F32))

        def p1(w, b=b, ln1_fm=ln1_fm, q_fm=q_fm, G64=G64, B64=B64,
               qkvwT_sb=qkvwT_sb, xring=xring, ln1tm=ln1tm, uu=uu, uu2=uu2,
               khat=khat, psKV=psKV, psQ=psQ, psACC=psACC,
               gw_ring=gw_ring, gw_junk=gw_junk, red64=red64, g11=g11,
               junk=junk):
            gred_done = {}

            def gamma_chunk(gi):
                if b != 0:
                    return
                wr = "sWR0" if gi % 2 == 0 else "sWR1"
                if gi >= 2:
                    w.wait("sync", "sS", gred_done[gi - 2])
                w.op("sync", lambda e, gi=gi: e.dma_start(
                    out=gw_ring[:, (gi % 2) * 1024:(gi % 2 + 1) * 1024],
                    in_=wchunk_src(gi)), inc=wr, k=16)
                w.wait("scalar", wr, w.v(wr))
                w.wait("scalar", "sS", w.v("sS"))
                w.op("scalar", lambda e, gi=gi: e.activation(
                    out=gw_junk[:], in_=gw_ring[:, (gi % 2) * 1024:(gi % 2 + 1) * 1024],
                    func=AF.Abs, accum_out=red64[:, gi:gi + 1]), inc="sS")
                gred_done[gi] = w.v("sS")
            for a in range(8):
                w.op("gpsimd", lambda e, a=a: e.dma_start(
                    out=qkvwT_sb[:, a * 3072:(a + 1) * 3072],
                    in_=qkvwT_d[a * 128:(a + 1) * 128, :],
                    max_dma_last_dim=8192), inc="sDW", k=16)
            w.op("sync", lambda e: e.dma_start(
                out=G64[:, 0:HD], in_=AP(gbscr_d.tensor, 0, [[0, 128], [1, HD]])),
                inc="sGB", k=16)
            w.op("sync", lambda e: e.dma_start(
                out=G64[:, HD:2 * HD],
                in_=AP(gbscr_d.tensor, C, [[0, 128], [1, HD]])), inc="sGB", k=16)
            w.op("sync", lambda e: e.dma_start(
                out=B64[:, 0:HD],
                in_=AP(gbscr_d.tensor, 2 * C, [[0, 128], [1, HD]])), inc="sGB", k=16)
            w.op("sync", lambda e: e.dma_start(
                out=B64[:, HD:2 * HD],
                in_=AP(gbscr_d.tensor, 3 * C, [[0, 128], [1, HD]])), inc="sGB", k=16)
            wload = w.v("sDW")
            gbload = w.v("sGB")

            evac_done = {}
            pass1g = {}
            hp_done = {}
            sq_done_g = {}
            khat_done = {}
            kvacc_done = {}
            tr_done = {}
            qevac_done = {}
            applyln_done = {}
            xcons_done = {}
            chain1_done = {}
            ksq_red = {}
            stats_done_map = {}

            def p1ldstats(tc):
                if tc >= 1:
                    w.wait("sync", "sS", applyln_done[tc - 1])
                    w.wait("sync", "sV", xcons_done[tc - 1])
                w.op("sync", lambda e, tc=tc: e.dma_start(
                    out=xring[:].rearrange("p (s d) -> p s d", d=C),
                    in_=x_d[b, tc * 512:(tc + 1) * 512, :].rearrange(
                        "(s p) d -> p s d", p=128)), inc="sXL", k=16)
                xload = w.v("sXL")

                w.wait("scalar", "sXL", xload)
                if tc >= 1:
                    w.wait("scalar", "sV", chain1_done[tc - 1])
                w.wait("vector", "sXL", xload)
                for s4 in range(4):
                    xs = xring[:, s4 * 1024:(s4 + 1) * 1024]
                    w.wait("scalar", "sS", w.v("sS"))
                    w.op("scalar", lambda e, xs=xs, s4=s4: e.activation(
                        out=junk2[:], in_=xs, func=AF.Square,
                        accum_out=ssqc[:, s4:s4 + 1]), inc="sS")
                    w.op("vector", lambda e, xs=xs, s4=s4: e.tensor_reduce(
                        out=sxc[:, s4:s4 + 1], in_=xs, axis=AX.X, op=ALU.add),
                        inc="sV" if s4 == 3 else None)
                sq_done = w.v("sS")
                xred_done = w.v("sV")
                w.wait("vector", "sS", sq_done)
                w.wait("vector", "sV", xred_done)

                w.vops("vector", [
                    lambda e: e.tensor_scalar(out=negm[:], in0=sxc[:],
                                              scalar1=-1.0 / C, scalar2=None,
                                              op0=ALU.mult),
                    lambda e: e.tensor_tensor(out=msq[:], in0=negm[:],
                                              in1=negm[:], op=ALU.mult),
                    lambda e: e.scalar_tensor_tensor(
                        out=vvar[:], in0=ssqc[:], scalar=1.0 / C, in1=msq[:],
                        op0=ALU.mult, op1=ALU.subtract)])
                chain1_done[tc] = w.v("sV")
                w.wait("scalar", "sV", w.v("sV"))
                w.op("scalar", lambda e: e.activation(
                    out=srt[:], in_=vvar[:], func=AF.Sqrt, bias=ceps[:]), inc="sS")
                w.wait("vector", "sS", w.v("sS"))

                w.vops("vector", [
                    lambda e: e.reciprocal(rrc[:], srt[:]),
                    lambda e: e.tensor_tensor(out=nmrc[:], in0=negm[:],
                                              in1=rrc[:], op=ALU.mult),
                    lambda e, tc=tc: e.tensor_copy(
                        out=r1_sb[:, tc * 4:(tc + 1) * 4], in_=rrc[:]),
                    lambda e, tc=tc: e.tensor_copy(
                        out=nmr1_sb[:, tc * 4:(tc + 1) * 4], in_=nmrc[:])])
                stats_done_map[tc] = w.v("sV")
                xcons_done[tc] = max(stats_done_map[tc], xred_done)

            def p1fronts(tc):
                for s4 in range(4):
                    s = tc * 4 + s4
                    lslot = s % 2
                    xs = xring[:, s4 * 1024:(s4 + 1) * 1024]
                    w.wait("scalar", "sV", stats_done_map[tc])
                    if s >= 2:
                        w.wait("scalar", *tr_done[s - 2])
                    w.op("scalar", lambda e, xs=xs, s4=s4, lslot=lslot: e.activation(
                        out=ln1tm[:, lslot * 1024:(lslot + 1) * 1024], in_=xs,
                        func=AF.Identity, scale=rrc[:, s4:s4 + 1],
                        bias=nmrc[:, s4:s4 + 1]), inc="sS")
                    apl = w.v("sS")
                    if s4 == 3:
                        applyln_done[tc] = apl
                    trs = "sTR0" if lslot == 0 else "sTR1"
                    w.wait("scalar", "sS", apl)
                    w.op("scalar", lambda e, s=s, lslot=lslot: e.dma_start(
                        out=AP(ln1_fm, s * 128, [[8 * T, 128], [T, 8], [1, 128]]),
                        in_=ln1tm[:, lslot * 1024:(lslot + 1) * 1024],
                        transpose=True), inc=trs, k=16)
                    tr_done[s] = (trs, w.v(trs))

            def p1body(tc):
                for s4 in range(4):
                    s = tc * 4 + s4
                    lslot = s % 2
                    if s >= 3:
                        w.wait("tensor", "sP", khat_done[s - 3])

                        def kvacc(e, sp=s - 3):
                            ii = None
                            lsp = sp % 3
                            for p in range(8):
                                ii = e.matmul(
                                    psACC[:, p * 128:(p + 1) * 128],
                                    khat[:, lsp * 2048 + p * 128:
                                         lsp * 2048 + (p + 1) * 128],
                                    khat[:, lsp * 2048 + 1024 + p * 128:
                                         lsp * 2048 + 1024 + (p + 1) * 128],
                                    start=(sp == 0 and p in (0, 4)), stop=False,
                                    skip_group_check=True)
                            return ii
                        w.op("tensor", kvacc, inc="sT")
                        kvacc_done[s - 3] = w.v("sT")

                    for g in range(4):
                        bank = g
                        jcol = 1024 + g * 512
                        half = g // 2
                        if g == 0:
                            w.wait("tensor", *tr_done[s])
                            if s == 0:
                                w.wait("tensor", "sDW", wload)
                        if s >= 1:
                            w.wait("tensor", "sV", pass1g[(s - 1, g)])
                        if s >= 1:
                            w.wait("vector", "sP", hp_done.get((s - 1, g // 2), 0))

                        def kvproj(e, s=s, jcol=jcol, bank=bank):
                            reg = psKV[:, bank * 512:(bank + 1) * 512]
                            e.matmul(reg, ones_row[:],
                                     kvb_row[0:1, jcol - 1024:jcol - 512],
                                     start=True, stop=False)
                            ii = None
                            for ci in range(8):
                                ii = e.matmul(
                                    reg,
                                    ln1_fm[:, ci * T + s * 128:
                                           ci * T + (s + 1) * 128],
                                    qkvwT_sb[:, ci * 3072 + jcol:
                                             ci * 3072 + jcol + 512],
                                    start=False, stop=(ci == 7))
                            return ii
                        w.op("tensor", kvproj, inc="sT")
                        projg = w.v("sT")

                        w.wait("vector", "sT", projg)
                        w.op("vector", lambda e, bank=bank, g=g: e.tensor_reduce(
                            out=khneg[:, g * 8:(g + 1) * 8],
                            in_=psKV[:, bank * 512:(bank + 1) * 512].rearrange(
                                "p (h d) -> p h d", d=HD),
                            axis=AX.X, op=ALU.add, negate=True), inc="sV")
                        w.wait("vector", "sV", w.v("sV"))
                        if s == 0 and g == 0:
                            w.wait("vector", "sGB", gbload)
                        w.op("vector", lambda e, bank=bank, g=g:
                             e.scalar_tensor_tensor(
                                 out=uu[:, g * 512:(g + 1) * 512].rearrange(
                                     "p (h d) -> p h d", d=HD),
                                 in0=psKV[:, bank * 512:(bank + 1) * 512].rearrange(
                                     "p (h d) -> p h d", d=HD),
                                 scalar=float(HD),
                                 in1=khneg[:, g * 8:(g + 1) * 8].unsqueeze(2)
                                 .broadcast_to([128, 8, HD]),
                                 op0=ALU.mult, op1=ALU.add), inc="sV")
                        pass1g[(s, g)] = w.v("sV")
                        w.wait("scalar", "sV", pass1g[(s, g)])
                        w.wait("scalar", "sS", w.v("sS"))
                        if (s - 1, g) in ksq_red:
                            w.wait("scalar", "sV", ksq_red[(s - 1, g)])
                        w.op("scalar", lambda e, g=g: e.activation(
                            out=junk[:, g * 512:(g + 1) * 512],
                            in_=uu[:, g * 512:(g + 1) * 512],
                            func=AF.Square), inc="sS")
                        sq_done_g[(s, g)] = w.v("sS")

                    for g in range(4):
                        bank = g % 2
                        w.wait("vector", "sS", sq_done_g[(s, g)])
                        w.op("vector", lambda e, g=g: e.tensor_reduce(
                            out=khsq[:, g * 8:(g + 1) * 8],
                            in_=junk[:, g * 512:(g + 1) * 512].rearrange(
                                "p (h d) -> p h d", d=HD),
                            axis=AX.X, op=ALU.add), inc="sV")
                        ksq_red[(s, g)] = w.v("sV")

                    for half in range(2):
                        hc = slice(half * 16, (half + 1) * 16)
                        w.wait("vector", "sV", ksq_red[(s, half * 2 + 1)])
                        w.vops("vector", [
                            lambda e, hc=hc: e.tensor_scalar(
                                out=khv[:, hc], in0=khsq[:, hc],
                                scalar1=1.0 / HD, scalar2=None,
                                op0=ALU.mult)])
                        w.wait("scalar", "sV", w.v("sV"))
                        w.op("scalar", lambda e, hc=hc: e.activation(
                            out=khsrt[:, hc], in_=khv[:, hc], func=AF.Sqrt,
                            bias=ceps2[:]), inc="sS")
                        w.wait("vector", "sS", w.v("sS"))
                        w.vops("vector", [
                            lambda e, hc=hc: e.reciprocal(khr[:, hc],
                                                          khsrt[:, hc])])
                        hr = slice(half * 1024, (half + 1) * 1024)
                        w.op("vector", lambda e, hc=hc, hr=hr: e.tensor_tensor(
                            out=uu2[:, hr].rearrange("p (h d) -> p h d", d=HD),
                            in0=uu[:, hr].rearrange("p (h d) -> p h d", d=HD),
                            in1=khr[:, hc].unsqueeze(2)
                            .broadcast_to([128, 16, HD]),
                            op=ALU.mult), inc="sV")
                        w.wait("gpsimd", "sV", w.v("sV"))
                        w.op("gpsimd", lambda e, half=half, hr=hr: e.tensor_tensor(
                            out=uu[:, hr].rearrange("p (h d) -> p h d", d=HD),
                            in0=uu2[:, hr].rearrange("p (h d) -> p h d", d=HD),
                            in1=G64[:, half * HD:(half + 1) * HD].unsqueeze(1)
                            .broadcast_to([128, 16, HD]),
                            op=ALU.mult), inc="sP")
                        w.wait("gpsimd", "sP", w.v("sP"))
                        if s >= 3 and half == 0:
                            w.wait("gpsimd", "sT", kvacc_done[s - 3])
                        w.op("gpsimd", lambda e, half=half, lslot=s % 3, hr=hr:
                             e.tensor_tensor(
                                 out=khat[:, lslot * 2048 + half * 1024:
                                          lslot * 2048 + (half + 1) * 1024]
                                 .rearrange("p (h d) -> p h d", d=HD),
                                 in0=uu[:, hr].rearrange("p (h d) -> p h d", d=HD),
                                 in1=B64[:, half * HD:(half + 1) * HD].unsqueeze(1)
                                 .broadcast_to([128, 16, HD]),
                                 op=ALU.add), inc="sP")
                        w.wait("gpsimd", "sP", w.v("sP"))
                        khat_done[s] = w.v("sP")
                        hp_done[(s, half)] = w.v("sP")

                    for u in range(4):
                        gamma_chunk(s * 4 + u)
                    if tc >= 1:
                        p1q_one(tc - 1, s4 * 2)
                        p1q_one(tc - 1, s4 * 2 + 1)

            def p1q_one(tc, jc):
                qslot = jc % 2
                if jc == 0:
                    w.wait("tensor", *tr_done[tc * 4 + 2])
                    w.wait("tensor", *tr_done[tc * 4 + 3])
                if jc >= 2:
                    w.wait("tensor", "sV", qevac_done[jc - 2])
                elif tc >= 1:
                    w.wait("tensor", "sV", qevac_done[6 + qslot])

                def qproj(e, jc=jc, qslot=qslot, tc=tc):
                    ii = None
                    for ci in range(8):
                        ii = e.matmul(
                            psQ[:, qslot * 512:(qslot + 1) * 512],
                            qkvwT_sb[:, ci * 3072 + jc * 128:
                                     ci * 3072 + (jc + 1) * 128],
                            ln1_fm[:, ci * T + tc * 512:ci * T + (tc + 1) * 512],
                            start=(ci == 0), stop=(ci == 7))
                    return ii
                w.op("tensor", qproj, inc="sT")
                w.wait("vector", "sT", w.v("sT"))
                w.op("vector", lambda e, jc=jc, qslot=qslot, tc=tc: e.tensor_scalar(
                    out=q_fm[:, jc * T + tc * 512:jc * T + (tc + 1) * 512],
                    in0=psQ[:, qslot * 512:(qslot + 1) * 512],
                    scalar1=qb_cols[:, jc:jc + 1], scalar2=None,
                    op0=ALU.add), inc="sV")
                qevac_done[jc] = w.v("sV")

            p1ldstats(0)
            for tc in range(NTC):
                p1fronts(tc)
                if tc + 1 < NTC:
                    p1ldstats(tc + 1)
                p1body(tc)
            for jc in range(8):
                p1q_one(NTC - 1, jc)

            for sp in (NSUB - 3, NSUB - 2, NSUB - 1):
                w.wait("tensor", "sP", khat_done[sp])

                def kvacc_last(e, sp=sp):
                    ii = None
                    lsp = sp % 3
                    for p in range(8):
                        ii = e.matmul(
                            psACC[:, p * 128:(p + 1) * 128],
                            khat[:, lsp * 2048 + p * 128:
                                 lsp * 2048 + (p + 1) * 128],
                            khat[:, lsp * 2048 + 1024 + p * 128:
                                 lsp * 2048 + 1024 + (p + 1) * 128],
                            start=False, stop=(sp == NSUB - 1),
                            skip_group_check=True)
                    return ii
                w.op("tensor", kvacc_last, inc="sT")
                kvacc_done[sp] = w.v("sT")

            if b == 0:
                w.wait("tensor", "sS", gred_done[63])
                w.wait("tensor", "sV", qevac_done[7])

                def gtotmm(e):
                    e.matmul(psQ[0:1, 0:32], onescol_f[:],
                             red64[:, 0:32], start=True, stop=True)
                    return e.matmul(psQ[0:1, 32:64], onescol_f[:],
                                    red64[:, 32:64], start=True, stop=True,
                                    skip_group_check=True)
                w.op("tensor", gtotmm, inc="sT")
                w.wait("vector", "sT", w.v("sT"))
                w.vops("vector", [
                    lambda e: e.tensor_reduce(out=g11[0:1, 2:3],
                                              in_=psQ[0:1, 0:32], axis=AX.X,
                                              op=ALU.add),
                    lambda e: e.tensor_reduce(out=g11[0:1, 3:4],
                                              in_=psQ[0:1, 32:64], axis=AX.X,
                                              op=ALU.add),
                    lambda e: e.tensor_scalar(
                        out=g11[0:1, 4:6], in0=g11[0:1, 2:4], scalar1=1.0 / WN,
                        scalar2=EPS, op0=ALU.mult, op1=ALU.add),
                    lambda e: e.reciprocal(g11[0:1, 6:8], g11[0:1, 4:6]),
                    lambda e: e.tensor_scalar(
                        out=g11[0:1, 8:10], in0=g11[0:1, 6:8], scalar1=-1.0,
                        scalar2=None, op0=ALU.mult)])
                w.wait("gpsimd", "sV", w.v("sV"))
                w.op("gpsimd", lambda e: e.dma_start(out=smscr_d[0:4, 0:1],
                                                     in_=g11[0:1, 6:10]),
                     inc="sGM", k=16)
                w.wait("sync", "sGM", w.v("sGM"))
                w.op("sync", lambda e: e.dma_start(
                    out=invg_col[:, 0:4],
                    in_=AP(smscr_d.tensor, 0, [[0, 128], [1, 4]])), inc="sIG", k=16)
                w.wait("sync", "sIG", w.v("sIG"))

            w.wait("vector", "sT", kvacc_done[NSUB - 1])

            def kvfin(e):
                ii = None
                for h in range(H):
                    p, r = h // 2, h % 2
                    base = r * 64
                    ii = e.scalar_tensor_tensor(
                        out=kvbd[base:base + 64,
                                 p * 128 + r * 64: p * 128 + r * 64 + 64],
                        in0=psACC[base:base + 64,
                                  p * 128 + r * 64: p * 128 + r * 64 + 64],
                        scalar=tempQ[base:base + 64, p:p + 1],
                        in1=abias_sb[base:base + 64, p * 64:(p + 1) * 64],
                        op0=ALU.mult, op1=ALU.add)
                return ii
            w.op("vector", kvfin, inc="sV")

            if debug:
                w.wait("gpsimd", "sV", w.v("sV"))
                w.wait("gpsimd", "sS", w.v("sS"))
                w.wait("gpsimd", "sT", w.v("sT"))
                for nm, src in [("d_kvbd", kvbd[:]), ("d_ln1fm", ln1_fm[:]),
                                ("d_q", q_fm[:]), ("d_khat", khat[:])]:
                    w.op("gpsimd", lambda e, nm=nm, src=src: e.dma_start(
                        out=dbg[nm], in_=src), inc="sDO", k=16)
                w.wait("gpsimd", "sDO", w.v("sDO"))

        if 'P1' in run_p:
            w.block(p1, f"p1_{b}")
        p1sc.close()

        p2sc = ExitStack()
        p1T_sb = p2sc.enter_context(nc.sbuf_tensor(f"p1T{b}", [128, 8 * C], BF16))
        p2T_sb = p2sc.enter_context(nc.sbuf_tensor(f"p2T{b}", [128, 8 * C], BF16))
        xring2 = p2sc.enter_context(nc.sbuf_tensor(f"xr2{b}", [128, 2 * 4096], F32))
        att2 = p2sc.enter_context(nc.sbuf_tensor(f"att2{b}", [128, 8 * 512], BF16))
        hring = p2sc.enter_context(nc.sbuf_tensor(f"hring{b}", [128, 8 * 512], BF16))
        aoring = p2sc.enter_context(nc.sbuf_tensor(f"aor{b}", [128, 4 * 1024], BF16))
        zring = p2sc.enter_context(nc.sbuf_tensor(f"zr{b}", [128, 1024], F32))
        ln1tm2 = p2sc.enter_context(nc.sbuf_tensor(f"l1b{b}", [128, 1024], BF16))
        y1ring = p2sc.enter_context(nc.sbuf_tensor(f"y1r{b}", [128, 2 * 1024], F32))
        psATT = p2sc.enter_context(nc.psum_tensor(f"psATT{b}", [128, 1024], F32))
        psP1 = p2sc.enter_context(nc.psum_tensor(f"psP1{b}", [128, 1024], F32))
        psP2 = p2sc.enter_context(nc.psum_tensor(f"psP2{b}", [128, 2048], F32))
        qw_ring = p2sc.enter_context(nc.sbuf_tensor(f"qwr{b}", [128, 2 * 1024], F32))
        qbuf = p2sc.enter_context(nc.sbuf_tensor(f"qbuf{b}", [128, 4 * 1024], F8))
        qbufB = p2sc.enter_context(nc.sbuf_tensor(f"qbufB{b}", [128, 2 * 1024], F8))
        qsA = p2sc.enter_context(nc.sbuf_tensor(f"qsA{b}", [128, 1024], F32))

        def p2(w, b=b, ln1_fm=ln1_fm, q_fm=q_fm, p1T_sb=p1T_sb, p2T_sb=p2T_sb,
               xring2=xring2, att2=att2, hring=hring, aoring=aoring, zring=zring,
               ln1tm2=ln1tm2, y1ring=y1ring, psATT=psATT, psP1=psP1, psP2=psP2,
               qw_ring=qw_ring, qbuf=qbuf, qbufB=qbufB, qsA=qsA):
            qtt_done = {}
            qst_done = {}
            qm_done = {}
            qld = {}
            qctr = [0]

            def quant_load(gi):
                if b != 0 or gi >= 64:
                    return
                wr = "sWR0" if gi % 2 == 0 else "sWR1"
                slot = gi % 2
                if gi >= 2:
                    w.wait("sync", *qtt_done[gi - 2])
                w.op("sync", lambda e, gi=gi, slot=slot: e.dma_start(
                    out=qw_ring[:, slot * 1024:(slot + 1) * 1024],
                    in_=wchunk_src(gi)), inc=wr, k=16)
                qld[gi] = (wr, w.v(wr))

            def quant_chunk(_ignored=None):
                if b != 0:
                    return
                gi = qctr[0]
                if gi >= 64:
                    return
                qctr[0] += 1
                quant_comp(gi)
                quant_load(gi + 2)

            lastA = [-1]
            lastB = [-1]

            def quant_comp(gi):
                so = "sSO0" if gi % 2 == 0 else "sSO1"
                gc = 0 if gi < 32 else 1
                slot = gi % 2
                if (gi % 16) in (0, 3, 6, 9, 12):
                    w.wait("vector", *qld[gi])
                    w.wait("vector", "sV", w.v("sV"))
                    if lastA[0] >= 0:
                        w.wait("vector", *qtt_done[lastA[0]])
                    lastA[0] = gi
                    w.op("vector", lambda e, slot=slot, gc=gc: e.tensor_scalar(
                        out=qbuf[:, 0:1024],
                        in0=qw_ring[:, slot * 1024:(slot + 1) * 1024],
                        scalar1=invg_col[:, gc:gc + 1], scalar2=0.5,
                        op0=ALU.mult, op1=ALU.is_ge), inc="sV")
                    w.wait("vector", "sV", w.v("sV"))
                    w.op("vector", lambda e, slot=slot, gc=gc: e.tensor_scalar(
                        out=qbuf[:, 1024:2048],
                        in0=qw_ring[:, slot * 1024:(slot + 1) * 1024],
                        scalar1=invg_col[:, 2 + gc:3 + gc], scalar2=0.5,
                        op0=ALU.mult, op1=ALU.is_ge), inc="sV")
                    w.wait("vector", "sV", w.v("sV"))
                    qts_done = w.v("sV")
                    w.wait("gpsimd", "sV", qts_done)
                    op = ALU.subtract
                    src0, src1 = qbuf[:, 0:1024], qbuf[:, 1024:2048]
                else:
                    w.wait("scalar", *qld[gi])
                    w.wait("scalar", "sS", w.v("sS"))
                    if lastB[0] >= 0:
                        w.wait("scalar", "sV", qm_done.get(lastB[0], 0))
                    w.op("scalar", lambda e, slot=slot, gc=gc: e.activation(
                        out=qsA[:],
                        in_=qw_ring[:, slot * 1024:(slot + 1) * 1024],
                        func=AF.Abs, scale=invg_col[:, gc:gc + 1]), inc="sS")
                    qa_done = w.v("sS")
                    if lastB[0] >= 0:
                        w.wait("scalar", *qtt_done[lastB[0]])
                    w.op("scalar", lambda e, slot=slot: e.activation(
                        out=qbufB[:, 1024:2048],
                        in_=qw_ring[:, slot * 1024:(slot + 1) * 1024],
                        func=AF.Sign), inc="sS")
                    qsg_done = w.v("sS")
                    w.wait("vector", "sS", qa_done)
                    if lastB[0] >= 0:
                        w.wait("vector", *qtt_done[lastB[0]])
                    lastB[0] = gi
                    w.op("vector", lambda e: e.tensor_scalar(
                        out=qbufB[:, 0:1024], in0=qsA[:],
                        scalar1=0.5, scalar2=None, op0=ALU.is_ge), inc="sV")
                    qm_done[gi] = w.v("sV")
                    w.wait("gpsimd", "sV", qm_done[gi])
                    w.wait("gpsimd", "sS", qsg_done)
                    op = ALU.mult
                    src0, src1 = qbufB[:, 0:1024], qbufB[:, 1024:2048]
                if gi >= 2:
                    w.wait("gpsimd", so, qst_done[gi - 2])
                w.op("gpsimd", lambda e, slot=slot, op=op, src0=src0, src1=src1:
                     e.tensor_tensor(
                         out=qbuf[:, (2 + slot) * 1024:(3 + slot) * 1024],
                         in0=src0, in1=src1, op=op), inc="sP")
                qtt_done[gi] = ("sP", w.v("sP"))
                w.wait("gpsimd", "sP", w.v("sP"))
                w.op("gpsimd", lambda e, gi=gi, slot=slot: e.dma_start(
                    out=wchunk_dst(gi),
                    in_=qbuf[:, (2 + slot) * 1024:(3 + slot) * 1024]), inc=so, k=16)
                qst_done[gi] = w.v(so)
            w.op("gpsimd", lambda e: e.dma_start(
                out=p1T_sb[:].rearrange("p (a j) -> p a j", j=C),
                in_=p1wT_d.rearrange("(a p) j -> p a j", p=128)), inc="sDW", k=16)
            w.op("gpsimd", lambda e: e.dma_start(
                out=p2T_sb[:].rearrange("p (a j) -> p a j", j=C),
                in_=p2wT_d.rearrange("(a p) j -> p a j", p=128)), inc="sDW", k=16)
            wload = w.v("sDW")
            quant_load(0)
            quant_load(1)

            att2_done = {}
            gelu_done = {}
            p2mm_done = {}
            p1_last = {}
            ao_done = {}
            zs_done = {}
            y1tt_done = {}
            y1st_done = {}
            lnb_done = {}
            chain1_done = {}
            xload_v = {}

            def xld(tc):
                x2slot = tc % 2
                if tc >= 2:
                    w.wait("sync", "sS", lnb_done[tc - 2])
                    w.wait("sync", "sV", y1tt_done[(tc - 1) * 4 - 1])
                xsem = "sXL" if x2slot == 0 else "sXL1"
                w.op("sync", lambda e, tc=tc, x2slot=x2slot: e.dma_start(
                    out=xring2[:, x2slot * 4096:(x2slot + 1) * 4096].rearrange(
                        "p (s d) -> p s d", d=C),
                    in_=x_d[b, tc * 512:(tc + 1) * 512, :].rearrange(
                        "(s p) d -> p s d", p=128)), inc=xsem, k=16)
                xload_v[tc] = (xsem, w.v(xsem))

            def attf(tc):
                for p in range(8):
                    if p >= 2:
                        w.wait("tensor", "sV", att2_done[p - 2])
                    elif tc >= 1:
                        w.wait("tensor", "sV", att2_done[6 + p])
                    w.op("tensor", lambda e, p=p, tc=tc: e.matmul(
                        psATT[:, (p % 2) * 512:(p % 2 + 1) * 512],
                        kvbd[:, p * 128:(p + 1) * 128],
                        q_fm[:, p * T + tc * 512:p * T + (tc + 1) * 512],
                        start=True, stop=True), inc="sT")
                    w.wait("vector", "sT", w.v("sT"))
                    if tc >= 1 and p == 0:
                        w.wait("vector", "sT", p1_last[tc - 1])
                    w.op("vector", lambda e, p=p, tc=tc: e.tensor_tensor(
                        out=att2[:, p * 512:(p + 1) * 512],
                        in0=psATT[:, (p % 2) * 512:(p % 2 + 1) * 512],
                        in1=ln1_fm[:, p * T + tc * 512:p * T + (tc + 1) * 512],
                        op=ALU.add), inc="sV")
                    att2_done[p] = w.v("sV")

            def p1f(tc):
                for jc in range(8):
                    qslot = jc % 2
                    w.wait("tensor", "sV", att2_done[7])
                    if tc == 0 and jc == 0:
                        w.wait("tensor", "sDW", wload)
                    if jc >= 2:
                        w.wait("tensor", "sS", gelu_done[jc - 2])
                    elif tc >= 1:
                        w.wait("tensor", "sS", gelu_done[6 + qslot])

                    def p1mm(e, jc=jc, qslot=qslot):
                        ii = None
                        for ci in range(8):
                            ii = e.matmul(
                                psP1[:, qslot * 512:(qslot + 1) * 512],
                                p1T_sb[:, ci * C + jc * 128:ci * C + (jc + 1) * 128],
                                att2[:, ci * 512:(ci + 1) * 512],
                                start=(ci == 0), stop=(ci == 7))
                        return ii
                    w.op("tensor", p1mm, inc="sT")
                    p1_last[tc] = w.v("sT")
                    w.wait("scalar", "sT", w.v("sT"))
                    if tc >= 1 and jc == 0:
                        w.wait("scalar", "sT", p2mm_done[tc * 4 - 1])
                    w.op("scalar", lambda e, jc=jc, qslot=qslot: e.activation(
                        out=hring[:, jc * 512:(jc + 1) * 512],
                        in_=psP1[:, qslot * 512:(qslot + 1) * 512],
                        func=AF.Gelu, bias=p1b_cols[:, jc:jc + 1]), inc="sS")
                    gelu_done[jc] = w.v("sS")

            def body(tc):
                x2slot = tc % 2
                xsem, xload = xload_v[tc]
                for s4 in range(4):
                    s = tc * 4 + s4
                    pslot = s4 % 2
                    lslot = s4 % 2
                    w.wait("scalar", xsem, xload)
                    if s >= 2:
                        w.wait("scalar", "sV", ao_done.get(s - 2, 0))
                    if s >= 1:
                        w.wait("scalar", "sV", ao_done.get(s - 1, 0))
                    w.op("scalar", lambda e, s=s, s4=s4,
                         x2slot=x2slot: e.activation(
                        out=ln1tm2[:],
                        in_=xring2[:, x2slot * 4096 + s4 * 1024:
                                   x2slot * 4096 + (s4 + 1) * 1024],
                        func=AF.Identity, scale=r1_sb[:, s:s + 1],
                        bias=nmr1_sb[:, s:s + 1]), inc="sS")
                    lnb_done[tc] = w.v("sS")

                    w.wait("tensor", "sS", gelu_done[7])
                    if s4 >= 2:
                        w.wait("tensor", "sV", ao_done[s - 2])

                    def p2mm(e, s4=s4, pslot=pslot):
                        ii = None
                        for jh in range(2):
                            reg = psP2[:, pslot * 1024 + jh * 512:
                                       pslot * 1024 + (jh + 1) * 512]
                            e.matmul(reg, ones_row[:],
                                     p2b_row[0:1, jh * 512:(jh + 1) * 512],
                                     start=True, stop=False)
                            for ci in range(8):
                                ii = e.matmul(
                                    reg,
                                    hring[:, ci * 512 + s4 * 128:
                                          ci * 512 + s4 * 128 + 128],
                                    p2T_sb[:, ci * C + jh * 512:
                                           ci * C + (jh + 1) * 512],
                                    start=False, stop=(ci == 7))
                        return ii
                    w.op("tensor", p2mm, inc="sT")
                    p2mm_done[s] = w.v("sT")
                    w.wait("vector", "sT", p2mm_done[s])
                    w.wait("vector", "sS", lnb_done[tc])
                    if s >= 4:
                        w.wait("vector", "sS", zs_done[s - 4])
                    w.op("vector", lambda e, s4=s4, pslot=pslot:
                         e.tensor_tensor(
                             out=aoring[:, s4 * 1024:(s4 + 1) * 1024],
                             in0=psP2[:, pslot * 1024:(pslot + 1) * 1024],
                             in1=ln1tm2[:],
                             op=ALU.add), inc="sV")
                    ao_done[s] = w.v("sV")
                    quant_chunk()
                    quant_chunk()
                    w.wait("scalar", "sV", ao_done[s])
                    if tc >= 1 and s4 == 0:
                        w.wait("scalar", "sV", chain1_done[tc - 1])
                    w.wait("scalar", "sS", w.v("sS"))
                    w.op("scalar", lambda e, s4=s4: e.activation(
                        out=junk2[:], in_=aoring[:, s4 * 1024:(s4 + 1) * 1024],
                        func=AF.Square, accum_out=ssqc[:, s4:s4 + 1]), inc="sS")
                    w.wait("vector", "sV", w.v("sV"))
                    w.op("vector", lambda e, s4=s4: e.tensor_reduce(
                        out=sxc[:, s4:s4 + 1],
                        in_=aoring[:, s4 * 1024:(s4 + 1) * 1024],
                        axis=AX.X, op=ALU.add), inc="sV")

                w.wait("vector", "sS", w.v("sS"))
                w.wait("vector", "sV", w.v("sV"))

                w.vops("vector", [
                    lambda e: e.tensor_scalar(out=negm[:], in0=sxc[:],
                                              scalar1=-1.0 / C, scalar2=None,
                                              op0=ALU.mult),
                    lambda e: e.tensor_tensor(out=msq[:], in0=negm[:],
                                              in1=negm[:], op=ALU.mult),
                    lambda e: e.scalar_tensor_tensor(
                        out=vvar[:], in0=ssqc[:], scalar=1.0 / C, in1=msq[:],
                        op0=ALU.mult, op1=ALU.subtract)])
                chain1_done[tc] = w.v("sV")
                w.wait("scalar", "sV", w.v("sV"))
                w.op("scalar", lambda e: e.activation(
                    out=srt[:], in_=vvar[:], func=AF.Sqrt, bias=ceps[:]), inc="sS")
                w.wait("vector", "sS", w.v("sS"))

                w.vops("vector", [
                    lambda e: e.reciprocal(rrc[:], srt[:]),
                    lambda e: e.tensor_tensor(out=nmrc[:], in0=negm[:],
                                              in1=rrc[:], op=ALU.mult)])
                stats_done = w.v("sV")

                for s4 in range(4):
                    s = tc * 4 + s4
                    zslot = s4 % 2
                    w.wait("scalar", "sV", stats_done)
                    if s >= 2:
                        w.wait("scalar", "sV", y1tt_done.get(s - 2, 0))
                    if s >= 1:
                        w.wait("scalar", "sV", y1tt_done.get(s - 1, 0))
                    w.op("scalar", lambda e, s4=s4: e.activation(
                        out=zring[:],
                        in_=aoring[:, s4 * 1024:(s4 + 1) * 1024],
                        func=AF.Identity, scale=rrc[:, s4:s4 + 1],
                        bias=nmrc[:, s4:s4 + 1]), inc="sS")
                    zs_done[s] = w.v("sS")
                    sy = "sY0" if zslot == 0 else "sY1"
                    w.wait("vector", xsem, xload)
                    w.wait("vector", "sS", zs_done[s])
                    if s >= 2:
                        w.wait("vector", sy, y1st_done.get(s - 2, 0))
                    w.op("vector", lambda e, s4=s4, zslot=zslot,
                         x2slot=x2slot: e.tensor_tensor(
                        out=y1ring[:, zslot * 1024:(zslot + 1) * 1024],
                        in0=xring2[:, x2slot * 4096 + s4 * 1024:
                                   x2slot * 4096 + (s4 + 1) * 1024],
                        in1=zring[:],
                        op=ALU.add), inc="sV")
                    y1tt_done[s] = w.v("sV")
                    w.wait("gpsimd", "sV", y1tt_done[s])
                    w.op("gpsimd", lambda e, s=s, zslot=zslot: e.dma_start(
                        out=y1s_d[b, s * 128:(s + 1) * 128, :],
                        in_=y1ring[:, zslot * 1024:(zslot + 1) * 1024]),
                        inc=sy, k=16)
                    y1st_done[s] = w.v(sy)
                    quant_chunk()
                    quant_chunk()

            xld(0)
            attf(0)
            p1f(0)
            for tc in range(NTC):
                if tc + 1 < NTC:
                    xld(tc + 1)
                    attf(tc + 1)
                body(tc)
                if tc + 1 < NTC:
                    p1f(tc + 1)

            w.wait("gpsimd", "sY0", w.v("sY0"))
            w.wait("gpsimd", "sY1", w.v("sY1"))
            w.wait("gpsimd", "sSO0", w.v("sSO0"))
            w.wait("gpsimd", "sSO1", w.v("sSO1"))
            if debug:
                w.op("gpsimd", lambda e: e.dma_start(out=dbg["d_att2"],
                                                     in_=att2[:]), inc="sDO", k=16)
                w.wait("gpsimd", "sDO", w.v("sDO"))

        if 'P2' in run_p:
            w.block(p2, f"p2_{b}")
        p2sc.close()
        absc.close()

        p3sc = ExitStack()
        wq1_sb = p3sc.enter_context(nc.sbuf_tensor(f"wq1{b}", [128, 8 * FF], F8))
        wq2_sb = p3sc.enter_context(nc.sbuf_tensor(f"wq2{b}", [128, 32 * C], F8))
        y1r3 = p3sc.enter_context(nc.sbuf_tensor(f"y1r3{b}", [128, 4 * 2048], F32))
        ln2tm = p3sc.enter_context(nc.sbuf_tensor(f"l2t{b}", [128, 2 * 1024], BF16))
        ln2fm = p3sc.enter_context(nc.sbuf_tensor(f"l2f{b}", [128, 2 * 2048], BF16))
        h1 = p3sc.enter_context(nc.sbuf_tensor(f"h1_{b}", [128, 2 * 32 * 256], BF16))
        ffnr = p3sc.enter_context(nc.sbuf_tensor(f"ffnr{b}", [128, 2 * 1024], F32))
        z3r = p3sc.enter_context(nc.sbuf_tensor(f"z3r{b}", [128, 1024], F32))
        outr = p3sc.enter_context(nc.sbuf_tensor(f"outr{b}", [128, 1024], F32))
        psF1 = p3sc.enter_context(nc.psum_tensor(f"psF1{b}", [128, 2048], F32))
        psF2 = p3sc.enter_context(nc.psum_tensor(f"psF2{b}", [128, 2048], F32))

        def p3(w, b=b, wq1_sb=wq1_sb, wq2_sb=wq2_sb, y1r3=y1r3, ln2tm=ln2tm,
               ln2fm=ln2fm, h1=h1, ffnr=ffnr, z3r=z3r, outr=outr, psF1=psF1,
               psF2=psF2):
            def wqloads():
                w.op("sync", lambda e: e.dma_start(
                    out=wq1_sb[:].rearrange("p (a f) -> p a f", f=FF),
                    in_=wq1s_d.rearrange("(a p) f -> p a f", p=128)),
                    inc="sWQ", k=16)
                wl = w.v("sWQ")
                w.op("sync", lambda e: e.dma_start(
                    out=wq2_sb[:].rearrange("p (a d) -> p a d", d=C),
                    in_=wq2s_d.rearrange("(a p) d -> p a d", p=128)),
                    inc="sGB", k=16)
                return wl, w.v("sGB")

            evac_done = {}
            tr_done3 = {}
            relu_done = {}
            f1_last = {}
            f2_done = {}
            ffn_done = {}
            outtt_done = {}
            outst_done = {}
            zs_done = {}
            apl_done = {}
            chain1_done = {}
            chain3_done = {}

            yload_map = {}

            def yld3(tc3):
                yslot = tc3 % 4
                ysem = ("sXL", "sXL1", "sXL2", "sXL3")[yslot]
                if tc3 >= 4:
                    w.wait("sync", "sV", outtt_done.get(2 * (tc3 - 4) + 1, 0))
                    w.wait("sync", "sV", apl_done.get(2 * (tc3 - 4) + 1, 0))
                w.op("sync", lambda e, tc3=tc3, yslot=yslot: e.dma_start(
                    out=y1r3[:, yslot * 2048:(yslot + 1) * 2048].rearrange(
                        "p (s d) -> p s d", d=C),
                    in_=y1s_d[b, tc3 * 256:(tc3 + 1) * 256, :].rearrange(
                        "(s p) d -> p s d", p=128)), inc=ysem, k=16)
                yload_map[tc3] = w.v(ysem)

            def p3front(tc3):
                slot = tc3 % 2
                s_hi = 2 * tc3 + 1
                yslot = tc3 % 4
                ysem = ("sXL", "sXL1", "sXL2", "sXL3")[yslot]
                yload = yload_map[tc3]

                w.wait("scalar", ysem, yload)
                if tc3 >= 1:
                    w.wait("scalar", "sV", chain1_done[tc3 - 1])
                w.wait("vector", ysem, yload)
                for s2 in range(2):
                    ysub = y1r3[:, yslot * 2048 + s2 * 1024:
                                yslot * 2048 + (s2 + 1) * 1024]
                    w.wait("scalar", "sS", w.v("sS"))
                    w.op("scalar", lambda e, ysub=ysub, s2=s2: e.activation(
                        out=junk2[:], in_=ysub, func=AF.Square,
                        accum_out=ssqc[:, s2:s2 + 1]), inc="sS")
                    w.op("vector", lambda e, ysub=ysub, s2=s2: e.tensor_reduce(
                        out=sxc[:, s2:s2 + 1], in_=ysub, axis=AX.X, op=ALU.add),
                        inc="sV" if s2 == 1 else None)
                w.wait("vector", "sS", w.v("sS"))
                w.wait("vector", "sV", w.v("sV"))


                w.vops("vector", [
                    lambda e: e.tensor_scalar(out=negm[:, 0:2], in0=sxc[:, 0:2],
                                              scalar1=-1.0 / C, scalar2=None,
                                              op0=ALU.mult),
                    lambda e: e.tensor_tensor(out=msq[:, 0:2], in0=negm[:, 0:2],
                                              in1=negm[:, 0:2], op=ALU.mult),
                    lambda e: e.scalar_tensor_tensor(
                        out=vvar[:, 0:2], in0=ssqc[:, 0:2], scalar=1.0 / C,
                        in1=msq[:, 0:2], op0=ALU.mult, op1=ALU.subtract)])
                chain1_done[tc3] = w.v("sV")
                w.wait("scalar", "sV", w.v("sV"))
                w.op("scalar", lambda e: e.activation(
                    out=srt[:, 0:2], in_=vvar[:, 0:2], func=AF.Sqrt,
                    bias=ceps[:]), inc="sS")
                w.wait("vector", "sS", w.v("sS"))

                w.vops("vector", [
                    lambda e: e.reciprocal(rrc[:, 0:2], srt[:, 0:2]),
                    lambda e: e.tensor_tensor(out=nmrc[:, 0:2], in0=negm[:, 0:2],
                                              in1=rrc[:, 0:2], op=ALU.mult)])
                stats_done = w.v("sV")

                for s2 in range(2):
                    s = tc3 * 2 + s2
                    lsl = s % 2
                    w.wait("vector", "sV", stats_done)
                    if s >= 2:
                        w.wait("vector", *tr_done3[s - 2])
                    w.op("vector", lambda e, s2=s2, yslot=yslot, lsl=lsl:
                         e.scalar_tensor_tensor(
                             out=ln2tm[:, lsl * 1024:(lsl + 1) * 1024],
                             in0=y1r3[:, yslot * 2048 + s2 * 1024:
                                      yslot * 2048 + (s2 + 1) * 1024],
                             scalar=rrc[:, s2:s2 + 1],
                             in1=nmrc[:, s2:s2 + 1].broadcast_to([128, 1024]),
                             op0=ALU.mult, op1=ALU.add), inc="sV")
                    apl_done[s] = w.v("sV")
                    if tc3 >= 2 and s2 == 0:
                        w.wait("scalar", "sT", f1_last.get(tc3 - 2, 0))
                    trs = "sTR0" if lsl == 0 else "sTR1"
                    w.wait("scalar", "sV", apl_done[s])
                    w.op("scalar", lambda e, s2=s2, slot=slot, lsl=lsl: e.dma_start(
                        out=AP(ln2fm, slot * 2048 + s2 * 128,
                               [[2 * 2048, 128], [256, 8], [1, 128]]),
                        in_=ln2tm[:, lsl * 1024:(lsl + 1) * 1024],
                        transpose=True), inc=trs, k=16)
                    tr_done3[s] = (trs, w.v(trs))

            relu_last = {}

            def f1blk(tc3):
                slot = tc3 % 2
                hsl = tc3 % 2
                s_hi = 2 * tc3 + 1
                for fc in (range(32) if p3sub in ('full', 'f1') else []):
                    fslot = fc % 4
                    if fc == 0:
                        w.wait("tensor", *tr_done3[s_hi - 1])
                        w.wait("tensor", *tr_done3[s_hi])
                    if tc3 == 0 and fc == 0:
                        w.wait("tensor", "sWQ", wload)
                    if fc >= 4:
                        w.wait("tensor", "sS", relu_done[fc - 4])
                    elif tc3 >= 1:
                        w.wait("tensor", "sS", relu_done.get(28 + fslot, 0))

                    def f1mm(e, fc=fc, fslot=fslot, slot=slot):
                        ii = None
                        for ci in range(8):
                            ii = e.matmul(
                                psF1[:, fslot * 512:fslot * 512 + 256],
                                wq1_sb[:, ci * FF + fc * 128:
                                       ci * FF + (fc + 1) * 128],
                                ln2fm[:, slot * 2048 + ci * 256:
                                      slot * 2048 + (ci + 1) * 256],
                                start=(ci == 0), stop=(ci == 7))
                        return ii
                    w.op("tensor", f1mm, inc="sT")
                    if fc == 31:
                        f1_last[tc3] = w.v("sT")
                    w.wait("scalar", "sT", w.v("sT"))
                    if tc3 >= 2 and fc == 0:
                        w.wait("scalar", "sT", f2_done.get(2 * (tc3 - 2) + 1, 0))
                    w.op("scalar", lambda e, fc=fc, fslot=fslot, hsl=hsl:
                         e.activation(
                             out=h1[:, hsl * 8192 + fc * 256:
                                    hsl * 8192 + (fc + 1) * 256],
                             in_=psF1[:, fslot * 512:fslot * 512 + 256],
                             func=AF.Relu, bias=b1_cols[:, fc:fc + 1]),
                        inc="sS")
                    relu_done[fc] = w.v("sS")
                relu_last[tc3] = relu_done.get(31, 0)

            def f2blk(tc3):
                slot = tc3 % 2
                hsl = tc3 % 2
                yslot = tc3 % 4
                ysem = ("sXL", "sXL1", "sXL2", "sXL3")[yslot]
                yload = yload_map[tc3]

                if p3sub in ('ln2', 'f1'):
                    for s2 in range(2):
                        s = tc3 * 2 + s2
                        if s >= 1:
                            w.wait("vector", "sOS", outst_done.get(s - 1, 0))
                        w.wait("vector", "sXL", yload)
                        w.op("vector", lambda e, s2=s2: e.tensor_copy(
                            out=outr[:], in_=y1r3[:, s2 * 1024:(s2 + 1) * 1024]),
                            inc="sV")
                        outtt_done[s] = w.v("sV")
                        apl_done.setdefault(s, 0)
                        w.wait("gpsimd", "sV", outtt_done[s])
                        w.op("gpsimd", lambda e, s=s: e.dma_start(
                            out=out_d[b, s * 128:(s + 1) * 128, :], in_=outr[:]),
                            inc="sOS", k=16)
                        outst_done[s] = w.v("sOS")
                    return
                for s2 in range(2):
                    s = tc3 * 2 + s2
                    pslot = s2
                    w.wait("tensor", "sS", relu_last.get(tc3, 0))
                    if tc3 == 0 and s2 == 0:
                        w.wait("tensor", "sGB", wload2)
                    if s >= 2:
                        w.wait("tensor", "sV", ffn_done.get(s - 2, 0))

                    def f2mm(e, s2=s2, pslot=pslot, hsl=hsl):
                        ii = None
                        for jh in range(2):
                            reg = psF2[:, pslot * 1024 + jh * 512:
                                       pslot * 1024 + (jh + 1) * 512]
                            for fc in range(32):
                                ii = e.matmul(
                                    reg,
                                    h1[:, hsl * 8192 + fc * 256 + s2 * 128:
                                       hsl * 8192 + fc * 256 + s2 * 128 + 128],
                                    wq2_sb[:, fc * C + jh * 512:
                                           fc * C + (jh + 1) * 512],
                                    start=(fc == 0), stop=(fc == 31))
                        return ii
                    w.op("tensor", f2mm, inc="sT")
                    f2_done[s] = w.v("sT")
                    w.wait("vector", "sT", f2_done[s])
                    w.op("vector", lambda e, s2=s2, pslot=pslot:
                         e.tensor_tensor(
                             out=ffnr[:, s2 * 1024:(s2 + 1) * 1024],
                             in0=psF2[:, pslot * 1024:(pslot + 1) * 1024],
                             in1=B2tile[:], op=ALU.add), inc="sV")
                    ffn_done[s] = w.v("sV")
                    w.wait("scalar", "sV", ffn_done[s])
                    if tc3 >= 1 and s2 == 0:
                        w.wait("scalar", "sV", chain3_done[tc3 - 1])
                    w.wait("scalar", "sS", w.v("sS"))
                    w.op("scalar", lambda e, s2=s2: e.activation(
                        out=junk2[:], in_=ffnr[:, s2 * 1024:(s2 + 1) * 1024],
                        func=AF.Square, accum_out=ssqc[:, 2 + s2:3 + s2]),
                        inc="sS")
                    w.wait("vector", "sS", w.v("sS"))
                    w.wait("vector", "sV", w.v("sV"))
                    w.op("vector", lambda e, s2=s2: e.tensor_reduce(
                        out=sxc[:, 2 + s2:3 + s2],
                        in_=ffnr[:, s2 * 1024:(s2 + 1) * 1024],
                        axis=AX.X, op=ALU.add), inc="sV")

                w.wait("vector", "sS", w.v("sS"))
                w.wait("vector", "sV", w.v("sV"))


                w.vops("vector", [
                    lambda e: e.tensor_scalar(out=negm[:, 2:4], in0=sxc[:, 2:4],
                                              scalar1=-1.0 / C, scalar2=None,
                                              op0=ALU.mult),
                    lambda e: e.tensor_tensor(out=msq[:, 2:4], in0=negm[:, 2:4],
                                              in1=negm[:, 2:4], op=ALU.mult),
                    lambda e: e.scalar_tensor_tensor(
                        out=vvar[:, 2:4], in0=ssqc[:, 2:4], scalar=1.0 / C,
                        in1=msq[:, 2:4], op0=ALU.mult, op1=ALU.subtract)])
                chain3_done[tc3] = w.v("sV")
                w.wait("scalar", "sV", w.v("sV"))
                w.op("scalar", lambda e: e.activation(
                    out=srt[:, 2:4], in_=vvar[:, 2:4], func=AF.Sqrt,
                    bias=ceps[:]), inc="sS")
                w.wait("vector", "sS", w.v("sS"))

                w.vops("vector", [
                    lambda e: e.reciprocal(rrc[:, 2:4], srt[:, 2:4]),
                    lambda e: e.tensor_tensor(out=nmrc[:, 2:4], in0=negm[:, 2:4],
                                              in1=rrc[:, 2:4], op=ALU.mult)])
                stats3 = w.v("sV")

                for s2 in range(2):
                    s = tc3 * 2 + s2
                    w.wait("vector", "sV", stats3)
                    if s >= 1:
                        w.wait("vector", "sV", outtt_done.get(s - 1, 0))
                    w.op("vector", lambda e, s2=s2: e.scalar_tensor_tensor(
                        out=z3r[:], in0=ffnr[:, s2 * 1024:(s2 + 1) * 1024],
                        scalar=rrc[:, 2 + s2:3 + s2],
                        in1=nmrc[:, 2 + s2:3 + s2].broadcast_to([128, 1024]),
                        op0=ALU.mult, op1=ALU.add), inc="sV")
                    zs_done[s] = w.v("sV")
                    w.wait("vector", "sV", zs_done[s])
                    w.wait("vector", ysem, yload)
                    if s >= 1:
                        w.wait("vector", "sOS", outst_done.get(s - 1, 0))
                    w.op("vector", lambda e, s2=s2, yslot=yslot: e.tensor_tensor(
                        out=outr[:],
                        in0=y1r3[:, yslot * 2048 + s2 * 1024:
                                 yslot * 2048 + (s2 + 1) * 1024],
                        in1=z3r[:], op=ALU.add), inc="sV")
                    outtt_done[s] = w.v("sV")
                    w.wait("gpsimd", "sV", outtt_done[s])
                    w.op("gpsimd", lambda e, s=s: e.dma_start(
                        out=out_d[b, s * 128:(s + 1) * 128, :], in_=outr[:]),
                        inc="sOS", k=16)
                    outst_done[s] = w.v("sOS")

            yld3(0)
            wload, wload2 = wqloads()
            yld3(1)
            yld3(2)
            p3front(0)
            f1blk(0)
            p3front(1)
            for tc3 in range(NTC3):
                if tc3 + 3 < NTC3:
                    yld3(tc3 + 3)
                if tc3 + 1 < NTC3:
                    f1blk(tc3 + 1)
                if tc3 + 2 < NTC3:
                    p3front(tc3 + 2)
                f2blk(tc3)

            w.wait("gpsimd", "sOS", w.v("sOS"))
            w.wait("gpsimd", "sSO0", w.v("sSO0"))
            w.wait("gpsimd", "sSO1", w.v("sSO1"))

        if 'P3' in run_p:
            w.block(p3, f"p3_{b}")
        p3sc.close()

    es.close()
    return nc


def host_inputs(inputs, core):
    x = np.ascontiguousarray(np.asarray(inputs["L0_attn_x"],
                                        np.float32)[2 * core:2 * core + 2])
    qkv_b = np.asarray(inputs["L0_attn_qkv_b"], np.float32)
    d = {
        "x": x,
        "qkvwT": np.ascontiguousarray(
            np.asarray(inputs["L0_attn_qkv_w"], np.float32).T),
        "qb_cols": np.ascontiguousarray(qkv_b[:C].reshape(8, 128).T),
        "kvb_row": np.ascontiguousarray(qkv_b[C:].reshape(1, 2 * C)),
        "p1wT": np.ascontiguousarray(np.asarray(inputs["L0_attn_p1_w"], np.float32).T),
        "p1b_cols": np.ascontiguousarray(
            np.asarray(inputs["L0_attn_p1_b"], np.float32).reshape(8, 128).T),
        "p2wT": np.ascontiguousarray(np.asarray(inputs["L0_attn_p2_w"], np.float32).T),
        "p2b_row": np.ascontiguousarray(
            np.asarray(inputs["L0_attn_p2_b"], np.float32).reshape(1, C)),
        "w1T": np.ascontiguousarray(np.asarray(inputs["L0_ffn_W1"], np.float32).T),
        "b1_cols": np.ascontiguousarray(
            np.asarray(inputs["L0_ffn_b1"], np.float32).reshape(32, 128).T),
        "w2T": np.ascontiguousarray(np.asarray(inputs["L0_ffn_W2"], np.float32).T),
        "b2_row": np.ascontiguousarray(
            np.asarray(inputs["L0_ffn_b2"], np.float32).reshape(1, C)),
        "g64": np.ascontiguousarray(np.concatenate([
            np.asarray(inputs["L0_attn_kn_g"], np.float32),
            np.asarray(inputs["L0_attn_vn_g"], np.float32),
            np.asarray(inputs["L0_attn_kn_b"], np.float32),
            np.asarray(inputs["L0_attn_vn_b"], np.float32)]).reshape(1, 256)),
        "tempr": np.ascontiguousarray(
            np.asarray(inputs["L0_attn_temperature"], np.float32).reshape(16, 1)),
    }
    ab = np.asarray(inputs["L0_attn_attn_bias"], np.float32).reshape(8, 2, HD, HD)
    abias = np.concatenate([ab[:, 0], ab[:, 1]], axis=1)
    d["abias"] = np.ascontiguousarray(abias.transpose(1, 0, 2).reshape(128, 512))
    return d


_NC_CACHE = {}


def kernel(**inputs):
    if "prog" not in _NC_CACHE:
        _NC_CACHE["prog"] = build_program(debug=False)
    nc = _NC_CACHE["prog"]
    in_maps = [host_inputs(inputs, core) for core in range(NCORES)]
    res = run_bass_kernel_spmd(nc, in_maps, core_ids=list(range(NCORES)))
    out = np.empty((16, T, C), np.float32)
    for core in range(NCORES):
        out[2 * core:2 * core + 2] = res.results[core]["out"]
    return out
